# version 04
# baseline (speedup 1.0000x reference)
import numpy as np
import ml_dtypes
from contextlib import ExitStack
import concourse.bass as bass
import concourse.mybir as mybir
from concourse.bass_utils import run_bass_kernel_spmd

F32 = mybir.dt.float32
BF16 = mybir.dt.bfloat16
I32 = mybir.dt.int32
I16 = mybir.dt.int16
ALU = mybir.AluOpType
AF = mybir.ActivationFunctionType
AX = mybir.AxisListType

B, N, HID = 4, 2048, 4096
H, KVH, D = 32, 8, 128
M = 1024
NTT = 8
NFF = 11008
NFT = 86
NKT = 32
GN = 16.0
THETA = 10000.0
EPS = 1e-5
NEG = -30000.0

ENGS = ("sync", "tensor", "vector", "scalar", "gpsimd")


def _il_perm():
    perm = np.empty(HID, dtype=np.int64)
    for j in range(NKT):
        for pp in range(128):
            perm[j * 128 + pp] = (pp // 4) * 128 + (pp % 4) * 32 + j
    return perm


IL_PERM = _il_perm()


def host_consts():
    ident = np.eye(128, dtype=np.float32)
    U = (np.triu(np.ones((128, 128))) * (-1.0 / GN)).astype(np.float32)
    maskPT = np.triu(np.ones((128, 128))).astype(np.float32)
    G32 = np.zeros((128, 32), np.float32)
    for pp in range(128):
        G32[pp, pp // 4] = 1.0
    ones1 = np.ones((128, 1), np.float32)
    onesr = np.ones((1, 128), np.float32)
    s_ = np.arange(128)[:, None]
    t_ = np.arange(64)[None, :]
    mvalid = (s_ >= t_) & (s_ <= t_ + 64)
    mT = np.where(mvalid, 0.0, NEG).astype(np.float32)
    maskT_norm = np.tile(mT, (1, 8))
    mA0 = np.where(mvalid & (s_ >= 64), 0.0, NEG).astype(np.float32)
    maskT_first0 = np.tile(np.concatenate([mA0, mT], axis=1), (1, 4))
    half = np.arange(D // 2, dtype=np.float64)
    inv = 1.0 / THETA ** (2.0 * half / D)
    pos = np.arange(N, dtype=np.float64)
    fr = pos[:, None] * inv[None, :]
    cos = np.cos(fr)
    sin = np.sin(fr)
    sc = 1.0 / np.sqrt(np.float64(D))
    return dict(ident=ident, U=U, maskPT=maskPT, G32=G32, ones1=ones1, onesr=onesr,
                maskT_norm=maskT_norm, maskT_first0=maskT_first0,
                cos=cos.astype(np.float32), sin=sin.astype(np.float32),
                cosq=(cos * sc).astype(np.float32), sinq=(sin * sc).astype(np.float32))


HC = host_consts()


class Sem:
    def __init__(self, handle):
        self.h = handle
        self.v = 0

    def inc(self, n=1):
        self.v += n
        return self.v


DATA_OPS = {"tensor_tensor", "tensor_scalar", "tensor_copy", "tensor_reduce",
            "reciprocal", "memset", "activation", "matmul", "transpose",
            "ldweights", "iota", "bn_stats", "bn_aggr"}
CHAINED = ("tensor", "vector", "scalar")


class RecIns:
    def __init__(self, auto=None):
        self.posts = []
        if auto is not None:
            self.posts.append(auto)
        self.auto = auto

    def then_inc(self, sem, n=1):
        assert self.auto is None, "explicit then_inc on chained compute op"
        self.posts.append((sem, n))
        return self


class Recorder:
    def __init__(self, p, eng):
        self.p = p
        self.eng = eng

    def wait_ge(self, sem, val):
        if val:
            self.p.q[self.eng].append(("w", sem, val))

    def nop(self):
        r = RecIns()
        self.p.q[self.eng].append(("nop", r))
        return r

    def dma_start(self, **kw):
        r = RecIns()
        self.p.q[self.eng].append(("dma", kw, r))
        return r

    def __getattr__(self, name):
        def g(*a, **k):
            ch = self.p.chain.get(self.eng) if name in DATA_OPS else None
            pre = ch.v if (ch is not None and ch.v) else None
            r = RecIns(auto=(ch.h, 1) if ch is not None else None)
            self.p.q[self.eng].append(("op", name, a, k, r, pre, ch))
            if ch is not None:
                ch.v += 1
            return r
        return g


class Prog:
    def __init__(self):
        self.q = {e: [] for e in ENGS}
        self.chain = {}

    def on(self, eng):
        rec = Recorder(self, eng)

        def deco(fn):
            fn(rec)
            return None
        return deco

    def replay(self, blk):
        for e in ENGS:
            recs = self.q[e]

            def section(h, recs=recs):
                for r in recs:
                    kind = r[0]
                    if kind == "w":
                        h.wait_ge(r[1], r[2])
                    elif kind == "nop":
                        ins = h.nop()
                        for sem, n in r[1].posts:
                            ins.then_inc(sem, n)
                    elif kind == "dma":
                        ins = h.dma_start(**r[1])
                        for sem, n in r[2].posts:
                            ins.then_inc(sem, n)
                    else:
                        _, name, a, k, ri, pre, ch = r
                        if pre is not None:
                            h.wait_ge(ch.h, pre)
                        ins = getattr(h, name)(*a, **k)
                        for sem, n in ri.posts:
                            ins.then_inc(sem, n)
            getattr(blk, e)(section)


class Ctx(dict):
    __getattr__ = dict.__getitem__
    __setattr__ = dict.__setitem__


def fence(c, waits=()):
    snap = {e: c.p.chain[e].v for e in CHAINED}
    for e in ENGS:
        @c.p.on(e)
        def _(h, e=e):
            for e2 in CHAINED:
                if e2 != e and snap[e2]:
                    h.wait_ge(c.p.chain[e2].h, snap[e2])
            for sem, val in waits:
                if val:
                    h.wait_ge(sem, val)


def build_program(mode="full"):
    nc = bass.Bass("TRN2", target_bir_lowering=False)
    T = {}

    def dram(name, shape, dty, kind="ExternalInput"):
        T[name] = nc.dram_tensor(name, shape, dty, kind=kind).ap()
        return T[name]

    dram("x_main", [M, HID], F32)
    dram("x_pre", [M, HID], F32)
    dram("wq_il", [HID, H * D], F32)
    dram("wk_il", [HID, KVH * D], F32)
    dram("wv_il", [HID, KVH * D], F32)
    dram("wo", [H * D, HID], F32)
    dram("gq_t", [NFT, 128, NKT, 16], I32)
    dram("uq_t", [NFT, 128, NKT, 16], I32)
    dram("dq_t", [NKT, 128, NFT, 16], I32)
    dram("gsr", [128, NFF], F32)
    dram("usr", [128, NFF], F32)
    dram("dsr", [NKT, NFF], F32)
    dram("gz", [32, NFF // 8], I32)
    dram("uz", [32, NFF // 8], I32)
    dram("dz", [NFT, 512], I32)
    dram("gsc", [32, NFF], F32)
    dram("usc", [32, NFF], F32)
    dram("dsc", [NFT, HID], F32)
    dram("identc", [128, 128], F32)
    dram("Uc", [128, 128], F32)
    dram("maskPTc", [128, 128], F32)
    dram("G32c", [128, 32], F32)
    dram("ones1c", [128, 1], F32)
    dram("onesrc", [1, 128], F32)
    dram("maskTn", [128, 512], F32)
    dram("maskTf", [128, 512], F32)
    dram("cosq", [M, 64], F32)
    dram("sinq", [M, 64], F32)
    dram("cosk", [M, 64], F32)
    dram("sink", [M, 64], F32)
    dram("coskh", [64, 64], F32)
    dram("sinkh", [64, 64], F32)
    dram("out", [M, HID], F32, kind="ExternalOutput")
    if mode == "ffn":
        dram("x2in", [M, HID], F32)
    dram("x1_sp", [M, HID], F32, kind="Internal")
    dram("h_sp", [NFT, 128, M], BF16, kind="Internal")
    dram("wd_sp", [NKT, 128, NFF], BF16, kind="Internal")
    dram("zs_sp", [128, 2 * NFF + HID], BF16, kind="Internal")
    dram("qs_sp", [NTT, 128, H * D], BF16, kind="Internal")
    dram("sq_sp", [NTT, 128, H * D], BF16, kind="Internal")
    dram("ks_sp", [16, 128, KVH * D], BF16, kind="Internal")
    dram("sk_sp", [NTT, 128, KVH * D], BF16, kind="Internal")
    dram("skh_sp", [64, KVH * D], BF16, kind="Internal")
    dram("gs_sp", [16, 128, KVH * D], F32, kind="Internal")
    dram("v_sp", [16, 128, KVH * D], BF16, kind="Internal")
    dram("dn_sp", [2, NKT, 128, 512], BF16, kind="Internal")
    dram("wo_sp", [2, NKT, 128, 512], BF16, kind="Internal")
    dram("H_sp", [NFT, M], BF16, kind="Internal")

    p = Prog()
    c = Ctx(nc=nc, T=T, p=p, mode=mode)
    es = ExitStack()
    with es:
        def sem(name):
            return Sem(es.enter_context(nc.semaphore(name)))
        c.sem = sem
        p.chain = {"tensor": sem("ch_t"), "vector": sem("ch_v"), "scalar": sem("ch_s")}
        c.cht = p.chain["tensor"]
        c.chv = p.chain["vector"]
        c.chs = p.chain["scalar"]
        c.bank = [es.enter_context(nc.psum_tensor(f"bank{i}", [128, 512], F32)) for i in range(8)]

        def mksb(stack):
            def sb(name, shape, dty):
                return stack.enter_context(nc.sbuf_tensor(name, shape, dty))
            return sb

        sb0 = mksb(es)
        c.ident = sb0("ident", [128, 128], BF16)
        c.identf = sb0("identf", [128, 128], F32)
        c.Uc = sb0("Uc_sb", [128, 128], F32)
        c.maskPT = sb0("maskPT", [128, 128], BF16)
        c.G32 = sb0("G32", [128, 32], BF16)
        c.ones1 = sb0("ones1", [128, 1], BF16)
        c.onesr = sb0("onesr", [1, 128], F32)
        c.maskTn = sb0("maskTn_sb", [128, 512], F32)
        c.maskTf = sb0("maskTf_sb", [128, 512], F32)
        emit_consts(c)

        with ExitStack() as s_p0:
            c.sb = mksb(s_p0)
            emit_zeros_prep(c)
        fence(c, waits=[c.zs_done])

        stk_xT1 = ExitStack()
        if mode == "ffn":
            with ExitStack() as s_x2:
                c.sb = mksb(s_x2)
                c.xT_ln1 = stk_xT1.enter_context(nc.sbuf_tensor("xT_ln1", [128, NKT, M], BF16))
                emit_x2_load(c)
            fence(c, waits=[c.x1_done])
        else:
            with ExitStack() as s_pa:
                c.sb = mksb(s_pa)
                emit_proj(c)
            fence(c, waits=[c.pa_sp, c.pa_spr, c.pa_spg])
            stk_oT = ExitStack()
            c.oT = stk_oT.enter_context(nc.sbuf_tensor("oT", [128, H, M], BF16))
            with ExitStack() as s_pb:
                c.sb = mksb(s_pb)
                emit_mixer(c)
            fence(c)
            with ExitStack() as s_pc:
                c.sb = mksb(s_pc)
                emit_wo(c)
            fence(c, waits=[c.wo_sp_done])
            stk_oT.close()
            c.xT_ln1 = stk_xT1.enter_context(nc.sbuf_tensor("xT_ln1", [128, NKT, M], BF16))
            with ExitStack() as s_pc2:
                c.sb = mksb(s_pc2)
                emit_postln(c, "ln1")
            fence(c, waits=[c.x1_done])

        if mode == "attn":
            emit_attn_out(c)
            stk_xT1.close()
        else:
            with ExitStack() as s_p4:
                c.sb = mksb(s_p4)
                emit_gate_up(c)
            fence(c, waits=[c.h_done, c.h_done2, c.H_done])
            stk_xT1.close()
            with ExitStack() as s_p5:
                c.sb = mksb(s_p5)
                emit_down(c)
            fence(c, waits=[c.dn_done])
            with ExitStack() as s_p6:
                c.sb = mksb(s_p6)
                emit_postln(c, "ln2")

        blk = es.enter_context(nc.Block())
        p.replay(blk)
    return nc


def emit_consts(c):
    T, p = c.T, c.p
    s_cg = c.sem("s_cg")
    s_cy = c.sem("s_cy")

    @p.on("gpsimd")
    def _(g):
        g.dma_start(out=c.ident[:], in_=T["identc"]).then_inc(s_cg.h, 16)
        g.dma_start(out=c.maskPT[:], in_=T["maskPTc"]).then_inc(s_cg.h, 16)
        g.dma_start(out=c.G32[:], in_=T["G32c"]).then_inc(s_cg.h, 16)
        g.dma_start(out=c.ones1[:], in_=T["ones1c"]).then_inc(s_cg.h, 16)
    s_cg.inc(64)

    @p.on("sync")
    def _(sy):
        sy.dma_start(out=c.Uc[:], in_=T["Uc"]).then_inc(s_cy.h, 16)
        sy.dma_start(out=c.identf[:], in_=T["identc"]).then_inc(s_cy.h, 16)
        sy.dma_start(out=c.onesr[:], in_=T["onesrc"]).then_inc(s_cy.h, 16)
        sy.dma_start(out=c.maskTn[:], in_=T["maskTn"]).then_inc(s_cy.h, 16)
        sy.dma_start(out=c.maskTf[:], in_=T["maskTf"]).then_inc(s_cy.h, 16)
    s_cy.inc(80)
    c.s_cg = s_cg
    c.s_cy = s_cy


def wait_consts(h, c):
    h.wait_ge(c.s_cg.h, c.s_cg.v)
    h.wait_ge(c.s_cy.h, c.s_cy.v)


def emit_zeros_prep(c):
    T, sb, p = c.T, c.sb, c.p
    chv = c.chv
    s_z = c.sem("s_z")
    s_zs = c.sem("s_zs")
    zq = sb("zq", [NFT, NFF // 8], I32)
    zsc = sb("zsc", [NFT, NFF], F32)
    znib = sb("znib", [NFT, NFF], I16)
    zout = sb("zout", [NFT, NFF], BF16)

    off = 0
    for zn, scn, rows, cols in (("gz", "gsc", 32, NFF), ("uz", "usc", 32, NFF),
                                ("dz", "dsc", NFT, HID)):
        @p.on("sync")
        def _(sy, zn=zn, scn=scn, rows=rows, cols=cols, prev=c.get("z_used", 0),
              prev_sp=c.get("z_spv", 0)):
            sy.wait_ge(chv.h, prev)
            if prev_sp:
                sy.wait_ge(s_zs.h, prev_sp)
            sy.dma_start(out=zq[0:rows, 0:cols // 8], in_=T[zn]).then_inc(s_z.h, 16)
            sy.dma_start(out=zsc[0:rows, 0:cols], in_=T[scn]).then_inc(s_z.h, 16)
        s_z.inc(32)
        tgt = s_z.v

        @p.on("vector")
        def _(v, rows=rows, cols=cols, tgt=tgt):
            v.wait_ge(s_z.h, tgt)
            z16 = zq[0:rows, 0:cols // 8].bitcast(I16).rearrange("r (w h) -> r w h", h=2)
            ov = znib[0:rows, 0:cols].rearrange("r (w e h) -> r w e h", e=4, h=2)
            for jb in range(4):
                v.tensor_scalar(out=ov[:, :, jb, :], in0=z16, scalar1=4 * jb, scalar2=15,
                                op0=ALU.logical_shift_right, op1=ALU.bitwise_and)
            v.tensor_scalar(out=zsc[0:rows, 0:cols], in0=zsc[0:rows, 0:cols],
                            scalar1=-1.0, scalar2=0.0, op0=ALU.mult, op1=ALU.add)
            v.tensor_tensor(out=zout[0:rows, 0:cols], in0=znib[0:rows, 0:cols],
                            in1=zsc[0:rows, 0:cols], op=ALU.mult)
        c["z_used"] = chv.v
        tgv = chv.v

        @p.on("sync")
        def _(sy, rows=rows, cols=cols, off=off, tgv=tgv):
            sy.wait_ge(chv.h, tgv)
            sy.dma_start(out=T["zs_sp"][0:rows, off:off + cols],
                         in_=zout[0:rows, 0:cols]).then_inc(s_zs.h, 16)
        s_zs.inc(16)
        c["z_spv"] = s_zs.v
        off += cols
    c.zs_done = (s_zs.h, s_zs.v)


def emit_proj(c):
    nc, T, sb, p, bank = c.nc, c.T, c.sb, c.p, c.bank
    cht, chv, chs = c.cht, c.chv, c.chs
    s_xl = [c.sem("s_xl0"), c.sem("s_xl1")]
    s_wl = [c.sem("s_wl0"), c.sem("s_wl1")]
    s_sp = c.sem("s_sp")
    s_spr = c.sem("s_spr")
    s_spg = c.sem("s_spg")
    s_cl = c.sem("s_cl")

    xbuf = [sb(f"xbuf{i}", [128, HID], F32) for i in range(2)]
    xT = sb("xT", [128, NKT, M], BF16)
    wsl = [sb(f"wsl{i}", [128, NKT, 512], BF16) for i in range(2)]
    expb = sb("expb", [128, 512], F32)
    sums = sb("sums", [128, 4], F32)
    recs = sb("recs", [128, 4], F32)
    pstage = sb("pstage", [128, 512], BF16)
    rstage = sb("rstage", [128, 512], BF16)
    gstage = sb("gstage", [128, 512], F32)
    ta = sb("ta", [128, 64], F32)
    tb = sb("tb", [128, 64], F32)
    cosb = sb("cosb", [128, 4, NTT, 64], F32)
    cosh = sb("cosh", [64, 128], F32)
    Bf = dict(expb=expb, sums=sums, recs=recs, pstage=pstage, rstage=rstage,
              gstage=gstage, ta=ta, tb=tb, cosb=cosb, cosh=cosh)
    Sm = dict(s_sp=s_sp, s_spr=s_spr, s_spg=s_spg, s_cl=s_cl)

    @p.on("scalar")
    def _(s):
        for i, nm in enumerate(("cosq", "sinq", "cosk", "sink")):
            s.dma_start(out=cosb[:, i],
                        in_=T[nm].rearrange("(tt p) f -> p tt f", p=128)).then_inc(s_cl.h, 16)
        s.dma_start(out=cosh[:, 0:64], in_=T["coskh"]).then_inc(s_cl.h, 16)
        s.dma_start(out=cosh[:, 64:128], in_=T["sinkh"]).then_inc(s_cl.h, 16)
    s_cl.inc(96)

    for phase in ("pre", "main"):
        xsrc = T["x_pre"] if phase == "pre" else T["x_main"]
        for tt in range(NTT):
            buf = xbuf[tt % 2]
            sxl = s_xl[tt % 2]

            @p.on("sync")
            def _(sy, tt=tt, buf=buf, prev_ev=c.get(f"pa_xtc_{tt - 2}", 0),
                  mmv=c.get("pa_mm_all", 0), sxl=sxl):
                sy.wait_ge(chv.h, prev_ev)
                sy.wait_ge(cht.h, mmv)
                sy.dma_start(out=buf[:], in_=xsrc[tt * 128:(tt + 1) * 128, :]).then_inc(sxl.h, 16)
            sxl.inc(16)
            xl_tgt = sxl.v
            v0 = chv.v

            @p.on("tensor")
            def _(t, buf=buf, xl_tgt=xl_tgt, v0=v0, first=(phase == "pre" and tt == 0), sxl=sxl):
                t.wait_ge(sxl.h, xl_tgt)
                if first:
                    wait_consts(t, c)
                for j in range(NKT):
                    if v0 + j - 1 > 0:
                        t.wait_ge(chv.h, v0 + j - 1)
                    xv = buf[:].rearrange("p (a b e) -> p a b e", a=32, b=4, e=32)[:, :, :, j]
                    pv = bank[j % 2].ap()
                    t.transpose(pv[:, 0:128], xv, c.identf[:])
            pe0 = cht.v - NKT

            @p.on("vector")
            def _(v, tt=tt, pe0=pe0):
                for j in range(NKT):
                    v.wait_ge(cht.h, pe0 + j + 1)
                    pv = bank[j % 2].ap()
                    v.tensor_copy(out=xT[:, j, tt * 128:(tt + 1) * 128], in_=pv[:, 0:128])
            c[f"pa_xtc_{tt}"] = chv.v
        xT_done = chv.v

        if phase == "main":
            groups = ([("q", fg) for fg in range(8)] +
                      [("k", 0), ("k", 1), ("v", 0), ("v", 1)])
        else:
            groups = [("k", 0), ("k", 1), ("v", 0), ("v", 1)]

        for gi, (kind, fg) in enumerate(groups):
            wslb = wsl[gi % 2]
            wsrc = {"q": "wq_il", "k": "wk_il", "v": "wv_il"}[kind]
            col0 = fg * 512

            swl = s_wl[gi % 2]

            @p.on("gpsimd")
            def _(g, wsrc=wsrc, col0=col0, wslb=wslb,
                  prev_mm=c.get(f"pa_mm_slot{gi % 2}", 0), xT_done=xT_done, swl=swl):
                g.wait_ge(cht.h, max(prev_mm, 0))
                g.wait_ge(chv.h, xT_done)
                g.dma_start(out=wslb[:],
                            in_=T[wsrc].rearrange("(j p) f -> p j f", p=128)[:, :, col0:col0 + 512]
                            ).then_inc(swl.h, 16)
            swl.inc(16)
            wl_tgt = swl.v

            for tt in range(NTT):
                bnk = bank[tt]
                ev_prev = c.get(f"pa_ev_{tt}", (0, 0))

                @p.on("tensor")
                def _(t, wslb=wslb, wl_tgt=wl_tgt, bnk=bnk, tt=tt,
                      ev_prev=ev_prev, xT_done=xT_done, swl=swl):
                    t.wait_ge(swl.h, wl_tgt)
                    t.wait_ge(chv.h, max(xT_done, ev_prev[0]))
                    if ev_prev[1]:
                        t.wait_ge(chs.h, ev_prev[1])
                    for j in range(NKT):
                        t.matmul(bnk.ap(), xT[:, j, tt * 128:(tt + 1) * 128],
                                 wslb[:, j, :], start=(j == 0), stop=(j == NKT - 1))
                mm_tgt = cht.v
                emit_proj_evac(c, phase, kind, fg, tt, bnk, mm_tgt, Bf, Sm)
                c[f"pa_ev_{tt}"] = (chv.v, chs.v)
            c[f"pa_mm_slot{gi % 2}"] = cht.v
        c["pa_mm_all"] = cht.v
    c.pa_sp = (s_sp.h, s_sp.v)
    c.pa_spr = (s_spr.h, s_spr.v)
    c.pa_spg = (s_spg.h, s_spg.v)


def emit_proj_evac(c, phase, kind, fg, tt, bnk, mm_tgt, Bf, Sm):
    T, p = c.T, c.p
    cht, chv, chs = c.cht, c.chv, c.chs
    s_sp, s_spr, s_spg, s_cl = Sm["s_sp"], Sm["s_spr"], Sm["s_spg"], Sm["s_cl"]
    expb, sums, recs = Bf["expb"], Bf["sums"], Bf["recs"]
    pstage, rstage, gstage = Bf["pstage"], Bf["rstage"], Bf["gstage"]
    ta, tb, cosb, cosh = Bf["ta"], Bf["tb"], Bf["cosb"], Bf["cosh"]
    main = (phase == "main")
    it = (NTT if main else 0) + tt

    if kind == "v":
        @p.on("scalar")
        def _(s, bnk=bnk, mm_tgt=mm_tgt, sp0=s_sp.v, vprev=chv.v):
            s.wait_ge(cht.h, mm_tgt)
            s.wait_ge(chv.h, vprev)
            if sp0:
                s.wait_ge(s_sp.h, sp0)
            s.activation(out=pstage[:], in_=bnk.ap(), func=AF.Identity)
        es_tgt = chs.v

        @p.on("sync")
        def _(sy, it=it, fg=fg, es_tgt=es_tgt):
            sy.wait_ge(chs.h, es_tgt)
            sy.dma_start(out=T["v_sp"][it, :, fg * 512:(fg + 1) * 512],
                         in_=pstage[:]).then_inc(s_sp.h, 16)
        s_sp.inc(16)
        return

    rope_this = main or (phase == "pre" and tt == 7 and kind == "k")
    cl_tgt = s_cl.v

    @p.on("scalar")
    def _(s, bnk=bnk, mm_tgt=mm_tgt, kind=kind, spg_pre=s_spg.v, vprev=chv.v):
        s.wait_ge(cht.h, mm_tgt)
        s.wait_ge(chv.h, vprev)
        if kind == "k" and spg_pre:
            s.wait_ge(s_spg.h, spg_pre)
        s.activation(out=expb[:], in_=bnk.ap(), func=AF.Exp)
        if kind == "k":
            s.activation(out=gstage[:], in_=bnk.ap(), func=AF.Exp, scale=-1.0)
            s.activation(out=gstage[:], in_=gstage[:], func=AF.Ln, bias=1.0)
    es_exp = chs.v

    @p.on("vector")
    def _(v, bnk=bnk, kind=kind, es_exp=es_exp, sp_pre=s_sp.v, spr_pre=s_spr.v,
          rope_this=rope_this, cl_tgt=cl_tgt, main=main):
        v.wait_ge(chs.h, es_exp)
        if sp_pre:
            v.wait_ge(s_sp.h, sp_pre)
        if rope_this and spr_pre:
            v.wait_ge(s_spr.h, spr_pre)
        for hh in range(4):
            v.tensor_reduce(out=sums[:, hh:hh + 1], in_=expb[:, hh * 128:(hh + 1) * 128],
                            axis=AX.X, op=ALU.add)
        if kind == "q":
            v.tensor_scalar(out=sums[:], in0=sums[:], scalar1=2.0, scalar2=0.0,
                            op0=ALU.mult, op1=ALU.add)
        v.reciprocal(out=recs[:], in_=sums[:])
        for hh in range(4):
            v.tensor_scalar(out=pstage[:, hh * 128:(hh + 1) * 128],
                            in0=expb[:, hh * 128:(hh + 1) * 128],
                            scalar1=recs[:, hh:hh + 1], scalar2=0.0,
                            op0=ALU.mult, op1=ALU.add)
        if rope_this:
            v.wait_ge(s_cl.h, cl_tgt)
            rows = slice(0, 128) if main else slice(64, 128)
            if main:
                ci = 0 if kind == "q" else 2
                cv = cosb[:, ci, tt, :]
                sv = cosb[:, ci + 1, tt, :]
            else:
                cv = cosh[:, 0:64]
                sv = cosh[:, 64:128]
            for hh in range(4):
                lo = slice(hh * 128, hh * 128 + 64)
                hi = slice(hh * 128 + 64, (hh + 1) * 128)
                pv = bnk.ap()
                crows = rows if main else slice(0, 64)
                cvr = cv[rows, :] if main else cv
                svr = sv[rows, :] if main else sv
                v.tensor_tensor(out=ta[rows, :], in0=pv[rows, lo], in1=cvr, op=ALU.mult)
                v.tensor_tensor(out=tb[rows, :], in0=pv[rows, hi], in1=svr, op=ALU.mult)
                v.tensor_tensor(out=rstage[rows, lo], in0=ta[rows, :], in1=tb[rows, :], op=ALU.subtract)
                v.tensor_tensor(out=ta[rows, :], in0=pv[rows, hi], in1=cvr, op=ALU.mult)
                v.tensor_tensor(out=tb[rows, :], in0=pv[rows, lo], in1=svr, op=ALU.mult)
                v.tensor_tensor(out=rstage[rows, hi], in0=ta[rows, :], in1=tb[rows, :], op=ALU.add)
    ev_done = chv.v

    @p.on("sync")
    def _(sy, kind=kind, fg=fg, it=it, tt=tt, ev_done=ev_done, rope_this=rope_this,
          main=main):
        sy.wait_ge(chv.h, ev_done)
        cols = slice(fg * 512, (fg + 1) * 512)
        if kind == "q":
            sy.dma_start(out=T["qs_sp"][tt, :, cols], in_=pstage[:]).then_inc(s_sp.h, 16)
            sy.dma_start(out=T["sq_sp"][tt, :, cols], in_=rstage[:]).then_inc(s_spr.h, 16)
        else:
            sy.dma_start(out=T["ks_sp"][it, :, cols], in_=pstage[:]).then_inc(s_sp.h, 16)
            sy.dma_start(out=T["gs_sp"][it, :, cols], in_=gstage[:]).then_inc(s_spg.h, 16)
            if main:
                sy.dma_start(out=T["sk_sp"][tt, :, cols], in_=rstage[:]).then_inc(s_spr.h, 16)
            elif rope_this:
                sy.dma_start(out=T["skh_sp"][:, cols], in_=rstage[64:128, :]).then_inc(s_spr.h, 16)
    s_sp.inc(16)
    if kind == "q" or (kind == "k" and (main or rope_this)):
        s_spr.inc(16)
    if kind == "k":
        s_spg.inc(16)


def emit_mixer(c):
    nc, T, sb, p, bank = c.nc, c.T, c.sb, c.p, c.bank
    cht, chv, chs = c.cht, c.chv, c.chs
    s_l = c.sem("s_bl")

    S = sb("S", [128, KVH * 128], F32)
    Sb = sb("Sb", [128, KVH * 128], BF16)
    skT = sb("skT", [128, KVH, 64 + M], BF16)
    oT = c.oT
    qs = sb("qs", [128, H * D], BF16)
    sq = sb("sq", [128, H * D], BF16)
    ks = sb("ks", [128, KVH * D], BF16)
    sk = sb("sk", [128, KVH * D], BF16)
    gs = sb("gs", [128, KVH * D], F32)
    vt = [sb(f"vt{i}", [128, KVH * D], BF16) for i in range(2)]
    qsT = sb("qsT", [128, H, 128], BF16)
    sqT = sb("sqT", [128, H, 128], BF16)
    koT = sb("koT", [128, 128], BF16)
    ko = sb("ko", [128, 128], BF16)
    eGT = sb("eGT", [128, 128], F32)
    enG = sb("enG", [128, 128], F32)
    qtT = sb("qtT", [128, 4, 128], BF16)
    PTg = sb("PTg", [128, 512], BF16)
    sws = sb("sws", [128, 512], F32)
    swe = sb("swe", [128, 512], BF16)
    swz = sb("swz", [1, 512], F32)
    PTs = sb("PTs", [128, 512], BF16)
    vwin = sb("vwin", [128, KVH * D], BF16)
    s_vw = c.sem("s_vw")
    c._s_vw = s_vw
    c._vwin = vwin

    @p.on("vector")
    def _(v):
        v.memset(S[:], 0.0)
        v.memset(Sb[:], 0.0)

    Bf = dict(S=S, Sb=Sb, skT=skT, oT=oT, qs=qs, sq=sq, ks=ks, sk=sk, gs=gs,
              qsT=qsT, sqT=sqT, koT=koT, ko=ko, eGT=eGT, enG=enG, qtT=qtT,
              PTg=PTg, sws=sws, swe=swe, swz=swz, PTs=PTs)

    for it in range(16):
        main = it >= NTT
        tt = it - NTT if main else it
        vtc = vt[it % 2]

        @p.on("sync")
        def _(sy, it=it, tt=tt, main=main, vtc=vtc, done=c.get("mix_done", 0)):
            sy.wait_ge(chv.h, done)
            sy.wait_ge(cht.h, c.get("mix_done_t", 0))
            sy.dma_start(out=ks[:], in_=T["ks_sp"][it]).then_inc(s_l.h, 16)
            sy.dma_start(out=gs[:], in_=T["gs_sp"][it]).then_inc(s_l.h, 16)
            sy.dma_start(out=vtc[:], in_=T["v_sp"][it]).then_inc(s_l.h, 16)
            if main:
                sy.dma_start(out=qs[:], in_=T["qs_sp"][tt]).then_inc(s_l.h, 16)
                sy.dma_start(out=sq[:], in_=T["sq_sp"][tt]).then_inc(s_l.h, 16)
                sy.dma_start(out=sk[:], in_=T["sk_sp"][tt]).then_inc(s_l.h, 16)
        s_l.inc(48 + (48 if main else 0))
        ld_tgt = s_l.v

        if main:
            vwin = c._vwin
            s_vw = c._s_vw

            @p.on("sync")
            def _(sy, ld_tgt=ld_tgt, vtc=vtc, vtp=vt[(it + 1) % 2]):
                sy.wait_ge(s_l.h, ld_tgt)
                sy.dma_start(out=vwin[0:64, :], in_=vtp[64:128, :]).then_inc(s_vw.h, 16)
                sy.dma_start(out=vwin[64:128, :], in_=vtc[0:64, :]).then_inc(s_vw.h, 16)
            s_vw.inc(32)
            c["vw_tgt"] = s_vw.v

            for grp, (srcb, dstb, nseg) in enumerate(((qs, qsT, H), (sq, sqT, H), (sk, None, KVH))):
                v0 = chv.v

                @p.on("tensor")
                def _(t, srcb=srcb, nseg=nseg, ld_tgt=ld_tgt, v0=v0, grp=grp):
                    if grp == 0:
                        t.wait_ge(s_l.h, ld_tgt)
                    for sg in range(nseg):
                        if v0 + sg - 1 > 0:
                            t.wait_ge(chv.h, v0 + sg - 1)
                        pv = bank[4 + sg % 2].ap().bitcast(BF16)
                        t.transpose(pv[:, 0:128], srcb[:, sg * 128:(sg + 1) * 128],
                                    c.ident[:])
                pe0 = cht.v - nseg

                @p.on("vector")
                def _(v, dstb=dstb, nseg=nseg, pe0=pe0, tt=tt, grp=grp):
                    for sg in range(nseg):
                        v.wait_ge(cht.h, pe0 + sg + 1)
                        pv = bank[4 + sg % 2].ap().bitcast(BF16)
                        if grp < 2:
                            v.tensor_copy(out=dstb[:, sg, :], in_=pv[:, 0:128])
                        else:
                            v.tensor_copy(out=skT[:, sg, 64 + tt * 128:64 + (tt + 1) * 128],
                                          in_=pv[:, 0:128])

        for kv in range(KVH):
            emit_chunk(c, it, tt, main, kv, Bf, vtc, vt[(it + 1) % 2], s_l, ld_tgt)
        c["mix_done"] = chv.v
        c["mix_done_t"] = cht.v


def emit_chunk(c, it, tt, main, kv, Bf, vtc, vtp, s_l, ld_tgt):
    p, bank = c.p, c.bank
    cht, chv, chs = c.cht, c.chv, c.chs
    S, Sb, skT, oT = Bf["S"], Bf["Sb"], Bf["skT"], Bf["oT"]
    qs, sq, ks, sk, gs = Bf["qs"], Bf["sq"], Bf["ks"], Bf["sk"], Bf["gs"]
    qsT, sqT, koT, ko = Bf["qsT"], Bf["sqT"], Bf["koT"], Bf["ko"]
    eGT, enG, qtT = Bf["eGT"], Bf["enG"], Bf["qtT"]
    PTg, sws, swe, swz, PTs = Bf["PTg"], Bf["sws"], Bf["swe"], Bf["swz"], Bf["PTs"]
    ksg = slice(kv * 128, (kv + 1) * 128)
    Sseg = slice(kv * 128, (kv + 1) * 128)

    bG = bank[6]
    bO = bank[2 if kv % 2 == 0 else 3]
    bW = bank[7]
    bD = bank[6]

    @p.on("tensor")
    def _(t, ld_tgt=ld_tgt, kv=kv, vpre=chv.v, spre=chs.v):
        if kv == 0 and not main:
            t.wait_ge(s_l.h, ld_tgt)
        t.wait_ge(chv.h, vpre)
        t.wait_ge(chs.h, spre)
        t.matmul(bG.ap()[:, 0:128], gs[:, ksg], c.Uc[:], start=True, stop=True)
        t.matmul(bG.ap()[:, 128:256], c.Uc[:], gs[:, ksg], start=True, stop=True)
    gmm = cht.v

    @p.on("scalar")
    def _(s, gmm=gmm):
        s.wait_ge(cht.h, gmm)
        s.activation(out=eGT[:], in_=bG.ap()[:, 0:128], func=AF.Exp)
        s.activation(out=enG[:], in_=bG.ap()[:, 128:256], func=AF.Exp, scale=-1.0)
    gex = chs.v

    @p.on("vector")
    def _(v, gex=gex, kv=kv):
        v.wait_ge(chs.h, gex)
        v.tensor_tensor(out=ko[:], in0=ks[:, ksg], in1=enG[:], op=ALU.mult)
    kov = chv.v

    if main:
        @p.on("tensor")
        def _(t, kov=kov):
            t.wait_ge(chv.h, kov)
            pv = bank[5].ap().bitcast(BF16)
            t.transpose(pv[:, 0:128], ko[:], c.ident[:])
        koT_mm = cht.v

        @p.on("vector")
        def _(v, tgt=koT_mm, kv=kv):
            v.wait_ge(cht.h, tgt)
            pv = bank[5].ap().bitcast(BF16)
            v.tensor_copy(out=koT[:], in_=pv[:, 0:128])
            for qi in range(4):
                qh = kv * 4 + qi
                v.tensor_tensor(out=qtT[:, qi, :], in0=qsT[:, qh, :], in1=eGT[:],
                                op=ALU.mult)
        qt_done = chv.v

        @p.on("tensor")
        def _(t, kv=kv, tt=tt, qt_done=qt_done):
            t.wait_ge(chv.h, qt_done)
            for qi in range(4):
                qh = kv * 4 + qi
                for blk_ in range(2):
                    w0 = tt * 128 + blk_ * 64
                    t.matmul(bW.ap()[:, (qi * 2 + blk_) * 64:(qi * 2 + blk_ + 1) * 64],
                             skT[:, kv, w0:w0 + 128],
                             sqT[:, qh, blk_ * 64:(blk_ + 1) * 64],
                             start=True, stop=True)
        swmm = cht.v

        @p.on("vector")
        def _(v, swmm=swmm, it=it):
            v.wait_ge(cht.h, swmm)
            mask = c.maskTf if it == NTT else c.maskTn
            v.tensor_tensor(out=sws[:], in0=bW.ap(), in1=mask[:], op=ALU.add)
        swm = chv.v

        @p.on("scalar")
        def _(s, swm=swm):
            s.wait_ge(chv.h, swm)
            s.activation(out=swe[:], in_=sws[:], func=AF.Exp)
        swex = chs.v

        @p.on("tensor")
        def _(t, swex=swex):
            t.wait_ge(chs.h, swex)
            t.matmul(bW.ap()[0:1, 0:512], c.ones1[:], swe[:, 0:512],
                     start=True, stop=True)
        zmm = cht.v

        @p.on("vector")
        def _(v, zmm=zmm):
            v.wait_ge(cht.h, zmm)
            v.reciprocal(out=swz[:], in_=bW.ap()[0:1, 0:512])
            v.tensor_scalar(out=swz[:], in0=swz[:], scalar1=0.5, scalar2=0.0,
                            op0=ALU.mult, op1=ALU.add)
        zrec = chv.v

        @p.on("tensor")
        def _(t, zrec=zrec):
            t.wait_ge(chv.h, zrec)
            t.matmul(bW.ap()[:, 0:512], c.onesr[:], swz[:], start=True, stop=True)
        repmm = cht.v

        @p.on("vector")
        def _(v, repmm=repmm):
            v.wait_ge(cht.h, repmm)
            v.tensor_tensor(out=PTs[:], in0=swe[:], in1=bW.ap(), op=ALU.mult)
        pts_done = chv.v

        @p.on("tensor")
        def _(t, kv=kv, qt_done=qt_done):
            t.wait_ge(chv.h, qt_done)
            for qi in range(4):
                t.matmul(bank[5].ap()[:, qi * 128:(qi + 1) * 128],
                         koT[:], qtT[:, qi, :], start=True, stop=True)
        ptgmm = cht.v

        @p.on("vector")
        def _(v, ptgmm=ptgmm):
            v.wait_ge(cht.h, ptgmm)
            for qi in range(4):
                v.tensor_tensor(out=PTg[:, qi * 128:(qi + 1) * 128],
                                in0=bank[5].ap()[:, qi * 128:(qi + 1) * 128],
                                in1=c.maskPT[:], op=ALU.mult)
        ptg_done = chv.v

        @p.on("tensor")
        def _(t, kv=kv, it=it, ptg_done=ptg_done, pts_done=pts_done,
              o_ev=c.get(f"oev_{kv % 2}", 0), vtc=vtc, vtp=vtp):
            t.wait_ge(chv.h, max(ptg_done, pts_done))
            if o_ev:
                t.wait_ge(chs.h, o_ev)
            t.wait_ge(c._s_vw.h, c.get("vw_tgt", 0))
            for qi in range(4):
                osl = bO.ap()[:, qi * 128:(qi + 1) * 128]
                t.matmul(osl, vtc[:, ksg], PTg[:, qi * 128:(qi + 1) * 128],
                         start=True, stop=False)
                t.matmul(osl, Sb[:, Sseg], qtT[:, qi, :], start=False, stop=False)
                pA = PTs[:, (qi * 2) * 64:(qi * 2 + 1) * 64]
                t.matmul(osl[:, 0:64], c._vwin[:, ksg], pA[:], start=False, stop=False)
                pB = PTs[:, (qi * 2 + 1) * 64:(qi * 2 + 2) * 64]
                t.matmul(osl[:, 64:128], vtc[:, ksg], pB[:], start=False, stop=True)
        omm = cht.v

        @p.on("scalar")
        def _(s, kv=kv, tt=tt, omm=omm):
            s.wait_ge(cht.h, omm)
            for qi in range(4):
                qh = kv * 4 + qi
                s.activation(out=oT[:, qh, tt * 128:(tt + 1) * 128],
                             in_=bO.ap()[:, qi * 128:(qi + 1) * 128],
                             func=AF.Identity)
        c[f"oev_{kv % 2}"] = chs.v

    @p.on("tensor")
    def _(t, kov=kov, vpre=chv.v, vtc=vtc):
        t.wait_ge(chv.h, max(kov, vpre))
        t.matmul(bD.ap()[:, 256:384], ko[:], vtc[:, ksg], start=True, stop=True)
    dmm = cht.v

    @p.on("vector")
    def _(v, dmm=dmm, kv=kv):
        v.wait_ge(cht.h, dmm)
        v.tensor_tensor(out=S[:, Sseg], in0=S[:, Sseg], in1=bD.ap()[:, 256:384], op=ALU.add)
        v.tensor_scalar(out=S[:, Sseg], in0=S[:, Sseg], scalar1=eGT[:, 127:128],
                        scalar2=0.0, op0=ALU.mult, op1=ALU.add)
        v.tensor_copy(out=Sb[:, Sseg], in_=S[:, Sseg])

    if it == NTT - 1 and kv == KVH - 1:
        s_halo = c.sem("s_halo")

        @p.on("sync")
        def _(sy, sp=chv.v):
            sy.wait_ge(chv.h, sp)
            sy.dma_start(out=sk[0:64, :], in_=c.T["skh_sp"]).then_inc(s_halo.h, 16)

        v0h = chv.v

        @p.on("tensor")
        def _(t, v0h=v0h):
            t.wait_ge(s_halo.h, 16)
            for kk in range(KVH):
                if v0h + kk - 1 > 0:
                    t.wait_ge(chv.h, v0h + kk - 1)
                pv = bank[4 + kk % 2].ap().bitcast(BF16)
                t.transpose(pv[:, 0:64], sk[0:64, kk * 128:(kk + 1) * 128], c.ident[0:64, 0:64])
        peh = cht.v - KVH

        @p.on("vector")
        def _(v, peh=peh):
            for kk in range(KVH):
                v.wait_ge(cht.h, peh + kk + 1)
                pv = bank[4 + kk % 2].ap().bitcast(BF16)
                v.tensor_copy(out=skT[:, kk, 0:64], in_=pv[:, 0:64])


def emit_wo(c):
    nc, T, sb, p, bank = c.nc, c.T, c.sb, c.p, c.bank
    cht, chv, chs = c.cht, c.chv, c.chs
    oT = c.oT
    s_wl = [c.sem("s_cwl0"), c.sem("s_cwl1")]
    s_sp = c.sem("s_csp")

    wsl = [sb(f"cwsl{i}", [128, NKT, 128], BF16) for i in range(2)]
    wo_st = sb("wo_st", [128, M], BF16)

    for ht in range(NKT):
        wslb = wsl[ht % 2]

        swl = s_wl[ht % 2]

        @p.on("gpsimd")
        def _(g, ht=ht, wslb=wslb, prev_mm=c.get(f"wo_mm_slot{ht % 2}", 0), swl=swl):
            g.wait_ge(cht.h, prev_mm)
            g.dma_start(out=wslb[:],
                        in_=T["wo"].rearrange("(o p) n -> p o n", p=128)[:, :, ht * 128:(ht + 1) * 128]
                        ).then_inc(swl.h, 16)
        swl.inc(16)
        wl_tgt = swl.v

        @p.on("tensor")
        def _(t, ht=ht, wslb=wslb, wl_tgt=wl_tgt, ev_prev=c.get("wo_ev_prev", 0), swl=swl):
            t.wait_ge(swl.h, wl_tgt)
            t.wait_ge(chs.h, ev_prev)
            for half in range(2):
                bnk = bank[half]
                for of in range(NKT):
                    t.matmul(bnk.ap(), wslb[:, of, :],
                             oT[:, of, half * 512:(half + 1) * 512],
                             start=(of == 0), stop=(of == NKT - 1))
        c[f"wo_mm_slot{ht % 2}"] = cht.v
        mm_tgt = cht.v

        @p.on("scalar")
        def _(s, mm_tgt=mm_tgt, spp=s_sp.v):
            s.wait_ge(cht.h, mm_tgt)
            if spp:
                s.wait_ge(s_sp.h, spp)
            s.activation(out=wo_st[:, 0:512], in_=bank[0].ap(), func=AF.Identity)
            s.activation(out=wo_st[:, 512:1024], in_=bank[1].ap(), func=AF.Identity)
        c["wo_ev_prev"] = chs.v
        ev_tgt = chs.v

        @p.on("sync")
        def _(sy, ht=ht, ev_tgt=ev_tgt):
            sy.wait_ge(chs.h, ev_tgt)
            sy.dma_start(out=T["wo_sp"][0, ht], in_=wo_st[:, 0:512]).then_inc(s_sp.h, 16)
            sy.dma_start(out=T["wo_sp"][1, ht], in_=wo_st[:, 512:1024]).then_inc(s_sp.h, 16)
        s_sp.inc(32)
    c.wo_sp_done = (s_sp.h, s_sp.v)


def emit_postln(c, which):
    nc, T, sb, p, bank = c.nc, c.T, c.sb, c.p, c.bank
    cht, chv, chs = c.cht, c.chv, c.chs
    s_l = [c.sem(f"s_{which}l0"), c.sem(f"s_{which}l1")]
    s_x = [c.sem(f"s_{which}x0"), c.sem(f"s_{which}x1")]
    s_o = c.sem(f"s_{which}o")
    src_sp = T["wo_sp"] if which == "ln1" else T["dn_sp"]
    x_src = T["x_main"] if which == "ln1" else T["x1_sp"]

    dsl = [sb(f"{which}dl{i}", [128, NKT, 128], BF16) for i in range(2)]
    xrb = [sb(f"{which}xr{i}", [128, HID], F32) for i in range(2)]
    dtm = sb(f"{which}dtm", [128, HID], BF16)
    lsum = sb(f"{which}sum", [128, HID], F32)
    lsq = sb(f"{which}sq", [128, HID], F32)
    stat = sb(f"{which}stat", [128, 4], F32)
    outb = sb(f"{which}out", [128, HID], F32)
    outbb = sb(f"{which}outb", [128, HID], BF16)

    for tt in range(NTT):
        half = tt // 4
        q = tt % 4
        dbuf = dsl[tt % 2]

        sl = s_l[tt % 2]
        sx = s_x[tt % 2]

        @p.on("sync")
        def _(sy, half=half, q=q, dbuf=dbuf, tr_prev=c.get(f"{which}_tr_slot{tt % 2}", 0), sl=sl):
            sy.wait_ge(cht.h, tr_prev)
            sy.dma_start(out=dbuf[:],
                         in_=src_sp[half, :, :, q * 128:(q + 1) * 128]
                         .rearrange("ht p t -> p ht t")).then_inc(sl.h, 16)
        sl.inc(16)
        l_tgt = sl.v

        @p.on("scalar")
        def _(s, tt=tt, xb=xrb[tt % 2], used=c.get(f"{which}_x_slot{tt % 2}", 0), sx=sx):
            s.wait_ge(chv.h, used)
            s.dma_start(out=xb[:], in_=x_src[tt * 128:(tt + 1) * 128, :]).then_inc(sx.h, 16)
        sx.inc(16)
        x_tgt = sx.v
        v0 = chv.v

        @p.on("tensor")
        def _(t, dbuf=dbuf, l_tgt=l_tgt, v0=v0, sl=sl):
            t.wait_ge(sl.h, l_tgt)
            for j in range(NKT):
                if v0 + j - 1 > 0:
                    t.wait_ge(chv.h, v0 + j - 1)
                pv = bank[j % 2].ap().bitcast(BF16)
                t.transpose(pv[:, 0:128], dbuf[:, j, :], c.ident[:])
        c[f"{which}_tr_slot{tt % 2}"] = cht.v
        pe0 = cht.v - NKT

        @p.on("vector")
        def _(v, pe0=pe0, xb=xrb[tt % 2], x_tgt=x_tgt, sx=sx):
            for j in range(NKT):
                v.wait_ge(cht.h, pe0 + j + 1)
                pv = bank[j % 2].ap().bitcast(BF16)
                v.tensor_copy(out=dtm[:, j * 128:(j + 1) * 128], in_=pv[:, 0:128])
            v.wait_ge(sx.h, x_tgt)
            v.tensor_tensor(out=lsum[:], in0=xb[:], in1=dtm[:], op=ALU.add)
            v.tensor_reduce(out=stat[:, 0:1], in_=lsum[:], axis=AX.X, op=ALU.add)
        c[f"{which}_x_slot{tt % 2}"] = chv.v
        sum_done = chv.v

        @p.on("scalar")
        def _(s, sum_done=sum_done):
            s.wait_ge(chv.h, sum_done)
            s.activation(out=lsq[:], in_=lsum[:], func=AF.Square)
        sq_done = chs.v

        @p.on("vector")
        def _(v, sq_done=sq_done):
            v.wait_ge(chs.h, sq_done)
            v.tensor_reduce(out=stat[:, 1:2], in_=lsq[:], axis=AX.X, op=ALU.add)
            v.tensor_scalar(out=stat[:, 0:2], in0=stat[:, 0:2], scalar1=1.0 / HID,
                            scalar2=0.0, op0=ALU.mult, op1=ALU.add)
            v.tensor_tensor(out=stat[:, 2:3], in0=stat[:, 0:1], in1=stat[:, 0:1], op=ALU.mult)
            v.tensor_tensor(out=stat[:, 2:3], in0=stat[:, 1:2], in1=stat[:, 2:3], op=ALU.subtract)
            v.tensor_scalar(out=stat[:, 2:3], in0=stat[:, 2:3], scalar1=1.0, scalar2=EPS,
                            op0=ALU.mult, op1=ALU.add)
        var_done = chv.v

        @p.on("scalar")
        def _(s, var_done=var_done):
            s.wait_ge(chv.h, var_done)
            s.activation(out=stat[:, 3:4], in_=stat[:, 2:3], func=AF.Ln)
        ln_done = chs.v

        @p.on("vector")
        def _(v, ln_done=ln_done):
            v.wait_ge(chs.h, ln_done)
            v.tensor_scalar(out=stat[:, 3:4], in0=stat[:, 3:4], scalar1=-0.5, scalar2=0.0,
                            op0=ALU.mult, op1=ALU.add)
        lnv_done = chv.v

        @p.on("scalar")
        def _(s, lnv_done=lnv_done):
            s.wait_ge(chv.h, lnv_done)
            s.activation(out=stat[:, 2:3], in_=stat[:, 3:4], func=AF.Exp)
        rstd_done = chs.v

        @p.on("vector")
        def _(v, rstd_done=rstd_done, o_prev=c.get(f"{which}_o_prev", 0), which2=which):
            v.wait_ge(chs.h, rstd_done)
            if o_prev:
                v.wait_ge(s_o.h, o_prev)
            v.tensor_scalar(out=outb[:], in0=lsum[:], scalar1=stat[:, 0:1],
                            scalar2=stat[:, 2:3], op0=ALU.subtract, op1=ALU.mult)
            if which2 == "ln1":
                v.tensor_copy(out=outbb[:], in_=outb[:])
        norm_done = chv.v

        @p.on("sync")
        def _(sy, tt=tt, norm_done=norm_done, which2=which):
            sy.wait_ge(chv.h, norm_done)
            dst = T["x1_sp"] if which2 == "ln1" else T["out"]
            sy.dma_start(out=dst[tt * 128:(tt + 1) * 128, :], in_=outb[:]).then_inc(s_o.h, 16)
        s_o.inc(16)
        c[f"{which}_o_prev"] = s_o.v

        if which == "ln1":
            xT_ln1 = c.xT_ln1
            v1 = chv.v

            @p.on("tensor")
            def _(t, norm_done=norm_done, v1=v1):
                t.wait_ge(chv.h, norm_done)
                for j in range(NKT):
                    if v1 + j - 1 > 0:
                        t.wait_ge(chv.h, v1 + j - 1)
                    xv = outbb[:].rearrange("p (a b e) -> p a b e", a=32, b=4, e=32)[:, :, :, j]
                    pv = bank[2 + j % 2].ap().bitcast(BF16)
                    t.transpose(pv[:, 0:128], xv, c.ident[:])
            c[f"{which}_tr_prev"] = cht.v
            pe1 = cht.v - NKT

            @p.on("vector")
            def _(v, tt=tt, pe1=pe1, xT_ln1=xT_ln1):
                for j in range(NKT):
                    v.wait_ge(cht.h, pe1 + j + 1)
                    pv = bank[2 + j % 2].ap().bitcast(BF16)
                    v.tensor_copy(out=xT_ln1[:, j, tt * 128:(tt + 1) * 128],
                                  in_=pv[:, 0:128])
            c[f"{which}_x_used"] = chv.v
    if which == "ln1":
        c.xT_done = chv.v
        c.x1_done = (s_o.h, s_o.v)
    else:
        @p.on("gpsimd")
        def _(g, tgt=s_o.v):
            g.wait_ge(s_o.h, tgt)


def emit_attn_out(c):
    T, p = c.T, c.p
    s_ao = c.sem("s_ao")

    @p.on("sync")
    def _(sy):
        sy.dma_start(out=T["out"], in_=T["x1_sp"]).then_inc(s_ao.h, 16)
    s_ao.inc(16)

    @p.on("gpsimd")
    def _(g, tgt=s_ao.v):
        g.wait_ge(s_ao.h, tgt)


def emit_x2_load(c):
    nc, T, sb, p, bank = c.nc, c.T, c.sb, c.p, c.bank
    cht, chv, chs = c.cht, c.chv, c.chs
    s_xl = [c.sem("s_fxl0"), c.sem("s_fxl1")]
    s_o = c.sem("s_fo")
    xbuf = [sb(f"fxb{i}", [128, HID], F32) for i in range(2)]
    xbb = sb("fxbb", [128, HID], BF16)
    xT_ln1 = c.xT_ln1

    for tt in range(NTT):
        buf = xbuf[tt % 2]

        sxl = s_xl[tt % 2]

        @p.on("sync")
        def _(sy, tt=tt, buf=buf, prev=c.get(f"fx_used_{tt - 2}", 0), sxl=sxl):
            sy.wait_ge(chv.h, prev)
            sy.dma_start(out=buf[:], in_=T["x2in"][tt * 128:(tt + 1) * 128, :]).then_inc(sxl.h, 16)
            sy.dma_start(out=T["x1_sp"][tt * 128:(tt + 1) * 128, :],
                         in_=T["x2in"][tt * 128:(tt + 1) * 128, :]).then_inc(s_o.h, 16)
        sxl.inc(16)
        s_o.inc(16)
        xl_tgt = sxl.v

        @p.on("vector")
        def _(v, buf=buf, xl_tgt=xl_tgt, prev_tr=c.get(f"fx_tr_{tt - 1}", 0), sxl=sxl):
            v.wait_ge(sxl.h, xl_tgt)
            v.wait_ge(cht.h, prev_tr)
            v.tensor_copy(out=xbb[:], in_=buf[:])
        cv_tgt = chv.v
        v1 = chv.v

        @p.on("tensor")
        def _(t, cv_tgt=cv_tgt, v1=v1, first=(tt == 0)):
            t.wait_ge(chv.h, cv_tgt)
            if first:
                wait_consts(t, c)
            for j in range(NKT):
                if v1 + j - 1 > 0:
                    t.wait_ge(chv.h, v1 + j - 1)
                xv = xbb[:].rearrange("p (a b e) -> p a b e", a=32, b=4, e=32)[:, :, :, j]
                pv = bank[j % 2].ap().bitcast(BF16)
                t.transpose(pv[:, 0:128], xv, c.ident[:])
        c[f"fx_tr_{tt}"] = cht.v
        pe0 = cht.v - NKT

        @p.on("vector")
        def _(v, tt=tt, pe0=pe0):
            for j in range(NKT):
                v.wait_ge(cht.h, pe0 + j + 1)
                pv = bank[j % 2].ap().bitcast(BF16)
                v.tensor_copy(out=xT_ln1[:, j, tt * 128:(tt + 1) * 128],
                              in_=pv[:, 0:128])
        c[f"fx_used_{tt}"] = chv.v
    c.xT_done = chv.v
    c.x1_done = (s_o.h, s_o.v)


def emit_gate_up(c):
    nc, T, sb, p, bank = c.nc, c.T, c.sb, c.p, c.bank
    cht, chv, chs = c.cht, c.chv, c.chs
    xT = c.xT_ln1
    s_pl = [c.sem("s_gpl0"), c.sem("s_gpl1")]
    s_sl = [c.sem("s_gsl0"), c.sem("s_gsl1")]
    s_zl = c.sem("s_gzl")
    s_hs = [c.sem("s_ghs0"), c.sem("s_ghs1")]
    s_Hsp = c.sem("s_gHsp")

    pkb = [sb(f"gpk{i}", [128, 2, NKT, 16], I32) for i in range(2)]
    nib = sb("gnib", [128, NKT * 128], I16)
    wsl = [sb(f"gws{i}", [128, 2, NKT, 128], BF16) for i in range(2)]
    ssl = [sb(f"gss{i}", [128, 2, 128], F32) for i in range(2)]
    negz = sb("gnegz", [32, 2 * NFF], BF16)
    Xs = sb("gXs", [32, M], BF16)
    est = sb("gest", [128, M], F32)
    sgt = sb("gsgt", [128, M], F32)
    hst = [sb(f"ghst{i}", [128, M], BF16) for i in range(2)]
    hrow = sb("ghrow", [1, M], BF16)

    @p.on("gpsimd")
    def _(g):
        g.dma_start(out=negz[:, 0:NFF], in_=T["zs_sp"][0:32, 0:NFF]).then_inc(s_zl.h, 16)
        g.dma_start(out=negz[:, NFF:2 * NFF], in_=T["zs_sp"][0:32, NFF:2 * NFF]).then_inc(s_zl.h, 16)
    s_zl.inc(32)

    @p.on("tensor")
    def _(t, xT_done=c.xT_done, zl=s_zl.v):
        t.wait_ge(chv.h, xT_done)
        for half in range(2):
            bnk = bank[6] if half == 0 else bank[7]
            for j in range(NKT):
                t.matmul(bnk.ap()[0:32, 0:512], c.G32[:],
                         xT[:, j, half * 512:(half + 1) * 512],
                         start=(j == 0), stop=(j == NKT - 1))
    Xmm = cht.v

    @p.on("vector")
    def _(v, tgt=Xmm):
        v.wait_ge(cht.h, tgt)
        v.tensor_copy(out=Xs[:, 0:512], in_=bank[6].ap()[0:32, 0:512])
        v.tensor_copy(out=Xs[:, 512:1024], in_=bank[7].ap()[0:32, 0:512])
    X_done = chv.v

    for f in range(NFT):
        pk = pkb[f % 2]
        ws = wsl[f % 2]
        ss = ssl[f % 2]
        dq_prev = c.get(f"g_dq_slot{f % 2}", 0)
        mm_prev = c.get(f"g_mm_slot{f % 2}", 0)

        spl = s_pl[f % 2]
        ssl_s = s_sl[f % 2]

        @p.on("sync")
        def _(sy, f=f, pk=pk, dq_prev=dq_prev, spl=spl):
            sy.wait_ge(chv.h, dq_prev)
            sy.dma_start(out=pk[:, 0], in_=T["gq_t"][f]).then_inc(spl.h, 16)
            sy.dma_start(out=pk[:, 1], in_=T["uq_t"][f]).then_inc(spl.h, 16)
        spl.inc(32)
        pl_tgt = spl.v

        @p.on("gpsimd")
        def _(g, f=f, ss=ss, dq_prev=dq_prev, ssl_s=ssl_s):
            g.wait_ge(chv.h, dq_prev)
            g.dma_start(out=ss[:, 0, :], in_=T["gsr"][:, f * 128:(f + 1) * 128]).then_inc(ssl_s.h, 16)
            g.dma_start(out=ss[:, 1, :], in_=T["usr"][:, f * 128:(f + 1) * 128]).then_inc(ssl_s.h, 16)
        ssl_s.inc(32)
        sl_tgt = ssl_s.v

        @p.on("vector")
        def _(v, f=f, pk=pk, ws=ws, ss=ss, pl_tgt=pl_tgt, sl_tgt=sl_tgt,
              mm_prev=mm_prev, spl=spl, ssl_s=ssl_s):
            v.wait_ge(spl.h, pl_tgt)
            v.wait_ge(ssl_s.h, sl_tgt)
            v.wait_ge(cht.h, mm_prev)
            for mat in range(2):
                z16 = pk[:, mat].bitcast(I16).rearrange("p j (w h) -> p j w h", h=2)
                ov = nib[:].rearrange("p (j w e h) -> p j w e h", j=NKT, e=4, h=2)
                for jb in range(4):
                    v.tensor_scalar(out=ov[:, :, :, jb, :], in0=z16, scalar1=4 * jb,
                                    scalar2=15, op0=ALU.logical_shift_right, op1=ALU.bitwise_and)
                nv = nib[:].rearrange("p (j n) -> p j n", j=NKT)
                for j in range(NKT):
                    v.tensor_tensor(out=ws[:, mat, j, :], in0=nv[:, j, :],
                                    in1=ss[:, mat, :], op=ALU.mult)
        c[f"g_dq_slot{f % 2}"] = chv.v
        dq_tgt = chv.v

        if f % 2 == 0:
            b_g, b_u = [bank[0], bank[1]], [bank[2], bank[3]]
        else:
            b_g, b_u = [bank[4], bank[5]], [bank[6], bank[7]]

        @p.on("tensor")
        def _(t, f=f, ws=ws, dq_tgt=dq_tgt, b_g=b_g, b_u=b_u, X_done=X_done,
              ev_prev2=c.get(f"g_ev_slot{f % 2}", 0), es_prev2=c.get(f"g_es_slot{f % 2}", 0),
              hv_prev=c.get(f"g_hv_slot{f % 2}", 0), zl=s_zl.v):
            t.wait_ge(chv.h, max(dq_tgt, X_done, ev_prev2, hv_prev))
            t.wait_ge(chs.h, es_prev2)
            if f == 0:
                t.wait_ge(s_zl.h, zl)
            for mat, bb in ((0, b_g), (1, b_u)):
                for half in range(2):
                    bnk = bb[half]
                    for j in range(NKT):
                        t.matmul(bnk.ap(), ws[:, mat, j, :],
                                 xT[:, j, half * 512:(half + 1) * 512],
                                 start=(j == 0), stop=False)
                    t.matmul(bnk.ap(), negz[:, mat * NFF + f * 128:mat * NFF + (f + 1) * 128],
                             Xs[:, half * 512:(half + 1) * 512], start=False, stop=True)
        c[f"g_mm_slot{f % 2}"] = cht.v
        mm_tgt = cht.v
        hb = hst[f % 2]

        @p.on("scalar")
        def _(s, mm_tgt=mm_tgt, b_g=b_g):
            s.wait_ge(cht.h, mm_tgt)
            for half in range(2):
                sl = slice(half * 512, (half + 1) * 512)
                s.activation(out=est[:, sl], in_=b_g[half].ap(), func=AF.Exp, scale=-1.0)
                s.activation(out=est[:, sl], in_=est[:, sl], func=AF.Ln, bias=1.0)
                s.activation(out=sgt[:, sl], in_=est[:, sl], func=AF.Exp, scale=-1.0)
        c[f"g_es_slot{f % 2}"] = chs.v
        es_tgt = chs.v

        shs = s_hs[f % 2]

        @p.on("vector")
        def _(v, es_tgt=es_tgt, b_g=b_g, b_u=b_u, hb=hb, hs_prev=c.get(f"g_hs_slot{f % 2}", 0), shs=shs):
            v.wait_ge(chs.h, es_tgt)
            if hs_prev:
                v.wait_ge(shs.h, hs_prev)
            for half in range(2):
                sl = slice(half * 512, (half + 1) * 512)
                v.tensor_tensor(out=sgt[:, sl], in0=b_g[half].ap(), in1=sgt[:, sl], op=ALU.mult)
                v.tensor_tensor(out=hb[:, sl], in0=sgt[:, sl], in1=b_u[half].ap(), op=ALU.mult)
        c[f"g_ev_slot{f % 2}"] = chv.v
        ev_tgt = chv.v

        @p.on("sync")
        def _(sy, f=f, hb=hb, ev_tgt=ev_tgt, shs=shs):
            sy.wait_ge(chv.h, ev_tgt)
            sy.dma_start(out=T["h_sp"][f], in_=hb[:]).then_inc(shs.h, 16)
        shs.inc(16)
        c[f"g_hs_slot{f % 2}"] = shs.v

        @p.on("tensor")
        def _(t, hb=hb, b_g=b_g, ev_tgt=ev_tgt):
            t.wait_ge(chv.h, ev_tgt)
            t.matmul(b_g[0].ap()[0:1, 0:512], c.ones1[:], hb[:, 0:512], start=True, stop=True)
            t.matmul(b_g[1].ap()[0:1, 0:512], c.ones1[:], hb[:, 512:1024], start=True, stop=True)
        Hm = cht.v

        @p.on("vector")
        def _(v, b_g=b_g, tgt=Hm, hsp_prev=s_Hsp.v):
            v.wait_ge(cht.h, tgt)
            if hsp_prev:
                v.wait_ge(s_Hsp.h, hsp_prev)
            v.tensor_copy(out=hrow[0:1, 0:512], in_=b_g[0].ap()[0:1, 0:512])
            v.tensor_copy(out=hrow[0:1, 512:1024], in_=b_g[1].ap()[0:1, 0:512])
        c[f"g_hv_slot{f % 2}"] = chv.v

        @p.on("scalar")
        def _(s, f=f, tgt=chv.v):
            s.wait_ge(chv.h, tgt)
            s.dma_start(out=T["H_sp"][f:f + 1, :], in_=hrow[:]).then_inc(s_Hsp.h, 16)
        s_Hsp.inc(16)
        c[f"g_hsp_slot{f % 2}"] = s_Hsp.v
    c.h_done = (s_hs[0].h, s_hs[0].v)
    c.h_done2 = (s_hs[1].h, s_hs[1].v)
    c.H_done = (s_Hsp.h, s_Hsp.v)


def emit_down(c):
    nc, T, sb, p, bank = c.nc, c.T, c.sb, c.p, c.bank
    cht, chv, chs = c.cht, c.chv, c.chs
    s_hl = c.sem("s_dhl")
    s_pl = [c.sem("s_dpl0"), c.sem("s_dpl1")]
    s_sl = c.sem("s_dsl")
    s_wl = [c.sem("s_dwl0"), c.sem("s_dwl1")]
    s_sp = c.sem("s_dsp")
    s_wsp = [c.sem("s_dwsp0"), c.sem("s_dwsp1")]
    s_zl = c.sem("s_dzl")
    s_H = c.sem("s_dH")

    hb = sb("dh", [128, NFT, 512], BF16)
    pkb = [sb(f"dpk{i}", [128, NFT, 16], I32) for i in range(2)]
    wd = [sb(f"dwd{i}", [128, NFT, 128], BF16) for i in range(2)]
    srd = sb("dsrd", [128, NFF], BF16)
    negzd = sb("dnegz", [NFT, HID], BF16)
    Hs = sb("dHs", [NFT, 512], BF16)
    dstage = sb("dstage", [128, 512], BF16)

    @p.on("gpsimd")
    def _(g):
        g.dma_start(out=negzd[:], in_=T["zs_sp"][0:NFT, 2 * NFF:2 * NFF + HID]).then_inc(s_zl.h, 16)
    s_zl.inc(16)

    for half in range(2):
        @p.on("sync")
        def _(sy, half=half, mm_prev=cht.v):
            sy.wait_ge(cht.h, mm_prev)
            sy.dma_start(out=hb[:],
                         in_=T["h_sp"].rearrange("f p t -> p f t")[:, :, half * 512:(half + 1) * 512]
                         ).then_inc(s_hl.h, 16)
        s_hl.inc(16)
        hl_tgt = s_hl.v

        @p.on("gpsimd")
        def _(g, half=half, H_used=c.get("d_H_used", 0)):
            g.wait_ge(cht.h, H_used)
            g.dma_start(out=Hs[:], in_=T["H_sp"][:, half * 512:(half + 1) * 512]
                        ).then_inc(s_H.h, 16)
        s_H.inc(16)
        H_done = s_H.v

        for ht in range(NKT):
            wdb = wd[ht % 2]
            if half == 0:
                pk = pkb[ht % 2]
                dq_prev = c.get(f"d_dq_slot{ht % 2}", 0)
                mm_slot = c.get(f"d_mm_slot{ht % 2}", 0)
                wsp_slot = c.get(f"d_wsp_slot{ht % 2}", 0)

                spl = s_pl[ht % 2]

                @p.on("sync")
                def _(sy, ht=ht, pk=pk, dq_prev=dq_prev, spl=spl):
                    sy.wait_ge(chv.h, dq_prev)
                    sy.dma_start(out=pk[:], in_=T["dq_t"][ht]).then_inc(spl.h, 16)
                spl.inc(16)
                pl_tgt = spl.v

                @p.on("gpsimd")
                def _(g, ht=ht, dq_all=c.get("d_dq_all", 0), zl=s_zl.v):
                    g.wait_ge(chv.h, dq_all)
                    if ht == 0:
                        g.wait_ge(s_zl.h, zl)
                    g.dma_start(out=srd[:], in_=T["dsr"][ht:ht + 1, :]
                                .broadcast_to([128, NFF])).then_inc(s_sl.h, 16)
                s_sl.inc(16)
                sl_tgt = s_sl.v

                swsp = s_wsp[ht % 2]

                @p.on("vector")
                def _(v, pk=pk, wdb=wdb, pl_tgt=pl_tgt, sl_tgt=sl_tgt,
                      mm_slot=mm_slot, wsp_slot=wsp_slot, spl=spl, swsp=swsp):
                    v.wait_ge(spl.h, pl_tgt)
                    v.wait_ge(s_sl.h, sl_tgt)
                    v.wait_ge(cht.h, mm_slot)
                    if wsp_slot:
                        v.wait_ge(swsp.h, wsp_slot)
                    z16 = pk[:].bitcast(I16).rearrange("p k (w h) -> p k w h", h=2)
                    wdi = wdb[:].rearrange("p k n -> p (k n)").bitcast(I16)
                    ov = wdi.rearrange("p (k w e h) -> p k w e h", k=NFT, e=4, h=2)
                    for jb in range(4):
                        v.tensor_scalar(out=ov[:, :, :, jb, :], in0=z16, scalar1=4 * jb,
                                        scalar2=15, op0=ALU.logical_shift_right, op1=ALU.bitwise_and)
                    nv = wdi.rearrange("p (k n) -> p k n", k=NFT)
                    sv = srd[:].rearrange("p (k n) -> p k n", k=NFT)
                    for kt in range(NFT):
                        v.tensor_tensor(out=wdb[:, kt, :], in0=nv[:, kt, :],
                                        in1=sv[:, kt, :], op=ALU.mult)
                c[f"d_dq_slot{ht % 2}"] = chv.v
                c["d_dq_all"] = chv.v
                w_sem, w_tgt = None, chv.v

                @p.on("scalar")
                def _(s, ht=ht, wdb=wdb, tgt=chv.v, swsp=swsp):
                    s.wait_ge(chv.h, tgt)
                    s.dma_start(out=T["wd_sp"][ht], in_=wdb[:].rearrange("p k n -> p (k n)")
                                ).then_inc(swsp.h, 16)
                swsp.inc(16)
                c[f"d_wsp_slot{ht % 2}"] = swsp.v
            else:
                mm_slot = c.get(f"d_mm_slot{ht % 2}", 0)

                swl = s_wl[ht % 2]

                @p.on("sync")
                def _(sy, ht=ht, wdb=wdb, mm_slot=mm_slot, swl=swl,
                      wsp_fin=s_wsp[ht % 2].v):
                    sy.wait_ge(cht.h, mm_slot)
                    sy.wait_ge(s_wsp[ht % 2].h, wsp_fin)
                    sy.dma_start(out=wdb[:].rearrange("p k n -> p (k n)"),
                                 in_=T["wd_sp"][ht]).then_inc(swl.h, 16)
                swl.inc(16)
                w_sem, w_tgt = swl, swl.v

            bnk = bank[ht % 2]
            ev_prev = c.get(f"d_ev_slot{ht % 2}", 0)

            @p.on("tensor")
            def _(t, ht=ht, wdb=wdb, bnk=bnk, w_sem=w_sem, w_tgt=w_tgt, H_done=H_done,
                  ev_prev=ev_prev, hl_tgt=hl_tgt):
                t.wait_ge(s_hl.h, hl_tgt)
                if w_sem is None:
                    t.wait_ge(chv.h, w_tgt)
                else:
                    t.wait_ge(w_sem.h, w_tgt)
                t.wait_ge(s_H.h, H_done)
                t.wait_ge(chv.h, ev_prev)
                for kt in range(NFT):
                    t.matmul(bnk.ap(), wdb[:, kt, :], hb[:, kt, :],
                             start=(kt == 0), stop=False)
                t.matmul(bnk.ap(), negzd[:, ht * 128:(ht + 1) * 128], Hs[:],
                         start=False, stop=True)
            c[f"d_mm_slot{ht % 2}"] = cht.v
            mm_tgt = cht.v

            @p.on("vector")
            def _(v, mm_tgt=mm_tgt, bnk=bnk, sp_prev=c.get("d_sp_prev", 0)):
                v.wait_ge(cht.h, mm_tgt)
                if sp_prev:
                    v.wait_ge(s_sp.h, sp_prev)
                v.tensor_copy(out=dstage[:], in_=bnk.ap())
            c[f"d_ev_slot{ht % 2}"] = chv.v

            @p.on("sync")
            def _(sy, half=half, ht=ht, tgt=chv.v):
                sy.wait_ge(chv.h, tgt)
                sy.dma_start(out=T["dn_sp"][half, ht], in_=dstage[:]).then_inc(s_sp.h, 16)
            s_sp.inc(16)
            c["d_sp_prev"] = s_sp.v
        c["d_H_used"] = cht.v
    c.dn_done = (s_sp.h, s_sp.v)


_NC_CACHE = {}


def _prep_inputs(inputs, b, hf):
    x = inputs["L0_attn_hidden_states"]
    wq = np.asarray(inputs["L0_attn_Wq"])
    wk = np.asarray(inputs["L0_attn_Wk"])
    wv = np.asarray(inputs["L0_attn_Wv"])
    wo = np.asarray(inputs["L0_attn_Wo"])
    gq = np.asarray(inputs["L0_ffn_gate_qweight"])
    gsc = np.asarray(inputs["L0_ffn_gate_scales"])
    gz = np.asarray(inputs["L0_ffn_gate_qzeros"])
    uq = np.asarray(inputs["L0_ffn_up_qweight"])
    usc = np.asarray(inputs["L0_ffn_up_scales"])
    uz = np.asarray(inputs["L0_ffn_up_qzeros"])
    dq = np.asarray(inputs["L0_ffn_down_qweight"])
    dsc = np.asarray(inputs["L0_ffn_down_scales"])
    dz = np.asarray(inputs["L0_ffn_down_qzeros"])

    glob = _prep_inputs._cache.get(id(inputs))
    if glob is None:
        wq_il = np.ascontiguousarray(wq[IL_PERM])
        wk_il = np.ascontiguousarray(wk[IL_PERM])
        wv_il = np.ascontiguousarray(wv[IL_PERM])
        gq_r = gq[IL_PERM].reshape(NKT, 128, 1376).transpose(1, 0, 2)
        gq_t = np.ascontiguousarray(
            gq_r.reshape(128, NKT, NFT, 16).transpose(2, 0, 1, 3))
        uq_r = uq[IL_PERM].reshape(NKT, 128, 1376).transpose(1, 0, 2)
        uq_t = np.ascontiguousarray(
            uq_r.reshape(128, NKT, NFT, 16).transpose(2, 0, 1, 3))
        dq_t = np.ascontiguousarray(
            dq.reshape(NFT, 128, 512).reshape(NFT, 128, NKT, 16).transpose(2, 1, 0, 3))
        gsr = np.ascontiguousarray(np.repeat(gsc, 4, axis=0))
        usr = np.ascontiguousarray(np.repeat(usc, 4, axis=0))
        dsr = np.ascontiguousarray(
            dsc.reshape(NFT, NKT, 128).transpose(1, 0, 2).reshape(NKT, NFF))
        glob = dict(wq_il=wq_il, wk_il=wk_il, wv_il=wv_il, wo=np.ascontiguousarray(wo),
                    gq_t=gq_t, uq_t=uq_t, dq_t=dq_t, gsr=gsr, usr=usr, dsr=dsr,
                    gz=np.ascontiguousarray(gz), uz=np.ascontiguousarray(uz),
                    dz=np.ascontiguousarray(dz), gsc=np.ascontiguousarray(gsc),
                    usc=np.ascontiguousarray(usc), dsc=np.ascontiguousarray(dsc))
        _prep_inputs._cache[id(inputs)] = glob

    x_main = np.ascontiguousarray(x[b, hf * M:(hf + 1) * M])
    x_pre = (np.ascontiguousarray(x[b, 0:M]) if hf == 1
             else np.zeros((M, HID), np.float32))
    start = hf * M
    d = dict(glob)
    d.update(
        x_main=x_main, x_pre=x_pre,
        identc=HC["ident"], Uc=HC["U"], maskPTc=HC["maskPT"], G32c=HC["G32"],
        ones1c=HC["ones1"], onesrc=HC["onesr"], maskTn=HC["maskT_norm"],
        maskTf=(HC["maskT_norm"] if hf == 1 else HC["maskT_first0"]),
        cosq=HC["cosq"][start:start + M], sinq=HC["sinq"][start:start + M],
        cosk=HC["cos"][start:start + M], sink=HC["sin"][start:start + M],
        coskh=HC["cos"][M - 64:M], sinkh=HC["sin"][M - 64:M],
    )
    return d


_prep_inputs._cache = {}


def kernel(**inputs):
    if "nc" not in _NC_CACHE:
        _NC_CACHE["nc"] = build_program("full")
    nc = _NC_CACHE["nc"]
    maps = []
    for core in range(8):
        b, hf = core // 2, core % 2
        maps.append(_prep_inputs(inputs, b, hf))
    _prep_inputs._cache.clear()
    res = run_bass_kernel_spmd(nc, maps, core_ids=list(range(8)))
    out = np.empty((B, N, HID), np.float32)
    for core in range(8):
        b, hf = core // 2, core % 2
        out[b, hf * M:(hf + 1) * M] = res.results[core]["out"]
    return out


# version 05
# speedup vs baseline: 1.7113x; 1.7113x over previous
import numpy as np
import ml_dtypes
from contextlib import ExitStack
import concourse.bass as bass
import concourse.mybir as mybir
from concourse.bass_utils import run_bass_kernel_spmd

F32 = mybir.dt.float32
BF16 = mybir.dt.bfloat16
I32 = mybir.dt.int32
I16 = mybir.dt.int16
ALU = mybir.AluOpType
AF = mybir.ActivationFunctionType
AX = mybir.AxisListType

B, N, HID = 4, 2048, 4096
H, KVH, D = 32, 8, 128
M = 1024
NTT = 8
NFF = 11008
NFT = 86
NKT = 32
GN = 16.0
THETA = 10000.0
EPS = 1e-5
NEG = -30000.0

ENGS = ("sync", "tensor", "vector", "scalar", "gpsimd")


def _il_perm():
    perm = np.empty(HID, dtype=np.int64)
    for j in range(NKT):
        for pp in range(128):
            perm[j * 128 + pp] = (pp // 4) * 128 + (pp % 4) * 32 + j
    return perm


IL_PERM = _il_perm()


def host_consts():
    ident = np.eye(128, dtype=np.float32)
    U = (np.triu(np.ones((128, 128))) * (-1.0 / GN)).astype(np.float32)
    maskPT = np.triu(np.ones((128, 128))).astype(np.float32)
    G32 = np.zeros((128, 32), np.float32)
    for pp in range(128):
        G32[pp, pp // 4] = 1.0
    ones1 = np.ones((128, 1), np.float32)
    onesr = np.ones((1, 128), np.float32)
    s_ = np.arange(128)[:, None]
    t_ = np.arange(64)[None, :]
    mvalid = (s_ >= t_) & (s_ <= t_ + 64)
    mT = np.where(mvalid, 0.0, NEG).astype(np.float32)
    maskT_norm = np.tile(mT, (1, 8))
    mA0 = np.where(mvalid & (s_ >= 64), 0.0, NEG).astype(np.float32)
    maskT_first0 = np.tile(np.concatenate([mA0, mT], axis=1), (1, 4))
    half = np.arange(D // 2, dtype=np.float64)
    inv = 1.0 / THETA ** (2.0 * half / D)
    pos = np.arange(N, dtype=np.float64)
    fr = pos[:, None] * inv[None, :]
    cos = np.cos(fr)
    sin = np.sin(fr)
    sc = 1.0 / np.sqrt(np.float64(D))
    return dict(ident=ident, U=U, maskPT=maskPT, G32=G32, ones1=ones1, onesr=onesr,
                maskT_norm=maskT_norm, maskT_first0=maskT_first0,
                cos=cos.astype(np.float32), sin=sin.astype(np.float32),
                cosq=(cos * sc).astype(np.float32), sinq=(sin * sc).astype(np.float32))


HC = host_consts()


class Sem:
    def __init__(self, handle):
        self.h = handle
        self.v = 0

    def inc(self, n=1):
        self.v += n
        return self.v


DATA_OPS = {"tensor_tensor", "tensor_scalar", "tensor_copy", "tensor_reduce",
            "reciprocal", "memset", "activation", "matmul", "transpose",
            "ldweights", "iota", "bn_stats", "bn_aggr"}
CHAINED = ("tensor", "vector", "scalar")


class RecIns:
    def __init__(self, auto=None):
        self.posts = []
        if auto is not None:
            self.posts.append(auto)
        self.auto = auto

    def then_inc(self, sem, n=1):
        assert self.auto is None, "explicit then_inc on chained compute op"
        self.posts.append((sem, n))
        return self


class Recorder:
    def __init__(self, p, eng):
        self.p = p
        self.eng = eng

    def wait_ge(self, sem, val):
        if val:
            self.p.q[self.eng].append(("w", sem, val))

    def nop(self):
        r = RecIns()
        self.p.q[self.eng].append(("nop", r))
        return r

    def dma_start(self, **kw):
        r = RecIns()
        self.p.q[self.eng].append(("dma", kw, r))
        return r

    def __getattr__(self, name):
        def g(*a, **k):
            ch = self.p.chain.get(self.eng) if name in DATA_OPS else None
            pre = ch.v if (ch is not None and ch.v and self.eng != "tensor") else None
            r = RecIns(auto=(ch.h, 1) if ch is not None else None)
            self.p.q[self.eng].append(("op", name, a, k, r, pre, ch))
            if ch is not None:
                ch.v += 1
            return r
        return g


class Prog:
    def __init__(self):
        self.q = {e: [] for e in ENGS}
        self.chain = {}

    def on(self, eng):
        rec = Recorder(self, eng)

        def deco(fn):
            fn(rec)
            return None
        return deco

    def replay(self, blk):
        for e in ENGS:
            recs = self.q[e]

            def section(h, recs=recs):
                for r in recs:
                    kind = r[0]
                    if kind == "w":
                        h.wait_ge(r[1], r[2])
                    elif kind == "nop":
                        ins = h.nop()
                        for sem, n in r[1].posts:
                            ins.then_inc(sem, n)
                    elif kind == "dma":
                        ins = h.dma_start(**r[1])
                        for sem, n in r[2].posts:
                            ins.then_inc(sem, n)
                    else:
                        _, name, a, k, ri, pre, ch = r
                        if pre is not None:
                            h.wait_ge(ch.h, pre)
                        ins = getattr(h, name)(*a, **k)
                        for sem, n in ri.posts:
                            ins.then_inc(sem, n)
            getattr(blk, e)(section)


class Ctx(dict):
    __getattr__ = dict.__getitem__
    __setattr__ = dict.__setitem__


def fence(c, waits=()):
    snap = {e: c.p.chain[e].v for e in CHAINED}
    for e in ENGS:
        @c.p.on(e)
        def _(h, e=e):
            for e2 in CHAINED:
                if e2 != e and snap[e2]:
                    h.wait_ge(c.p.chain[e2].h, snap[e2])
            for sem, val in waits:
                if val:
                    h.wait_ge(sem, val)


def build_program(mode="full"):
    nc = bass.Bass("TRN2", target_bir_lowering=False)
    T = {}

    def dram(name, shape, dty, kind="ExternalInput"):
        T[name] = nc.dram_tensor(name, shape, dty, kind=kind).ap()
        return T[name]

    dram("x_main", [M, HID], F32)
    dram("x_pre", [M, HID], F32)
    dram("wq_il", [HID, H * D], F32)
    dram("wk_il", [HID, KVH * D], F32)
    dram("wv_il", [HID, KVH * D], F32)
    dram("wo", [H * D, HID], F32)
    dram("gq_t", [NFT, 128, NKT, 16], I32)
    dram("uq_t", [NFT, 128, NKT, 16], I32)
    dram("dq_t", [NKT, 128, NFT, 16], I32)
    dram("gsr", [128, NFF], F32)
    dram("usr", [128, NFF], F32)
    dram("dsr", [NKT, NFF], F32)
    dram("gz", [32, NFF // 8], I32)
    dram("uz", [32, NFF // 8], I32)
    dram("dz", [NFT, 512], I32)
    dram("gsc", [32, NFF], F32)
    dram("usc", [32, NFF], F32)
    dram("dsc", [NFT, HID], F32)
    dram("identc", [128, 128], F32)
    dram("Uc", [128, 128], F32)
    dram("maskPTc", [128, 128], F32)
    dram("G32c", [128, 32], F32)
    dram("ones1c", [128, 1], F32)
    dram("onesrc", [1, 128], F32)
    dram("maskTn", [128, 512], F32)
    dram("maskTf", [128, 512], F32)
    dram("cosq", [M, 64], F32)
    dram("sinq", [M, 64], F32)
    dram("cosk", [M, 64], F32)
    dram("sink", [M, 64], F32)
    dram("coskh", [64, 64], F32)
    dram("sinkh", [64, 64], F32)
    dram("out", [M, HID], F32, kind="ExternalOutput")
    if mode == "ffn":
        dram("x2in", [M, HID], F32)
    dram("x1_sp", [M, HID], F32, kind="Internal")
    dram("h_sp", [NFT, 128, M], BF16, kind="Internal")
    dram("wd_sp", [NKT, 128, NFF], BF16, kind="Internal")
    dram("zs_sp", [128, 2 * NFF + HID], BF16, kind="Internal")
    dram("qs_sp", [NTT, 128, H * D], BF16, kind="Internal")
    dram("sq_sp", [NTT, 128, H * D], BF16, kind="Internal")
    dram("ks_sp", [16, 128, KVH * D], BF16, kind="Internal")
    dram("sk_sp", [NTT, 128, KVH * D], BF16, kind="Internal")
    dram("skh_sp", [64, KVH * D], BF16, kind="Internal")
    dram("gs_sp", [16, 128, KVH * D], F32, kind="Internal")
    dram("v_sp", [16, 128, KVH * D], BF16, kind="Internal")
    dram("dn_sp", [2, NKT, 128, 512], BF16, kind="Internal")
    dram("wo_sp", [2, NKT, 128, 512], BF16, kind="Internal")
    dram("H_sp", [NFT, M], BF16, kind="Internal")

    p = Prog()
    c = Ctx(nc=nc, T=T, p=p, mode=mode)
    es = ExitStack()
    with es:
        def sem(name):
            return Sem(es.enter_context(nc.semaphore(name)))
        c.sem = sem
        p.chain = {"tensor": sem("ch_t"), "vector": sem("ch_v"), "scalar": sem("ch_s")}
        c.cht = p.chain["tensor"]
        c.chv = p.chain["vector"]
        c.chs = p.chain["scalar"]
        c.bank = [es.enter_context(nc.psum_tensor(f"bank{i}", [128, 512], F32)) for i in range(8)]

        def mksb(stack):
            def sb(name, shape, dty):
                return stack.enter_context(nc.sbuf_tensor(name, shape, dty))
            return sb

        sb0 = mksb(es)
        c.ident = sb0("ident", [128, 128], BF16)
        c.identf = sb0("identf", [128, 128], F32)
        c.Uc = sb0("Uc_sb", [128, 128], F32)
        c.maskPT = sb0("maskPT", [128, 128], BF16)
        c.G32 = sb0("G32", [128, 32], BF16)
        c.ones1 = sb0("ones1", [128, 1], BF16)
        c.onesr = sb0("onesr", [1, 128], F32)
        c.maskTn = sb0("maskTn_sb", [128, 512], F32)
        c.maskTf = sb0("maskTf_sb", [128, 512], F32)
        emit_consts(c)

        with ExitStack() as s_p0:
            c.sb = mksb(s_p0)
            emit_zeros_prep(c)
        fence(c, waits=[c.zs_done])

        stk_xT1 = ExitStack()
        if mode == "ffn":
            with ExitStack() as s_x2:
                c.sb = mksb(s_x2)
                c.xT_ln1 = stk_xT1.enter_context(nc.sbuf_tensor("xT_ln1", [128, NKT, M], BF16))
                emit_x2_load(c)
            fence(c, waits=[c.x1_done])
        else:
            with ExitStack() as s_pa:
                c.sb = mksb(s_pa)
                emit_proj(c)
            fence(c, waits=[c.pa_sp, c.pa_spr, c.pa_spg])
            stk_oT = ExitStack()
            c.oT = stk_oT.enter_context(nc.sbuf_tensor("oT", [128, H, M], BF16))
            with ExitStack() as s_pb:
                c.sb = mksb(s_pb)
                emit_mixer(c)
            fence(c)
            with ExitStack() as s_pc:
                c.sb = mksb(s_pc)
                emit_wo(c)
            fence(c, waits=[c.wo_sp_done])
            stk_oT.close()
            c.xT_ln1 = stk_xT1.enter_context(nc.sbuf_tensor("xT_ln1", [128, NKT, M], BF16))
            with ExitStack() as s_pc2:
                c.sb = mksb(s_pc2)
                emit_postln(c, "ln1")
            fence(c, waits=[c.x1_done])

        if mode == "attn":
            emit_attn_out(c)
            stk_xT1.close()
        else:
            with ExitStack() as s_p4:
                c.sb = mksb(s_p4)
                emit_gate_up(c)
            fence(c, waits=[c.h_done, c.h_done2, c.H_done])
            stk_xT1.close()
            with ExitStack() as s_p5:
                c.sb = mksb(s_p5)
                emit_down(c)
            fence(c, waits=[c.dn_done])
            with ExitStack() as s_p6:
                c.sb = mksb(s_p6)
                emit_postln(c, "ln2")

        blk = es.enter_context(nc.Block())
        p.replay(blk)
    return nc


def emit_consts(c):
    T, p = c.T, c.p
    s_cg = c.sem("s_cg")
    s_cy = c.sem("s_cy")

    @p.on("gpsimd")
    def _(g):
        g.dma_start(out=c.ident[:], in_=T["identc"]).then_inc(s_cg.h, 16)
        g.dma_start(out=c.maskPT[:], in_=T["maskPTc"]).then_inc(s_cg.h, 16)
        g.dma_start(out=c.G32[:], in_=T["G32c"]).then_inc(s_cg.h, 16)
        g.dma_start(out=c.ones1[:], in_=T["ones1c"]).then_inc(s_cg.h, 16)
    s_cg.inc(64)

    @p.on("sync")
    def _(sy):
        sy.dma_start(out=c.Uc[:], in_=T["Uc"]).then_inc(s_cy.h, 16)
        sy.dma_start(out=c.identf[:], in_=T["identc"]).then_inc(s_cy.h, 16)
        sy.dma_start(out=c.onesr[:], in_=T["onesrc"]).then_inc(s_cy.h, 16)
        sy.dma_start(out=c.maskTn[:], in_=T["maskTn"]).then_inc(s_cy.h, 16)
        sy.dma_start(out=c.maskTf[:], in_=T["maskTf"]).then_inc(s_cy.h, 16)
    s_cy.inc(80)
    c.s_cg = s_cg
    c.s_cy = s_cy


def wait_consts(h, c):
    h.wait_ge(c.s_cg.h, c.s_cg.v)
    h.wait_ge(c.s_cy.h, c.s_cy.v)


def emit_zeros_prep(c):
    T, sb, p = c.T, c.sb, c.p
    chv = c.chv
    s_z = c.sem("s_z")
    s_zs = c.sem("s_zs")
    zq = sb("zq", [NFT, NFF // 8], I32)
    zsc = sb("zsc", [NFT, NFF], F32)
    znib = sb("znib", [NFT, NFF], I16)
    zout = sb("zout", [NFT, NFF], BF16)

    off = 0
    for zn, scn, rows, cols in (("gz", "gsc", 32, NFF), ("uz", "usc", 32, NFF),
                                ("dz", "dsc", NFT, HID)):
        @p.on("sync")
        def _(sy, zn=zn, scn=scn, rows=rows, cols=cols, prev=c.get("z_used", 0),
              prev_sp=c.get("z_spv", 0)):
            sy.wait_ge(chv.h, prev)
            if prev_sp:
                sy.wait_ge(s_zs.h, prev_sp)
            sy.dma_start(out=zq[0:rows, 0:cols // 8], in_=T[zn]).then_inc(s_z.h, 16)
            sy.dma_start(out=zsc[0:rows, 0:cols], in_=T[scn]).then_inc(s_z.h, 16)
        s_z.inc(32)
        tgt = s_z.v

        @p.on("vector")
        def _(v, rows=rows, cols=cols, tgt=tgt):
            v.wait_ge(s_z.h, tgt)
            z16 = zq[0:rows, 0:cols // 8].bitcast(I16).rearrange("r (w h) -> r w h", h=2)
            ov = znib[0:rows, 0:cols].rearrange("r (w e h) -> r w e h", e=4, h=2)
            for jb in range(4):
                v.tensor_scalar(out=ov[:, :, jb, :], in0=z16, scalar1=4 * jb, scalar2=15,
                                op0=ALU.logical_shift_right, op1=ALU.bitwise_and)
            v.tensor_scalar(out=zsc[0:rows, 0:cols], in0=zsc[0:rows, 0:cols],
                            scalar1=-1.0, scalar2=0.0, op0=ALU.mult, op1=ALU.add)
            v.tensor_tensor(out=zout[0:rows, 0:cols], in0=znib[0:rows, 0:cols],
                            in1=zsc[0:rows, 0:cols], op=ALU.mult)
        c["z_used"] = chv.v
        tgv = chv.v

        @p.on("sync")
        def _(sy, rows=rows, cols=cols, off=off, tgv=tgv):
            sy.wait_ge(chv.h, tgv)
            sy.dma_start(out=T["zs_sp"][0:rows, off:off + cols],
                         in_=zout[0:rows, 0:cols]).then_inc(s_zs.h, 16)
        s_zs.inc(16)
        c["z_spv"] = s_zs.v
        off += cols
    c.zs_done = (s_zs.h, s_zs.v)


def emit_proj(c):
    nc, T, sb, p, bank = c.nc, c.T, c.sb, c.p, c.bank
    cht, chv, chs = c.cht, c.chv, c.chs
    s_xl = [c.sem("s_xl0"), c.sem("s_xl1")]
    s_wl = [c.sem("s_wl0"), c.sem("s_wl1")]
    s_sp = c.sem("s_sp")
    s_spr = c.sem("s_spr")
    s_spg = c.sem("s_spg")
    s_cl = c.sem("s_cl")

    xbuf = [sb(f"xbuf{i}", [128, HID], F32) for i in range(2)]
    xT = sb("xT", [128, NKT, M], BF16)
    wsl = [sb(f"wsl{i}", [128, NKT, 512], BF16) for i in range(2)]
    expb = sb("expb", [128, 512], F32)
    sums = sb("sums", [128, 4], F32)
    recs = sb("recs", [128, 4], F32)
    pstage = sb("pstage", [128, 512], BF16)
    rstage = sb("rstage", [128, 512], BF16)
    gstage = sb("gstage", [128, 512], F32)
    ta = sb("ta", [128, 64], F32)
    tb = sb("tb", [128, 64], F32)
    cosb = sb("cosb", [128, 4, NTT, 64], F32)
    cosh = sb("cosh", [64, 128], F32)
    Bf = dict(expb=expb, sums=sums, recs=recs, pstage=pstage, rstage=rstage,
              gstage=gstage, ta=ta, tb=tb, cosb=cosb, cosh=cosh)
    Sm = dict(s_sp=s_sp, s_spr=s_spr, s_spg=s_spg, s_cl=s_cl)

    @p.on("scalar")
    def _(s):
        for i, nm in enumerate(("cosq", "sinq", "cosk", "sink")):
            s.dma_start(out=cosb[:, i],
                        in_=T[nm].rearrange("(tt p) f -> p tt f", p=128)).then_inc(s_cl.h, 16)
        s.dma_start(out=cosh[:, 0:64], in_=T["coskh"]).then_inc(s_cl.h, 16)
        s.dma_start(out=cosh[:, 64:128], in_=T["sinkh"]).then_inc(s_cl.h, 16)
    s_cl.inc(96)

    for phase in ("pre", "main"):
        xsrc = T["x_pre"] if phase == "pre" else T["x_main"]
        for tt in range(NTT):
            buf = xbuf[tt % 2]
            sxl = s_xl[tt % 2]

            @p.on("sync")
            def _(sy, tt=tt, buf=buf, prev_ev=c.get(f"pa_xtc_{tt - 2}", 0),
                  mmv=c.get("pa_mm_all", 0), sxl=sxl):
                sy.wait_ge(chv.h, prev_ev)
                sy.wait_ge(cht.h, mmv)
                sy.dma_start(out=buf[:], in_=xsrc[tt * 128:(tt + 1) * 128, :]).then_inc(sxl.h, 16)
            sxl.inc(16)
            xl_tgt = sxl.v
            v0 = chv.v

            @p.on("tensor")
            def _(t, buf=buf, xl_tgt=xl_tgt, v0=v0, first=(phase == "pre" and tt == 0), sxl=sxl):
                t.wait_ge(sxl.h, xl_tgt)
                if first:
                    wait_consts(t, c)
                for j in range(NKT):
                    if v0 + j - 1 > 0:
                        t.wait_ge(chv.h, v0 + j - 1)
                    xv = buf[:].rearrange("p (a b e) -> p a b e", a=32, b=4, e=32)[:, :, :, j]
                    pv = bank[j % 2].ap()
                    t.transpose(pv[:, 0:128], xv, c.identf[:])
            pe0 = cht.v - NKT

            @p.on("vector")
            def _(v, tt=tt, pe0=pe0):
                for j in range(NKT):
                    v.wait_ge(cht.h, pe0 + j + 1)
                    pv = bank[j % 2].ap()
                    v.tensor_copy(out=xT[:, j, tt * 128:(tt + 1) * 128], in_=pv[:, 0:128])
            c[f"pa_xtc_{tt}"] = chv.v
        xT_done = chv.v

        if phase == "main":
            groups = ([("q", fg) for fg in range(8)] +
                      [("k", 0), ("k", 1), ("v", 0), ("v", 1)])
        else:
            groups = [("k", 0), ("k", 1), ("v", 0), ("v", 1)]

        for gi, (kind, fg) in enumerate(groups):
            wslb = wsl[gi % 2]
            wsrc = {"q": "wq_il", "k": "wk_il", "v": "wv_il"}[kind]
            col0 = fg * 512

            swl = s_wl[gi % 2]

            @p.on("gpsimd")
            def _(g, wsrc=wsrc, col0=col0, wslb=wslb,
                  prev_mm=c.get(f"pa_mm_slot{gi % 2}", 0), xT_done=xT_done, swl=swl):
                g.wait_ge(cht.h, max(prev_mm, 0))
                g.wait_ge(chv.h, xT_done)
                g.dma_start(out=wslb[:],
                            in_=T[wsrc].rearrange("(j p) f -> p j f", p=128)[:, :, col0:col0 + 512]
                            ).then_inc(swl.h, 16)
            swl.inc(16)
            wl_tgt = swl.v

            for tt in range(NTT):
                bnk = bank[tt]
                ev_prev = c.get(f"pa_ev_{tt}", (0, 0))

                @p.on("tensor")
                def _(t, wslb=wslb, wl_tgt=wl_tgt, bnk=bnk, tt=tt,
                      ev_prev=ev_prev, xT_done=xT_done, swl=swl):
                    t.wait_ge(swl.h, wl_tgt)
                    t.wait_ge(chv.h, max(xT_done, ev_prev[0]))
                    if ev_prev[1]:
                        t.wait_ge(chs.h, ev_prev[1])
                    for j in range(NKT):
                        t.matmul(bnk.ap(), xT[:, j, tt * 128:(tt + 1) * 128],
                                 wslb[:, j, :], start=(j == 0), stop=(j == NKT - 1))
                mm_tgt = cht.v
                emit_proj_evac(c, phase, kind, fg, tt, bnk, mm_tgt, Bf, Sm)
                c[f"pa_ev_{tt}"] = (chv.v, chs.v)
            c[f"pa_mm_slot{gi % 2}"] = cht.v
        c["pa_mm_all"] = cht.v
    c.pa_sp = (s_sp.h, s_sp.v)
    c.pa_spr = (s_spr.h, s_spr.v)
    c.pa_spg = (s_spg.h, s_spg.v)


def emit_proj_evac(c, phase, kind, fg, tt, bnk, mm_tgt, Bf, Sm):
    T, p = c.T, c.p
    cht, chv, chs = c.cht, c.chv, c.chs
    s_sp, s_spr, s_spg, s_cl = Sm["s_sp"], Sm["s_spr"], Sm["s_spg"], Sm["s_cl"]
    expb, sums, recs = Bf["expb"], Bf["sums"], Bf["recs"]
    pstage, rstage, gstage = Bf["pstage"], Bf["rstage"], Bf["gstage"]
    ta, tb, cosb, cosh = Bf["ta"], Bf["tb"], Bf["cosb"], Bf["cosh"]
    main = (phase == "main")
    it = (NTT if main else 0) + tt

    if kind == "v":
        @p.on("scalar")
        def _(s, bnk=bnk, mm_tgt=mm_tgt, sp0=s_sp.v, vprev=chv.v):
            s.wait_ge(cht.h, mm_tgt)
            s.wait_ge(chv.h, vprev)
            if sp0:
                s.wait_ge(s_sp.h, sp0)
            s.activation(out=pstage[:], in_=bnk.ap(), func=AF.Identity)
        es_tgt = chs.v

        @p.on("sync")
        def _(sy, it=it, fg=fg, es_tgt=es_tgt):
            sy.wait_ge(chs.h, es_tgt)
            sy.dma_start(out=T["v_sp"][it, :, fg * 512:(fg + 1) * 512],
                         in_=pstage[:]).then_inc(s_sp.h, 16)
        s_sp.inc(16)
        return

    rope_this = main or (phase == "pre" and tt == 7 and kind == "k")
    cl_tgt = s_cl.v

    @p.on("scalar")
    def _(s, bnk=bnk, mm_tgt=mm_tgt, kind=kind, spg_pre=s_spg.v, vprev=chv.v):
        s.wait_ge(cht.h, mm_tgt)
        s.wait_ge(chv.h, vprev)
        if kind == "k" and spg_pre:
            s.wait_ge(s_spg.h, spg_pre)
        s.activation(out=expb[:], in_=bnk.ap(), func=AF.Exp)
        if kind == "k":
            s.activation(out=gstage[:], in_=bnk.ap(), func=AF.Exp, scale=-1.0)
            s.activation(out=gstage[:], in_=gstage[:], func=AF.Ln, bias=1.0)
    es_exp = chs.v

    @p.on("vector")
    def _(v, bnk=bnk, kind=kind, es_exp=es_exp, sp_pre=s_sp.v, spr_pre=s_spr.v,
          rope_this=rope_this, cl_tgt=cl_tgt, main=main):
        v.wait_ge(chs.h, es_exp)
        if sp_pre:
            v.wait_ge(s_sp.h, sp_pre)
        if rope_this and spr_pre:
            v.wait_ge(s_spr.h, spr_pre)
        for hh in range(4):
            v.tensor_reduce(out=sums[:, hh:hh + 1], in_=expb[:, hh * 128:(hh + 1) * 128],
                            axis=AX.X, op=ALU.add)
        if kind == "q":
            v.tensor_scalar(out=sums[:], in0=sums[:], scalar1=2.0, scalar2=0.0,
                            op0=ALU.mult, op1=ALU.add)
        v.reciprocal(out=recs[:], in_=sums[:])
        for hh in range(4):
            v.tensor_scalar(out=pstage[:, hh * 128:(hh + 1) * 128],
                            in0=expb[:, hh * 128:(hh + 1) * 128],
                            scalar1=recs[:, hh:hh + 1], scalar2=0.0,
                            op0=ALU.mult, op1=ALU.add)
        if rope_this:
            v.wait_ge(s_cl.h, cl_tgt)
            rows = slice(0, 128) if main else slice(64, 128)
            if main:
                ci = 0 if kind == "q" else 2
                cv = cosb[:, ci, tt, :]
                sv = cosb[:, ci + 1, tt, :]
            else:
                cv = cosh[:, 0:64]
                sv = cosh[:, 64:128]
            for hh in range(4):
                lo = slice(hh * 128, hh * 128 + 64)
                hi = slice(hh * 128 + 64, (hh + 1) * 128)
                pv = bnk.ap()
                crows = rows if main else slice(0, 64)
                cvr = cv[rows, :] if main else cv
                svr = sv[rows, :] if main else sv
                v.tensor_tensor(out=ta[rows, :], in0=pv[rows, lo], in1=cvr, op=ALU.mult)
                v.tensor_tensor(out=tb[rows, :], in0=pv[rows, hi], in1=svr, op=ALU.mult)
                v.tensor_tensor(out=rstage[rows, lo], in0=ta[rows, :], in1=tb[rows, :], op=ALU.subtract)
                v.tensor_tensor(out=ta[rows, :], in0=pv[rows, hi], in1=cvr, op=ALU.mult)
                v.tensor_tensor(out=tb[rows, :], in0=pv[rows, lo], in1=svr, op=ALU.mult)
                v.tensor_tensor(out=rstage[rows, hi], in0=ta[rows, :], in1=tb[rows, :], op=ALU.add)
    ev_done = chv.v

    @p.on("sync")
    def _(sy, kind=kind, fg=fg, it=it, tt=tt, ev_done=ev_done, rope_this=rope_this,
          main=main):
        sy.wait_ge(chv.h, ev_done)
        cols = slice(fg * 512, (fg + 1) * 512)
        if kind == "q":
            sy.dma_start(out=T["qs_sp"][tt, :, cols], in_=pstage[:]).then_inc(s_sp.h, 16)
            sy.dma_start(out=T["sq_sp"][tt, :, cols], in_=rstage[:]).then_inc(s_spr.h, 16)
        else:
            sy.dma_start(out=T["ks_sp"][it, :, cols], in_=pstage[:]).then_inc(s_sp.h, 16)
            sy.dma_start(out=T["gs_sp"][it, :, cols], in_=gstage[:]).then_inc(s_spg.h, 16)
            if main:
                sy.dma_start(out=T["sk_sp"][tt, :, cols], in_=rstage[:]).then_inc(s_spr.h, 16)
            elif rope_this:
                sy.dma_start(out=T["skh_sp"][:, cols], in_=rstage[64:128, :]).then_inc(s_spr.h, 16)
    s_sp.inc(16)
    if kind == "q" or (kind == "k" and (main or rope_this)):
        s_spr.inc(16)
    if kind == "k":
        s_spg.inc(16)


def emit_mixer(c):
    nc, T, sb, p, bank = c.nc, c.T, c.sb, c.p, c.bank
    cht, chv, chs = c.cht, c.chv, c.chs
    s_l = c.sem("s_bl")

    S = sb("S", [128, KVH * 128], F32)
    Sb = sb("Sb", [128, KVH * 128], BF16)
    skT = sb("skT", [128, KVH, 64 + M], BF16)
    oT = c.oT
    qs = sb("qs", [128, H * D], BF16)
    sq = sb("sq", [128, H * D], BF16)
    ks = sb("ks", [128, KVH * D], BF16)
    sk = sb("sk", [128, KVH * D], BF16)
    gs = sb("gs", [128, KVH * D], F32)
    vt = [sb(f"vt{i}", [128, KVH * D], BF16) for i in range(2)]
    qsT = sb("qsT", [128, H, 128], BF16)
    sqT = sb("sqT", [128, H, 128], BF16)
    koT = sb("koT", [128, 128], BF16)
    ko = sb("ko", [128, 128], BF16)
    eGT = sb("eGT", [128, 128], F32)
    enG = sb("enG", [128, 128], F32)
    qtT = sb("qtT", [128, 4, 128], BF16)
    PTg = sb("PTg", [128, 512], BF16)
    sws = sb("sws", [128, 512], F32)
    swe = sb("swe", [128, 512], BF16)
    swz = sb("swz", [1, 512], F32)
    PTs = sb("PTs", [128, 512], BF16)
    vwin = sb("vwin", [128, KVH * D], BF16)
    s_vw = c.sem("s_vw")
    c._s_vw = s_vw
    c._vwin = vwin

    @p.on("vector")
    def _(v):
        v.memset(S[:], 0.0)
        v.memset(Sb[:], 0.0)

    Bf = dict(S=S, Sb=Sb, skT=skT, oT=oT, qs=qs, sq=sq, ks=ks, sk=sk, gs=gs,
              qsT=qsT, sqT=sqT, koT=koT, ko=ko, eGT=eGT, enG=enG, qtT=qtT,
              PTg=PTg, sws=sws, swe=swe, swz=swz, PTs=PTs)

    for it in range(16):
        main = it >= NTT
        tt = it - NTT if main else it
        vtc = vt[it % 2]

        @p.on("sync")
        def _(sy, it=it, tt=tt, main=main, vtc=vtc, done=c.get("mix_done", 0)):
            sy.wait_ge(chv.h, done)
            sy.wait_ge(cht.h, c.get("mix_done_t", 0))
            sy.dma_start(out=ks[:], in_=T["ks_sp"][it]).then_inc(s_l.h, 16)
            sy.dma_start(out=gs[:], in_=T["gs_sp"][it]).then_inc(s_l.h, 16)
            sy.dma_start(out=vtc[:], in_=T["v_sp"][it]).then_inc(s_l.h, 16)
            if main:
                sy.dma_start(out=qs[:], in_=T["qs_sp"][tt]).then_inc(s_l.h, 16)
                sy.dma_start(out=sq[:], in_=T["sq_sp"][tt]).then_inc(s_l.h, 16)
                sy.dma_start(out=sk[:], in_=T["sk_sp"][tt]).then_inc(s_l.h, 16)
        s_l.inc(48 + (48 if main else 0))
        ld_tgt = s_l.v

        if main:
            vwin = c._vwin
            s_vw = c._s_vw

            @p.on("sync")
            def _(sy, ld_tgt=ld_tgt, vtc=vtc, vtp=vt[(it + 1) % 2]):
                sy.wait_ge(s_l.h, ld_tgt)
                sy.dma_start(out=vwin[0:64, :], in_=vtp[64:128, :]).then_inc(s_vw.h, 16)
                sy.dma_start(out=vwin[64:128, :], in_=vtc[0:64, :]).then_inc(s_vw.h, 16)
            s_vw.inc(32)
            c["vw_tgt"] = s_vw.v

            for grp, (srcb, dstb, nseg) in enumerate(((qs, qsT, H), (sq, sqT, H), (sk, None, KVH))):
                v0 = chv.v

                @p.on("tensor")
                def _(t, srcb=srcb, nseg=nseg, ld_tgt=ld_tgt, v0=v0, grp=grp):
                    if grp == 0:
                        t.wait_ge(s_l.h, ld_tgt)
                    for sg in range(nseg):
                        if v0 + sg - 1 > 0:
                            t.wait_ge(chv.h, v0 + sg - 1)
                        pv = bank[4 + sg % 2].ap().bitcast(BF16)
                        t.transpose(pv[:, 0:128], srcb[:, sg * 128:(sg + 1) * 128],
                                    c.ident[:])
                pe0 = cht.v - nseg

                @p.on("vector")
                def _(v, dstb=dstb, nseg=nseg, pe0=pe0, tt=tt, grp=grp):
                    for sg in range(nseg):
                        v.wait_ge(cht.h, pe0 + sg + 1)
                        pv = bank[4 + sg % 2].ap().bitcast(BF16)
                        if grp < 2:
                            v.tensor_copy(out=dstb[:, sg, :], in_=pv[:, 0:128])
                        else:
                            v.tensor_copy(out=skT[:, sg, 64 + tt * 128:64 + (tt + 1) * 128],
                                          in_=pv[:, 0:128])

        for kv in range(KVH):
            emit_chunk(c, it, tt, main, kv, Bf, vtc, vt[(it + 1) % 2], s_l, ld_tgt)
        c["mix_done"] = chv.v
        c["mix_done_t"] = cht.v


def emit_chunk(c, it, tt, main, kv, Bf, vtc, vtp, s_l, ld_tgt):
    p, bank = c.p, c.bank
    cht, chv, chs = c.cht, c.chv, c.chs
    S, Sb, skT, oT = Bf["S"], Bf["Sb"], Bf["skT"], Bf["oT"]
    qs, sq, ks, sk, gs = Bf["qs"], Bf["sq"], Bf["ks"], Bf["sk"], Bf["gs"]
    qsT, sqT, koT, ko = Bf["qsT"], Bf["sqT"], Bf["koT"], Bf["ko"]
    eGT, enG, qtT = Bf["eGT"], Bf["enG"], Bf["qtT"]
    PTg, sws, swe, swz, PTs = Bf["PTg"], Bf["sws"], Bf["swe"], Bf["swz"], Bf["PTs"]
    ksg = slice(kv * 128, (kv + 1) * 128)
    Sseg = slice(kv * 128, (kv + 1) * 128)

    bG = bank[6]
    bO = bank[2 if kv % 2 == 0 else 3]
    bW = bank[7]
    bD = bank[6]

    @p.on("tensor")
    def _(t, ld_tgt=ld_tgt, kv=kv, vpre=chv.v, spre=chs.v):
        if kv == 0 and not main:
            t.wait_ge(s_l.h, ld_tgt)
        t.wait_ge(chv.h, vpre)
        t.wait_ge(chs.h, spre)
        t.matmul(bG.ap()[:, 0:128], gs[:, ksg], c.Uc[:], start=True, stop=True)
        t.matmul(bG.ap()[:, 128:256], c.Uc[:], gs[:, ksg], start=True, stop=True)
    gmm = cht.v

    @p.on("scalar")
    def _(s, gmm=gmm):
        s.wait_ge(cht.h, gmm)
        s.activation(out=eGT[:], in_=bG.ap()[:, 0:128], func=AF.Exp)
        s.activation(out=enG[:], in_=bG.ap()[:, 128:256], func=AF.Exp, scale=-1.0)
    gex = chs.v

    @p.on("vector")
    def _(v, gex=gex, kv=kv):
        v.wait_ge(chs.h, gex)
        v.tensor_tensor(out=ko[:], in0=ks[:, ksg], in1=enG[:], op=ALU.mult)
    kov = chv.v

    if main:
        @p.on("tensor")
        def _(t, kov=kov):
            t.wait_ge(chv.h, kov)
            pv = bank[5].ap().bitcast(BF16)
            t.transpose(pv[:, 0:128], ko[:], c.ident[:])
        koT_mm = cht.v

        @p.on("vector")
        def _(v, tgt=koT_mm, kv=kv):
            v.wait_ge(cht.h, tgt)
            pv = bank[5].ap().bitcast(BF16)
            v.tensor_copy(out=koT[:], in_=pv[:, 0:128])
            for qi in range(4):
                qh = kv * 4 + qi
                v.tensor_tensor(out=qtT[:, qi, :], in0=qsT[:, qh, :], in1=eGT[:],
                                op=ALU.mult)
        qt_done = chv.v

        @p.on("tensor")
        def _(t, kv=kv, tt=tt, qt_done=qt_done):
            t.wait_ge(chv.h, qt_done)
            for qi in range(4):
                qh = kv * 4 + qi
                for blk_ in range(2):
                    w0 = tt * 128 + blk_ * 64
                    t.matmul(bW.ap()[:, (qi * 2 + blk_) * 64:(qi * 2 + blk_ + 1) * 64],
                             skT[:, kv, w0:w0 + 128],
                             sqT[:, qh, blk_ * 64:(blk_ + 1) * 64],
                             start=True, stop=True)
        swmm = cht.v

        @p.on("vector")
        def _(v, swmm=swmm, it=it):
            v.wait_ge(cht.h, swmm)
            mask = c.maskTf if it == NTT else c.maskTn
            v.tensor_tensor(out=sws[:], in0=bW.ap(), in1=mask[:], op=ALU.add)
        swm = chv.v

        @p.on("scalar")
        def _(s, swm=swm):
            s.wait_ge(chv.h, swm)
            s.activation(out=swe[:], in_=sws[:], func=AF.Exp)
        swex = chs.v

        @p.on("tensor")
        def _(t, swex=swex):
            t.wait_ge(chs.h, swex)
            t.matmul(bW.ap()[0:1, 0:512], c.ones1[:], swe[:, 0:512],
                     start=True, stop=True)
        zmm = cht.v

        @p.on("vector")
        def _(v, zmm=zmm):
            v.wait_ge(cht.h, zmm)
            v.reciprocal(out=swz[:], in_=bW.ap()[0:1, 0:512])
            v.tensor_scalar(out=swz[:], in0=swz[:], scalar1=0.5, scalar2=0.0,
                            op0=ALU.mult, op1=ALU.add)
        zrec = chv.v

        @p.on("tensor")
        def _(t, zrec=zrec):
            t.wait_ge(chv.h, zrec)
            t.matmul(bW.ap()[:, 0:512], c.onesr[:], swz[:], start=True, stop=True)
        repmm = cht.v

        @p.on("vector")
        def _(v, repmm=repmm):
            v.wait_ge(cht.h, repmm)
            v.tensor_tensor(out=PTs[:], in0=swe[:], in1=bW.ap(), op=ALU.mult)
        pts_done = chv.v

        @p.on("tensor")
        def _(t, kv=kv, qt_done=qt_done):
            t.wait_ge(chv.h, qt_done)
            for qi in range(4):
                t.matmul(bank[5].ap()[:, qi * 128:(qi + 1) * 128],
                         koT[:], qtT[:, qi, :], start=True, stop=True)
        ptgmm = cht.v

        @p.on("vector")
        def _(v, ptgmm=ptgmm):
            v.wait_ge(cht.h, ptgmm)
            for qi in range(4):
                v.tensor_tensor(out=PTg[:, qi * 128:(qi + 1) * 128],
                                in0=bank[5].ap()[:, qi * 128:(qi + 1) * 128],
                                in1=c.maskPT[:], op=ALU.mult)
        ptg_done = chv.v

        @p.on("tensor")
        def _(t, kv=kv, it=it, ptg_done=ptg_done, pts_done=pts_done,
              o_ev=c.get(f"oev_{kv % 2}", 0), vtc=vtc, vtp=vtp):
            t.wait_ge(chv.h, max(ptg_done, pts_done))
            if o_ev:
                t.wait_ge(chs.h, o_ev)
            t.wait_ge(c._s_vw.h, c.get("vw_tgt", 0))
            for qi in range(4):
                osl = bO.ap()[:, qi * 128:(qi + 1) * 128]
                t.matmul(osl, vtc[:, ksg], PTg[:, qi * 128:(qi + 1) * 128],
                         start=True, stop=False)
                t.matmul(osl, Sb[:, Sseg], qtT[:, qi, :], start=False, stop=False)
                pA = PTs[:, (qi * 2) * 64:(qi * 2 + 1) * 64]
                t.matmul(osl[:, 0:64], c._vwin[:, ksg], pA[:], start=False, stop=False)
                pB = PTs[:, (qi * 2 + 1) * 64:(qi * 2 + 2) * 64]
                t.matmul(osl[:, 64:128], vtc[:, ksg], pB[:], start=False, stop=True)
        omm = cht.v

        @p.on("scalar")
        def _(s, kv=kv, tt=tt, omm=omm):
            s.wait_ge(cht.h, omm)
            for qi in range(4):
                qh = kv * 4 + qi
                s.activation(out=oT[:, qh, tt * 128:(tt + 1) * 128],
                             in_=bO.ap()[:, qi * 128:(qi + 1) * 128],
                             func=AF.Identity)
        c[f"oev_{kv % 2}"] = chs.v

    @p.on("tensor")
    def _(t, kov=kov, vpre=chv.v, vtc=vtc):
        t.wait_ge(chv.h, max(kov, vpre))
        t.matmul(bD.ap()[:, 256:384], ko[:], vtc[:, ksg], start=True, stop=True)
    dmm = cht.v

    @p.on("vector")
    def _(v, dmm=dmm, kv=kv):
        v.wait_ge(cht.h, dmm)
        v.tensor_tensor(out=S[:, Sseg], in0=S[:, Sseg], in1=bD.ap()[:, 256:384], op=ALU.add)
        v.tensor_scalar(out=S[:, Sseg], in0=S[:, Sseg], scalar1=eGT[:, 127:128],
                        scalar2=0.0, op0=ALU.mult, op1=ALU.add)
        v.tensor_copy(out=Sb[:, Sseg], in_=S[:, Sseg])

    if it == NTT - 1 and kv == KVH - 1:
        s_halo = c.sem("s_halo")

        @p.on("sync")
        def _(sy, sp=chv.v):
            sy.wait_ge(chv.h, sp)
            sy.dma_start(out=sk[0:64, :], in_=c.T["skh_sp"]).then_inc(s_halo.h, 16)

        v0h = chv.v

        @p.on("tensor")
        def _(t, v0h=v0h):
            t.wait_ge(s_halo.h, 16)
            for kk in range(KVH):
                if v0h + kk - 1 > 0:
                    t.wait_ge(chv.h, v0h + kk - 1)
                pv = bank[4 + kk % 2].ap().bitcast(BF16)
                t.transpose(pv[:, 0:64], sk[0:64, kk * 128:(kk + 1) * 128], c.ident[0:64, 0:64])
        peh = cht.v - KVH

        @p.on("vector")
        def _(v, peh=peh):
            for kk in range(KVH):
                v.wait_ge(cht.h, peh + kk + 1)
                pv = bank[4 + kk % 2].ap().bitcast(BF16)
                v.tensor_copy(out=skT[:, kk, 0:64], in_=pv[:, 0:64])


def emit_wo(c):
    nc, T, sb, p, bank = c.nc, c.T, c.sb, c.p, c.bank
    cht, chv, chs = c.cht, c.chv, c.chs
    oT = c.oT
    s_wl = [c.sem("s_cwl0"), c.sem("s_cwl1")]
    s_sp = c.sem("s_csp")

    wsl = [sb(f"cwsl{i}", [128, NKT, 128], BF16) for i in range(2)]
    wo_st = sb("wo_st", [128, M], BF16)

    for ht in range(NKT):
        wslb = wsl[ht % 2]

        swl = s_wl[ht % 2]

        @p.on("gpsimd")
        def _(g, ht=ht, wslb=wslb, prev_mm=c.get(f"wo_mm_slot{ht % 2}", 0), swl=swl):
            g.wait_ge(cht.h, prev_mm)
            g.dma_start(out=wslb[:],
                        in_=T["wo"].rearrange("(o p) n -> p o n", p=128)[:, :, ht * 128:(ht + 1) * 128]
                        ).then_inc(swl.h, 16)
        swl.inc(16)
        wl_tgt = swl.v

        @p.on("tensor")
        def _(t, ht=ht, wslb=wslb, wl_tgt=wl_tgt, ev_prev=c.get("wo_ev_prev", 0), swl=swl):
            t.wait_ge(swl.h, wl_tgt)
            t.wait_ge(chs.h, ev_prev)
            for half in range(2):
                bnk = bank[half]
                for of in range(NKT):
                    t.matmul(bnk.ap(), wslb[:, of, :],
                             oT[:, of, half * 512:(half + 1) * 512],
                             start=(of == 0), stop=(of == NKT - 1))
        c[f"wo_mm_slot{ht % 2}"] = cht.v
        mm_tgt = cht.v

        @p.on("scalar")
        def _(s, mm_tgt=mm_tgt, spp=s_sp.v):
            s.wait_ge(cht.h, mm_tgt)
            if spp:
                s.wait_ge(s_sp.h, spp)
            s.activation(out=wo_st[:, 0:512], in_=bank[0].ap(), func=AF.Identity)
            s.activation(out=wo_st[:, 512:1024], in_=bank[1].ap(), func=AF.Identity)
        c["wo_ev_prev"] = chs.v
        ev_tgt = chs.v

        @p.on("sync")
        def _(sy, ht=ht, ev_tgt=ev_tgt):
            sy.wait_ge(chs.h, ev_tgt)
            sy.dma_start(out=T["wo_sp"][0, ht], in_=wo_st[:, 0:512]).then_inc(s_sp.h, 16)
            sy.dma_start(out=T["wo_sp"][1, ht], in_=wo_st[:, 512:1024]).then_inc(s_sp.h, 16)
        s_sp.inc(32)
    c.wo_sp_done = (s_sp.h, s_sp.v)


def emit_postln(c, which):
    nc, T, sb, p, bank = c.nc, c.T, c.sb, c.p, c.bank
    cht, chv, chs = c.cht, c.chv, c.chs
    s_l = [c.sem(f"s_{which}l0"), c.sem(f"s_{which}l1")]
    s_x = [c.sem(f"s_{which}x0"), c.sem(f"s_{which}x1")]
    s_o = c.sem(f"s_{which}o")
    src_sp = T["wo_sp"] if which == "ln1" else T["dn_sp"]
    x_src = T["x_main"] if which == "ln1" else T["x1_sp"]

    dsl = [sb(f"{which}dl{i}", [128, NKT, 128], BF16) for i in range(2)]
    xrb = [sb(f"{which}xr{i}", [128, HID], F32) for i in range(2)]
    dtm = sb(f"{which}dtm", [128, HID], BF16)
    lsum = sb(f"{which}sum", [128, HID], F32)
    lsq = sb(f"{which}sq", [128, HID], F32)
    stat = sb(f"{which}stat", [128, 4], F32)
    outb = sb(f"{which}out", [128, HID], F32)
    outbb = sb(f"{which}outb", [128, HID], BF16)

    for tt in range(NTT):
        half = tt // 4
        q = tt % 4
        dbuf = dsl[tt % 2]

        sl = s_l[tt % 2]
        sx = s_x[tt % 2]

        @p.on("sync")
        def _(sy, half=half, q=q, dbuf=dbuf, tr_prev=c.get(f"{which}_tr_slot{tt % 2}", 0), sl=sl):
            sy.wait_ge(cht.h, tr_prev)
            sy.dma_start(out=dbuf[:],
                         in_=src_sp[half, :, :, q * 128:(q + 1) * 128]
                         .rearrange("ht p t -> p ht t")).then_inc(sl.h, 16)
        sl.inc(16)
        l_tgt = sl.v

        @p.on("scalar")
        def _(s, tt=tt, xb=xrb[tt % 2], used=c.get(f"{which}_x_slot{tt % 2}", 0), sx=sx):
            s.wait_ge(chv.h, used)
            s.dma_start(out=xb[:], in_=x_src[tt * 128:(tt + 1) * 128, :]).then_inc(sx.h, 16)
        sx.inc(16)
        x_tgt = sx.v
        v0 = chv.v

        @p.on("tensor")
        def _(t, dbuf=dbuf, l_tgt=l_tgt, v0=v0, sl=sl):
            t.wait_ge(sl.h, l_tgt)
            for j in range(NKT):
                if v0 + j - 1 > 0:
                    t.wait_ge(chv.h, v0 + j - 1)
                pv = bank[j % 2].ap().bitcast(BF16)
                t.transpose(pv[:, 0:128], dbuf[:, j, :], c.ident[:])
        c[f"{which}_tr_slot{tt % 2}"] = cht.v
        pe0 = cht.v - NKT

        @p.on("vector")
        def _(v, pe0=pe0, xb=xrb[tt % 2], x_tgt=x_tgt, sx=sx):
            for j in range(NKT):
                v.wait_ge(cht.h, pe0 + j + 1)
                pv = bank[j % 2].ap().bitcast(BF16)
                v.tensor_copy(out=dtm[:, j * 128:(j + 1) * 128], in_=pv[:, 0:128])
            v.wait_ge(sx.h, x_tgt)
            v.tensor_tensor(out=lsum[:], in0=xb[:], in1=dtm[:], op=ALU.add)
            v.tensor_reduce(out=stat[:, 0:1], in_=lsum[:], axis=AX.X, op=ALU.add)
        c[f"{which}_x_slot{tt % 2}"] = chv.v
        sum_done = chv.v

        @p.on("scalar")
        def _(s, sum_done=sum_done):
            s.wait_ge(chv.h, sum_done)
            s.activation(out=lsq[:], in_=lsum[:], func=AF.Square)
        sq_done = chs.v

        @p.on("vector")
        def _(v, sq_done=sq_done):
            v.wait_ge(chs.h, sq_done)
            v.tensor_reduce(out=stat[:, 1:2], in_=lsq[:], axis=AX.X, op=ALU.add)
            v.tensor_scalar(out=stat[:, 0:2], in0=stat[:, 0:2], scalar1=1.0 / HID,
                            scalar2=0.0, op0=ALU.mult, op1=ALU.add)
            v.tensor_tensor(out=stat[:, 2:3], in0=stat[:, 0:1], in1=stat[:, 0:1], op=ALU.mult)
            v.tensor_tensor(out=stat[:, 2:3], in0=stat[:, 1:2], in1=stat[:, 2:3], op=ALU.subtract)
            v.tensor_scalar(out=stat[:, 2:3], in0=stat[:, 2:3], scalar1=1.0, scalar2=EPS,
                            op0=ALU.mult, op1=ALU.add)
        var_done = chv.v

        @p.on("scalar")
        def _(s, var_done=var_done):
            s.wait_ge(chv.h, var_done)
            s.activation(out=stat[:, 3:4], in_=stat[:, 2:3], func=AF.Ln)
        ln_done = chs.v

        @p.on("vector")
        def _(v, ln_done=ln_done):
            v.wait_ge(chs.h, ln_done)
            v.tensor_scalar(out=stat[:, 3:4], in0=stat[:, 3:4], scalar1=-0.5, scalar2=0.0,
                            op0=ALU.mult, op1=ALU.add)
        lnv_done = chv.v

        @p.on("scalar")
        def _(s, lnv_done=lnv_done):
            s.wait_ge(chv.h, lnv_done)
            s.activation(out=stat[:, 2:3], in_=stat[:, 3:4], func=AF.Exp)
        rstd_done = chs.v

        @p.on("vector")
        def _(v, rstd_done=rstd_done, o_prev=c.get(f"{which}_o_prev", 0), which2=which):
            v.wait_ge(chs.h, rstd_done)
            if o_prev:
                v.wait_ge(s_o.h, o_prev)
            v.tensor_scalar(out=outb[:], in0=lsum[:], scalar1=stat[:, 0:1],
                            scalar2=stat[:, 2:3], op0=ALU.subtract, op1=ALU.mult)
            if which2 == "ln1":
                v.tensor_copy(out=outbb[:], in_=outb[:])
        norm_done = chv.v

        @p.on("sync")
        def _(sy, tt=tt, norm_done=norm_done, which2=which):
            sy.wait_ge(chv.h, norm_done)
            dst = T["x1_sp"] if which2 == "ln1" else T["out"]
            sy.dma_start(out=dst[tt * 128:(tt + 1) * 128, :], in_=outb[:]).then_inc(s_o.h, 16)
        s_o.inc(16)
        c[f"{which}_o_prev"] = s_o.v

        if which == "ln1":
            xT_ln1 = c.xT_ln1
            v1 = chv.v

            @p.on("tensor")
            def _(t, norm_done=norm_done, v1=v1):
                t.wait_ge(chv.h, norm_done)
                for j in range(NKT):
                    if v1 + j - 1 > 0:
                        t.wait_ge(chv.h, v1 + j - 1)
                    xv = outbb[:].rearrange("p (a b e) -> p a b e", a=32, b=4, e=32)[:, :, :, j]
                    pv = bank[2 + j % 2].ap().bitcast(BF16)
                    t.transpose(pv[:, 0:128], xv, c.ident[:])
            c[f"{which}_tr_prev"] = cht.v
            pe1 = cht.v - NKT

            @p.on("vector")
            def _(v, tt=tt, pe1=pe1, xT_ln1=xT_ln1):
                for j in range(NKT):
                    v.wait_ge(cht.h, pe1 + j + 1)
                    pv = bank[2 + j % 2].ap().bitcast(BF16)
                    v.tensor_copy(out=xT_ln1[:, j, tt * 128:(tt + 1) * 128],
                                  in_=pv[:, 0:128])
            c[f"{which}_x_used"] = chv.v
    if which == "ln1":
        c.xT_done = chv.v
        c.x1_done = (s_o.h, s_o.v)
    else:
        @p.on("gpsimd")
        def _(g, tgt=s_o.v):
            g.wait_ge(s_o.h, tgt)


def emit_attn_out(c):
    T, p = c.T, c.p
    s_ao = c.sem("s_ao")

    @p.on("sync")
    def _(sy):
        sy.dma_start(out=T["out"], in_=T["x1_sp"]).then_inc(s_ao.h, 16)
    s_ao.inc(16)

    @p.on("gpsimd")
    def _(g, tgt=s_ao.v):
        g.wait_ge(s_ao.h, tgt)


def emit_x2_load(c):
    nc, T, sb, p, bank = c.nc, c.T, c.sb, c.p, c.bank
    cht, chv, chs = c.cht, c.chv, c.chs
    s_xl = [c.sem("s_fxl0"), c.sem("s_fxl1")]
    s_o = c.sem("s_fo")
    xbuf = [sb(f"fxb{i}", [128, HID], F32) for i in range(2)]
    xbb = sb("fxbb", [128, HID], BF16)
    xT_ln1 = c.xT_ln1

    for tt in range(NTT):
        buf = xbuf[tt % 2]

        sxl = s_xl[tt % 2]

        @p.on("sync")
        def _(sy, tt=tt, buf=buf, prev=c.get(f"fx_used_{tt - 2}", 0), sxl=sxl):
            sy.wait_ge(chv.h, prev)
            sy.dma_start(out=buf[:], in_=T["x2in"][tt * 128:(tt + 1) * 128, :]).then_inc(sxl.h, 16)
            sy.dma_start(out=T["x1_sp"][tt * 128:(tt + 1) * 128, :],
                         in_=T["x2in"][tt * 128:(tt + 1) * 128, :]).then_inc(s_o.h, 16)
        sxl.inc(16)
        s_o.inc(16)
        xl_tgt = sxl.v

        @p.on("vector")
        def _(v, buf=buf, xl_tgt=xl_tgt, prev_tr=c.get(f"fx_tr_{tt - 1}", 0), sxl=sxl):
            v.wait_ge(sxl.h, xl_tgt)
            v.wait_ge(cht.h, prev_tr)
            v.tensor_copy(out=xbb[:], in_=buf[:])
        cv_tgt = chv.v
        v1 = chv.v

        @p.on("tensor")
        def _(t, cv_tgt=cv_tgt, v1=v1, first=(tt == 0)):
            t.wait_ge(chv.h, cv_tgt)
            if first:
                wait_consts(t, c)
            for j in range(NKT):
                if v1 + j - 1 > 0:
                    t.wait_ge(chv.h, v1 + j - 1)
                xv = xbb[:].rearrange("p (a b e) -> p a b e", a=32, b=4, e=32)[:, :, :, j]
                pv = bank[j % 2].ap().bitcast(BF16)
                t.transpose(pv[:, 0:128], xv, c.ident[:])
        c[f"fx_tr_{tt}"] = cht.v
        pe0 = cht.v - NKT

        @p.on("vector")
        def _(v, tt=tt, pe0=pe0):
            for j in range(NKT):
                v.wait_ge(cht.h, pe0 + j + 1)
                pv = bank[j % 2].ap().bitcast(BF16)
                v.tensor_copy(out=xT_ln1[:, j, tt * 128:(tt + 1) * 128],
                              in_=pv[:, 0:128])
        c[f"fx_used_{tt}"] = chv.v
    c.xT_done = chv.v
    c.x1_done = (s_o.h, s_o.v)


def emit_gate_up(c):
    nc, T, sb, p, bank = c.nc, c.T, c.sb, c.p, c.bank
    cht, chv, chs = c.cht, c.chv, c.chs
    xT = c.xT_ln1
    s_pl = [c.sem("s_gpl0"), c.sem("s_gpl1")]
    s_sl = [c.sem("s_gsl0"), c.sem("s_gsl1")]
    s_zl = c.sem("s_gzl")
    s_hs = [c.sem("s_ghs0"), c.sem("s_ghs1")]
    s_Hsp = c.sem("s_gHsp")

    pkb = [sb(f"gpk{i}", [128, 2, NKT, 16], I32) for i in range(2)]
    nib = sb("gnib", [128, NKT * 128], I16)
    wsl = [sb(f"gws{i}", [128, 2, NKT, 128], BF16) for i in range(2)]
    ssl = [sb(f"gss{i}", [128, 2, 128], F32) for i in range(2)]
    negz = sb("gnegz", [32, 2 * NFF], BF16)
    Xs = sb("gXs", [32, M], BF16)
    est = sb("gest", [128, M], F32)
    sgt = sb("gsgt", [128, M], F32)
    hst = [sb(f"ghst{i}", [128, M], BF16) for i in range(2)]
    hrow = sb("ghrow", [1, M], BF16)

    @p.on("gpsimd")
    def _(g):
        g.dma_start(out=negz[:, 0:NFF], in_=T["zs_sp"][0:32, 0:NFF]).then_inc(s_zl.h, 16)
        g.dma_start(out=negz[:, NFF:2 * NFF], in_=T["zs_sp"][0:32, NFF:2 * NFF]).then_inc(s_zl.h, 16)
    s_zl.inc(32)

    @p.on("tensor")
    def _(t, xT_done=c.xT_done, zl=s_zl.v):
        t.wait_ge(chv.h, xT_done)
        for half in range(2):
            bnk = bank[6] if half == 0 else bank[7]
            for j in range(NKT):
                t.matmul(bnk.ap()[0:32, 0:512], c.G32[:],
                         xT[:, j, half * 512:(half + 1) * 512],
                         start=(j == 0), stop=(j == NKT - 1))
    Xmm = cht.v

    @p.on("vector")
    def _(v, tgt=Xmm):
        v.wait_ge(cht.h, tgt)
        v.tensor_copy(out=Xs[:, 0:512], in_=bank[6].ap()[0:32, 0:512])
        v.tensor_copy(out=Xs[:, 512:1024], in_=bank[7].ap()[0:32, 0:512])
    X_done = chv.v

    for f in range(NFT):
        pk = pkb[f % 2]
        ws = wsl[f % 2]
        ss = ssl[f % 2]
        dq_prev = c.get(f"g_dq_slot{f % 2}", 0)
        mm_prev = c.get(f"g_mm_slot{f % 2}", 0)

        spl = s_pl[f % 2]
        ssl_s = s_sl[f % 2]

        @p.on("sync")
        def _(sy, f=f, pk=pk, dq_prev=dq_prev, spl=spl):
            sy.wait_ge(chv.h, dq_prev)
            sy.dma_start(out=pk[:, 0], in_=T["gq_t"][f]).then_inc(spl.h, 16)
            sy.dma_start(out=pk[:, 1], in_=T["uq_t"][f]).then_inc(spl.h, 16)
        spl.inc(32)
        pl_tgt = spl.v

        @p.on("gpsimd")
        def _(g, f=f, ss=ss, dq_prev=dq_prev, ssl_s=ssl_s):
            g.wait_ge(chv.h, dq_prev)
            g.dma_start(out=ss[:, 0, :], in_=T["gsr"][:, f * 128:(f + 1) * 128]).then_inc(ssl_s.h, 16)
            g.dma_start(out=ss[:, 1, :], in_=T["usr"][:, f * 128:(f + 1) * 128]).then_inc(ssl_s.h, 16)
        ssl_s.inc(32)
        sl_tgt = ssl_s.v

        @p.on("vector")
        def _(v, f=f, pk=pk, ws=ws, ss=ss, pl_tgt=pl_tgt, sl_tgt=sl_tgt,
              mm_prev=mm_prev, spl=spl, ssl_s=ssl_s):
            v.wait_ge(spl.h, pl_tgt)
            v.wait_ge(ssl_s.h, sl_tgt)
            v.wait_ge(cht.h, mm_prev)
            for mat in range(2):
                z16 = pk[:, mat].bitcast(I16).rearrange("p j (w h) -> p j w h", h=2)
                ov = nib[:].rearrange("p (j w e h) -> p j w e h", j=NKT, e=4, h=2)
                for jb in range(4):
                    v.tensor_scalar(out=ov[:, :, :, jb, :], in0=z16, scalar1=4 * jb,
                                    scalar2=15, op0=ALU.logical_shift_right, op1=ALU.bitwise_and)
                nv = nib[:].rearrange("p (j n) -> p j n", j=NKT)
                for j in range(NKT):
                    v.tensor_tensor(out=ws[:, mat, j, :], in0=nv[:, j, :],
                                    in1=ss[:, mat, :], op=ALU.mult)
        c[f"g_dq_slot{f % 2}"] = chv.v
        dq_tgt = chv.v

        if f % 2 == 0:
            b_g, b_u = [bank[0], bank[1]], [bank[2], bank[3]]
        else:
            b_g, b_u = [bank[4], bank[5]], [bank[6], bank[7]]

        @p.on("tensor")
        def _(t, f=f, ws=ws, dq_tgt=dq_tgt, b_g=b_g, b_u=b_u, X_done=X_done,
              ev_prev2=c.get(f"g_ev_slot{f % 2}", 0), es_prev2=c.get(f"g_es_slot{f % 2}", 0),
              hv_prev=c.get(f"g_hv_slot{f % 2}", 0), zl=s_zl.v):
            t.wait_ge(chv.h, max(dq_tgt, X_done, ev_prev2, hv_prev))
            t.wait_ge(chs.h, es_prev2)
            if f == 0:
                t.wait_ge(s_zl.h, zl)
            for mat, bb in ((0, b_g), (1, b_u)):
                for half in range(2):
                    bnk = bb[half]
                    for j in range(NKT):
                        t.matmul(bnk.ap(), ws[:, mat, j, :],
                                 xT[:, j, half * 512:(half + 1) * 512],
                                 start=(j == 0), stop=False)
                    t.matmul(bnk.ap(), negz[:, mat * NFF + f * 128:mat * NFF + (f + 1) * 128],
                             Xs[:, half * 512:(half + 1) * 512], start=False, stop=True)
        c[f"g_mm_slot{f % 2}"] = cht.v
        mm_tgt = cht.v
        hb = hst[f % 2]

        @p.on("scalar")
        def _(s, mm_tgt=mm_tgt, b_g=b_g):
            s.wait_ge(cht.h, mm_tgt)
            for half in range(2):
                sl = slice(half * 512, (half + 1) * 512)
                s.activation(out=est[:, sl], in_=b_g[half].ap(), func=AF.Exp, scale=-1.0)
                s.activation(out=est[:, sl], in_=est[:, sl], func=AF.Ln, bias=1.0)
                s.activation(out=sgt[:, sl], in_=est[:, sl], func=AF.Exp, scale=-1.0)
        c[f"g_es_slot{f % 2}"] = chs.v
        es_tgt = chs.v

        shs = s_hs[f % 2]

        @p.on("vector")
        def _(v, es_tgt=es_tgt, b_g=b_g, b_u=b_u, hb=hb, hs_prev=c.get(f"g_hs_slot{f % 2}", 0), shs=shs):
            v.wait_ge(chs.h, es_tgt)
            if hs_prev:
                v.wait_ge(shs.h, hs_prev)
            for half in range(2):
                sl = slice(half * 512, (half + 1) * 512)
                v.tensor_tensor(out=sgt[:, sl], in0=b_g[half].ap(), in1=sgt[:, sl], op=ALU.mult)
                v.tensor_tensor(out=hb[:, sl], in0=sgt[:, sl], in1=b_u[half].ap(), op=ALU.mult)
        c[f"g_ev_slot{f % 2}"] = chv.v
        ev_tgt = chv.v

        @p.on("sync")
        def _(sy, f=f, hb=hb, ev_tgt=ev_tgt, shs=shs):
            sy.wait_ge(chv.h, ev_tgt)
            sy.dma_start(out=T["h_sp"][f], in_=hb[:]).then_inc(shs.h, 16)
        shs.inc(16)
        c[f"g_hs_slot{f % 2}"] = shs.v

        @p.on("tensor")
        def _(t, hb=hb, b_g=b_g, ev_tgt=ev_tgt):
            t.wait_ge(chv.h, ev_tgt)
            t.matmul(b_g[0].ap()[0:1, 0:512], c.ones1[:], hb[:, 0:512], start=True, stop=True)
            t.matmul(b_g[1].ap()[0:1, 0:512], c.ones1[:], hb[:, 512:1024], start=True, stop=True)
        Hm = cht.v

        @p.on("vector")
        def _(v, b_g=b_g, tgt=Hm, hsp_prev=s_Hsp.v):
            v.wait_ge(cht.h, tgt)
            if hsp_prev:
                v.wait_ge(s_Hsp.h, hsp_prev)
            v.tensor_copy(out=hrow[0:1, 0:512], in_=b_g[0].ap()[0:1, 0:512])
            v.tensor_copy(out=hrow[0:1, 512:1024], in_=b_g[1].ap()[0:1, 0:512])
        c[f"g_hv_slot{f % 2}"] = chv.v

        @p.on("scalar")
        def _(s, f=f, tgt=chv.v):
            s.wait_ge(chv.h, tgt)
            s.dma_start(out=T["H_sp"][f:f + 1, :], in_=hrow[:]).then_inc(s_Hsp.h, 16)
        s_Hsp.inc(16)
        c[f"g_hsp_slot{f % 2}"] = s_Hsp.v
    c.h_done = (s_hs[0].h, s_hs[0].v)
    c.h_done2 = (s_hs[1].h, s_hs[1].v)
    c.H_done = (s_Hsp.h, s_Hsp.v)


def emit_down(c):
    nc, T, sb, p, bank = c.nc, c.T, c.sb, c.p, c.bank
    cht, chv, chs = c.cht, c.chv, c.chs
    s_hl = c.sem("s_dhl")
    s_pl = [c.sem("s_dpl0"), c.sem("s_dpl1")]
    s_sl = c.sem("s_dsl")
    s_wl = [c.sem("s_dwl0"), c.sem("s_dwl1")]
    s_sp = c.sem("s_dsp")
    s_wsp = [c.sem("s_dwsp0"), c.sem("s_dwsp1")]
    s_zl = c.sem("s_dzl")
    s_H = c.sem("s_dH")

    hb = sb("dh", [128, NFT, 512], BF16)
    pkb = [sb(f"dpk{i}", [128, NFT, 16], I32) for i in range(2)]
    wd = [sb(f"dwd{i}", [128, NFT, 128], BF16) for i in range(2)]
    srd = sb("dsrd", [128, NFF], BF16)
    negzd = sb("dnegz", [NFT, HID], BF16)
    Hs = sb("dHs", [NFT, 512], BF16)
    dstage = sb("dstage", [128, 512], BF16)

    @p.on("gpsimd")
    def _(g):
        g.dma_start(out=negzd[:], in_=T["zs_sp"][0:NFT, 2 * NFF:2 * NFF + HID]).then_inc(s_zl.h, 16)
    s_zl.inc(16)

    for half in range(2):
        @p.on("sync")
        def _(sy, half=half, mm_prev=cht.v):
            sy.wait_ge(cht.h, mm_prev)
            sy.dma_start(out=hb[:],
                         in_=T["h_sp"].rearrange("f p t -> p f t")[:, :, half * 512:(half + 1) * 512]
                         ).then_inc(s_hl.h, 16)
        s_hl.inc(16)
        hl_tgt = s_hl.v

        @p.on("gpsimd")
        def _(g, half=half, H_used=c.get("d_H_used", 0)):
            g.wait_ge(cht.h, H_used)
            g.dma_start(out=Hs[:], in_=T["H_sp"][:, half * 512:(half + 1) * 512]
                        ).then_inc(s_H.h, 16)
        s_H.inc(16)
        H_done = s_H.v

        for ht in range(NKT):
            wdb = wd[ht % 2]
            if half == 0:
                pk = pkb[ht % 2]
                dq_prev = c.get(f"d_dq_slot{ht % 2}", 0)
                mm_slot = c.get(f"d_mm_slot{ht % 2}", 0)
                wsp_slot = c.get(f"d_wsp_slot{ht % 2}", 0)

                spl = s_pl[ht % 2]

                @p.on("sync")
                def _(sy, ht=ht, pk=pk, dq_prev=dq_prev, spl=spl):
                    sy.wait_ge(chv.h, dq_prev)
                    sy.dma_start(out=pk[:], in_=T["dq_t"][ht]).then_inc(spl.h, 16)
                spl.inc(16)
                pl_tgt = spl.v

                @p.on("gpsimd")
                def _(g, ht=ht, dq_all=c.get("d_dq_all", 0), zl=s_zl.v):
                    g.wait_ge(chv.h, dq_all)
                    if ht == 0:
                        g.wait_ge(s_zl.h, zl)
                    g.dma_start(out=srd[:], in_=T["dsr"][ht:ht + 1, :]
                                .broadcast_to([128, NFF])).then_inc(s_sl.h, 16)
                s_sl.inc(16)
                sl_tgt = s_sl.v

                swsp = s_wsp[ht % 2]

                @p.on("vector")
                def _(v, pk=pk, wdb=wdb, pl_tgt=pl_tgt, sl_tgt=sl_tgt,
                      mm_slot=mm_slot, wsp_slot=wsp_slot, spl=spl, swsp=swsp):
                    v.wait_ge(spl.h, pl_tgt)
                    v.wait_ge(s_sl.h, sl_tgt)
                    v.wait_ge(cht.h, mm_slot)
                    if wsp_slot:
                        v.wait_ge(swsp.h, wsp_slot)
                    z16 = pk[:].bitcast(I16).rearrange("p k (w h) -> p k w h", h=2)
                    wdi = wdb[:].rearrange("p k n -> p (k n)").bitcast(I16)
                    ov = wdi.rearrange("p (k w e h) -> p k w e h", k=NFT, e=4, h=2)
                    for jb in range(4):
                        v.tensor_scalar(out=ov[:, :, :, jb, :], in0=z16, scalar1=4 * jb,
                                        scalar2=15, op0=ALU.logical_shift_right, op1=ALU.bitwise_and)
                    nv = wdi.rearrange("p (k n) -> p k n", k=NFT)
                    sv = srd[:].rearrange("p (k n) -> p k n", k=NFT)
                    for kt in range(NFT):
                        v.tensor_tensor(out=wdb[:, kt, :], in0=nv[:, kt, :],
                                        in1=sv[:, kt, :], op=ALU.mult)
                c[f"d_dq_slot{ht % 2}"] = chv.v
                c["d_dq_all"] = chv.v
                w_sem, w_tgt = None, chv.v

                @p.on("scalar")
                def _(s, ht=ht, wdb=wdb, tgt=chv.v, swsp=swsp):
                    s.wait_ge(chv.h, tgt)
                    s.dma_start(out=T["wd_sp"][ht], in_=wdb[:].rearrange("p k n -> p (k n)")
                                ).then_inc(swsp.h, 16)
                swsp.inc(16)
                c[f"d_wsp_slot{ht % 2}"] = swsp.v
            else:
                mm_slot = c.get(f"d_mm_slot{ht % 2}", 0)

                swl = s_wl[ht % 2]

                @p.on("sync")
                def _(sy, ht=ht, wdb=wdb, mm_slot=mm_slot, swl=swl,
                      wsp_fin=s_wsp[ht % 2].v):
                    sy.wait_ge(cht.h, mm_slot)
                    sy.wait_ge(s_wsp[ht % 2].h, wsp_fin)
                    sy.dma_start(out=wdb[:].rearrange("p k n -> p (k n)"),
                                 in_=T["wd_sp"][ht]).then_inc(swl.h, 16)
                swl.inc(16)
                w_sem, w_tgt = swl, swl.v

            bnk = bank[ht % 2]
            ev_prev = c.get(f"d_ev_slot{ht % 2}", 0)

            @p.on("tensor")
            def _(t, ht=ht, wdb=wdb, bnk=bnk, w_sem=w_sem, w_tgt=w_tgt, H_done=H_done,
                  ev_prev=ev_prev, hl_tgt=hl_tgt):
                t.wait_ge(s_hl.h, hl_tgt)
                if w_sem is None:
                    t.wait_ge(chv.h, w_tgt)
                else:
                    t.wait_ge(w_sem.h, w_tgt)
                t.wait_ge(s_H.h, H_done)
                t.wait_ge(chv.h, ev_prev)
                for kt in range(NFT):
                    t.matmul(bnk.ap(), wdb[:, kt, :], hb[:, kt, :],
                             start=(kt == 0), stop=False)
                t.matmul(bnk.ap(), negzd[:, ht * 128:(ht + 1) * 128], Hs[:],
                         start=False, stop=True)
            c[f"d_mm_slot{ht % 2}"] = cht.v
            mm_tgt = cht.v

            @p.on("vector")
            def _(v, mm_tgt=mm_tgt, bnk=bnk, sp_prev=c.get("d_sp_prev", 0)):
                v.wait_ge(cht.h, mm_tgt)
                if sp_prev:
                    v.wait_ge(s_sp.h, sp_prev)
                v.tensor_copy(out=dstage[:], in_=bnk.ap())
            c[f"d_ev_slot{ht % 2}"] = chv.v

            @p.on("sync")
            def _(sy, half=half, ht=ht, tgt=chv.v):
                sy.wait_ge(chv.h, tgt)
                sy.dma_start(out=T["dn_sp"][half, ht], in_=dstage[:]).then_inc(s_sp.h, 16)
            s_sp.inc(16)
            c["d_sp_prev"] = s_sp.v
        c["d_H_used"] = cht.v
    c.dn_done = (s_sp.h, s_sp.v)


_NC_CACHE = {}


def _prep_inputs(inputs, b, hf):
    x = inputs["L0_attn_hidden_states"]
    wq = np.asarray(inputs["L0_attn_Wq"])
    wk = np.asarray(inputs["L0_attn_Wk"])
    wv = np.asarray(inputs["L0_attn_Wv"])
    wo = np.asarray(inputs["L0_attn_Wo"])
    gq = np.asarray(inputs["L0_ffn_gate_qweight"])
    gsc = np.asarray(inputs["L0_ffn_gate_scales"])
    gz = np.asarray(inputs["L0_ffn_gate_qzeros"])
    uq = np.asarray(inputs["L0_ffn_up_qweight"])
    usc = np.asarray(inputs["L0_ffn_up_scales"])
    uz = np.asarray(inputs["L0_ffn_up_qzeros"])
    dq = np.asarray(inputs["L0_ffn_down_qweight"])
    dsc = np.asarray(inputs["L0_ffn_down_scales"])
    dz = np.asarray(inputs["L0_ffn_down_qzeros"])

    glob = _prep_inputs._cache.get(id(inputs))
    if glob is None:
        wq_il = np.ascontiguousarray(wq[IL_PERM])
        wk_il = np.ascontiguousarray(wk[IL_PERM])
        wv_il = np.ascontiguousarray(wv[IL_PERM])
        gq_r = gq[IL_PERM].reshape(NKT, 128, 1376).transpose(1, 0, 2)
        gq_t = np.ascontiguousarray(
            gq_r.reshape(128, NKT, NFT, 16).transpose(2, 0, 1, 3))
        uq_r = uq[IL_PERM].reshape(NKT, 128, 1376).transpose(1, 0, 2)
        uq_t = np.ascontiguousarray(
            uq_r.reshape(128, NKT, NFT, 16).transpose(2, 0, 1, 3))
        dq_t = np.ascontiguousarray(
            dq.reshape(NFT, 128, 512).reshape(NFT, 128, NKT, 16).transpose(2, 1, 0, 3))
        gsr = np.ascontiguousarray(np.repeat(gsc, 4, axis=0))
        usr = np.ascontiguousarray(np.repeat(usc, 4, axis=0))
        dsr = np.ascontiguousarray(
            dsc.reshape(NFT, NKT, 128).transpose(1, 0, 2).reshape(NKT, NFF))
        glob = dict(wq_il=wq_il, wk_il=wk_il, wv_il=wv_il, wo=np.ascontiguousarray(wo),
                    gq_t=gq_t, uq_t=uq_t, dq_t=dq_t, gsr=gsr, usr=usr, dsr=dsr,
                    gz=np.ascontiguousarray(gz), uz=np.ascontiguousarray(uz),
                    dz=np.ascontiguousarray(dz), gsc=np.ascontiguousarray(gsc),
                    usc=np.ascontiguousarray(usc), dsc=np.ascontiguousarray(dsc))
        _prep_inputs._cache[id(inputs)] = glob

    x_main = np.ascontiguousarray(x[b, hf * M:(hf + 1) * M])
    x_pre = (np.ascontiguousarray(x[b, 0:M]) if hf == 1
             else np.zeros((M, HID), np.float32))
    start = hf * M
    d = dict(glob)
    d.update(
        x_main=x_main, x_pre=x_pre,
        identc=HC["ident"], Uc=HC["U"], maskPTc=HC["maskPT"], G32c=HC["G32"],
        ones1c=HC["ones1"], onesrc=HC["onesr"], maskTn=HC["maskT_norm"],
        maskTf=(HC["maskT_norm"] if hf == 1 else HC["maskT_first0"]),
        cosq=HC["cosq"][start:start + M], sinq=HC["sinq"][start:start + M],
        cosk=HC["cos"][start:start + M], sink=HC["sin"][start:start + M],
        coskh=HC["cos"][M - 64:M], sinkh=HC["sin"][M - 64:M],
    )
    return d


_prep_inputs._cache = {}


def kernel(**inputs):
    if "nc" not in _NC_CACHE:
        _NC_CACHE["nc"] = build_program("full")
    nc = _NC_CACHE["nc"]
    maps = []
    for core in range(8):
        b, hf = core // 2, core % 2
        maps.append(_prep_inputs(inputs, b, hf))
    _prep_inputs._cache.clear()
    res = run_bass_kernel_spmd(nc, maps, core_ids=list(range(8)))
    out = np.empty((B, N, HID), np.float32)
    for core in range(8):
        b, hf = core // 2, core % 2
        out[b, hf * M:(hf + 1) * M] = res.results[core]["out"]
    return out


# version 06
# speedup vs baseline: 2.0930x; 1.2230x over previous
import numpy as np
import ml_dtypes
from contextlib import ExitStack
import concourse.bass as bass
import concourse.mybir as mybir
from concourse.bass_utils import run_bass_kernel_spmd

F32 = mybir.dt.float32
BF16 = mybir.dt.bfloat16
I32 = mybir.dt.int32
I16 = mybir.dt.int16
ALU = mybir.AluOpType
AF = mybir.ActivationFunctionType
AX = mybir.AxisListType

B, N, HID = 4, 2048, 4096
H, KVH, D = 32, 8, 128
M = 1024
NTT = 8
NFF = 11008
NFT = 86
NKT = 32
GN = 16.0
THETA = 10000.0
EPS = 1e-5
NEG = -30000.0

ENGS = ("sync", "tensor", "vector", "scalar", "gpsimd")


def _il_perm():
    perm = np.empty(HID, dtype=np.int64)
    for j in range(NKT):
        for pp in range(128):
            perm[j * 128 + pp] = (pp // 4) * 128 + (pp % 4) * 32 + j
    return perm


IL_PERM = _il_perm()


def host_consts():
    ident = np.eye(128, dtype=np.float32)
    U = (np.triu(np.ones((128, 128))) * (-1.0 / GN)).astype(np.float32)
    maskPT = np.triu(np.ones((128, 128))).astype(np.float32)
    G32 = np.zeros((128, 32), np.float32)
    for pp in range(128):
        G32[pp, pp // 4] = 1.0
    ones1 = np.ones((128, 1), np.float32)
    onesr = np.ones((1, 128), np.float32)
    s_ = np.arange(128)[:, None]
    t_ = np.arange(64)[None, :]
    mvalid = (s_ >= t_) & (s_ <= t_ + 64)
    mT = np.where(mvalid, 0.0, NEG).astype(np.float32)
    maskT_norm = np.tile(mT, (1, 8))
    mA0 = np.where(mvalid & (s_ >= 64), 0.0, NEG).astype(np.float32)
    maskT_first0 = np.tile(np.concatenate([mA0, mT], axis=1), (1, 4))
    half = np.arange(D // 2, dtype=np.float64)
    inv = 1.0 / THETA ** (2.0 * half / D)
    pos = np.arange(N, dtype=np.float64)
    fr = pos[:, None] * inv[None, :]
    cos = np.cos(fr)
    sin = np.sin(fr)
    sc = 1.0 / np.sqrt(np.float64(D))
    return dict(ident=ident, U=U, maskPT=maskPT, G32=G32, ones1=ones1, onesr=onesr,
                maskT_norm=maskT_norm, maskT_first0=maskT_first0,
                cos=cos.astype(np.float32), sin=sin.astype(np.float32),
                cosq=(cos * sc).astype(np.float32), sinq=(sin * sc).astype(np.float32))


HC = host_consts()


class Sem:
    def __init__(self, handle):
        self.h = handle
        self.v = 0

    def inc(self, n=1):
        self.v += n
        return self.v


DATA_OPS = {"tensor_tensor", "tensor_scalar", "tensor_copy", "tensor_reduce",
            "reciprocal", "memset", "activation", "matmul", "transpose",
            "ldweights", "iota", "bn_stats", "bn_aggr"}
CHAINED = ("tensor", "vector", "scalar")


class RecIns:
    def __init__(self, auto=None):
        self.posts = []
        if auto is not None:
            self.posts.append(auto)
        self.auto = auto

    def then_inc(self, sem, n=1):
        assert self.auto is None, "explicit then_inc on chained compute op"
        self.posts.append((sem, n))
        return self


class Recorder:
    def __init__(self, p, eng):
        self.p = p
        self.eng = eng

    def wait_ge(self, sem, val):
        if val:
            self.p.q[self.eng].append(("w", sem, val))

    def nop(self):
        r = RecIns()
        self.p.q[self.eng].append(("nop", r))
        return r

    def dma_start(self, **kw):
        r = RecIns()
        self.p.q[self.eng].append(("dma", kw, r))
        return r

    def __getattr__(self, name):
        def g(*a, **k):
            ch = self.p.chain.get(self.eng) if name in DATA_OPS else None
            pre = ch.v if (ch is not None and ch.v and self.eng != "tensor") else None
            r = RecIns(auto=(ch.h, 1) if ch is not None else None)
            self.p.q[self.eng].append(("op", name, a, k, r, pre, ch))
            if ch is not None:
                ch.v += 1
            return r
        return g


class Prog:
    def __init__(self):
        self.q = {e: [] for e in ENGS}
        self.chain = {}

    def on(self, eng):
        rec = Recorder(self, eng)

        def deco(fn):
            fn(rec)
            return None
        return deco

    def replay(self, blk):
        for e in ENGS:
            recs = self.q[e]

            def section(h, recs=recs):
                for r in recs:
                    kind = r[0]
                    if kind == "w":
                        h.wait_ge(r[1], r[2])
                    elif kind == "nop":
                        ins = h.nop()
                        for sem, n in r[1].posts:
                            ins.then_inc(sem, n)
                    elif kind == "dma":
                        ins = h.dma_start(**r[1])
                        for sem, n in r[2].posts:
                            ins.then_inc(sem, n)
                    else:
                        _, name, a, k, ri, pre, ch = r
                        if pre is not None:
                            h.wait_ge(ch.h, pre)
                        ins = getattr(h, name)(*a, **k)
                        for sem, n in ri.posts:
                            ins.then_inc(sem, n)
            getattr(blk, e)(section)


class Ctx(dict):
    __getattr__ = dict.__getitem__
    __setattr__ = dict.__setitem__


def fence(c, waits=()):
    snap = {e: c.p.chain[e].v for e in CHAINED}
    for e in ENGS:
        @c.p.on(e)
        def _(h, e=e):
            for e2 in CHAINED:
                if e2 != e and snap[e2]:
                    h.wait_ge(c.p.chain[e2].h, snap[e2])
            for sem, val in waits:
                if val:
                    h.wait_ge(sem, val)


def build_program(mode="full"):
    nc = bass.Bass("TRN2", target_bir_lowering=False)
    T = {}

    def dram(name, shape, dty, kind="ExternalInput"):
        T[name] = nc.dram_tensor(name, shape, dty, kind=kind).ap()
        return T[name]

    dram("x_main", [M, HID], F32)
    dram("x_pre", [M, HID], F32)
    dram("wq_il", [HID, H * D], F32)
    dram("wk_il", [HID, KVH * D], F32)
    dram("wv_il", [HID, KVH * D], F32)
    dram("wo", [H * D, HID], F32)
    dram("gq_t", [NFT, 128, NKT, 16], I32)
    dram("uq_t", [NFT, 128, NKT, 16], I32)
    dram("dq_t", [NKT, 128, NFT, 16], I32)
    dram("gsr", [128, NFF], F32)
    dram("usr", [128, NFF], F32)
    dram("dsr", [NKT, NFF], F32)
    dram("gz", [32, NFF // 8], I32)
    dram("uz", [32, NFF // 8], I32)
    dram("dz", [NFT, 512], I32)
    dram("gsc", [32, NFF], F32)
    dram("usc", [32, NFF], F32)
    dram("dsc", [NFT, HID], F32)
    dram("identc", [128, 128], F32)
    dram("Uc", [128, 128], F32)
    dram("maskPTc", [128, 128], F32)
    dram("G32c", [128, 32], F32)
    dram("ones1c", [128, 1], F32)
    dram("onesrc", [1, 128], F32)
    dram("maskTn", [128, 512], F32)
    dram("maskTf", [128, 512], F32)
    dram("cosq", [M, 64], F32)
    dram("sinq", [M, 64], F32)
    dram("cosk", [M, 64], F32)
    dram("sink", [M, 64], F32)
    dram("coskh", [64, 64], F32)
    dram("sinkh", [64, 64], F32)
    dram("out", [M, HID], F32, kind="ExternalOutput")
    if mode == "ffn":
        dram("x2in", [M, HID], F32)
    dram("x1_sp", [M, HID], F32, kind="Internal")
    dram("h_sp", [NFT, 128, M], BF16, kind="Internal")
    dram("wd_sp", [NKT, 128, NFF], BF16, kind="Internal")
    dram("zs_sp", [128, 2 * NFF + HID], BF16, kind="Internal")
    dram("qs_sp", [NTT, 128, H * D], BF16, kind="Internal")
    dram("sq_sp", [NTT, 128, H * D], BF16, kind="Internal")
    dram("ks_sp", [16, 128, KVH * D], BF16, kind="Internal")
    dram("sk_sp", [NTT, 128, KVH * D], BF16, kind="Internal")
    dram("skh_sp", [64, KVH * D], BF16, kind="Internal")
    dram("gs_sp", [16, 128, KVH * D], F32, kind="Internal")
    dram("v_sp", [16, 128, KVH * D], BF16, kind="Internal")
    dram("dn_sp", [2, NKT, 128, 512], BF16, kind="Internal")
    dram("wo_sp", [2, NKT, 128, 512], BF16, kind="Internal")
    dram("H_sp", [NFT, M], BF16, kind="Internal")

    p = Prog()
    c = Ctx(nc=nc, T=T, p=p, mode=mode)
    es = ExitStack()
    with es:
        def sem(name):
            return Sem(es.enter_context(nc.semaphore(name)))
        c.sem = sem
        p.chain = {"tensor": sem("ch_t"), "vector": sem("ch_v"), "scalar": sem("ch_s")}
        c.cht = p.chain["tensor"]
        c.chv = p.chain["vector"]
        c.chs = p.chain["scalar"]
        c.bank = [es.enter_context(nc.psum_tensor(f"bank{i}", [128, 512], F32)) for i in range(8)]

        def mksb(stack):
            def sb(name, shape, dty):
                return stack.enter_context(nc.sbuf_tensor(name, shape, dty))
            return sb

        sb0 = mksb(es)
        c.ident = sb0("ident", [128, 128], BF16)
        c.identf = sb0("identf", [128, 128], F32)
        c.Uc = sb0("Uc_sb", [128, 128], F32)
        c.maskPT = sb0("maskPT", [128, 128], BF16)
        c.G32 = sb0("G32", [128, 32], BF16)
        c.ones1 = sb0("ones1", [128, 1], BF16)
        c.onesr = sb0("onesr", [1, 128], F32)
        c.maskTn = sb0("maskTn_sb", [128, 512], F32)
        c.maskTf = sb0("maskTf_sb", [128, 512], F32)
        emit_consts(c)

        with ExitStack() as s_p0:
            c.sb = mksb(s_p0)
            emit_zeros_prep(c)
        fence(c, waits=[c.zs_done])

        stk_xT1 = ExitStack()
        if mode == "ffn":
            with ExitStack() as s_x2:
                c.sb = mksb(s_x2)
                c.xT_ln1 = stk_xT1.enter_context(nc.sbuf_tensor("xT_ln1", [128, NKT, M], BF16))
                emit_x2_load(c)
            fence(c, waits=[c.x1_done])
        else:
            with ExitStack() as s_pa:
                c.sb = mksb(s_pa)
                emit_proj(c)
            fence(c, waits=[c.pa_sp, c.pa_spr, c.pa_spg])
            stk_oT = ExitStack()
            c.oT = stk_oT.enter_context(nc.sbuf_tensor("oT", [128, H, M], BF16))
            with ExitStack() as s_pb:
                c.sb = mksb(s_pb)
                emit_mixer(c)
            fence(c)
            with ExitStack() as s_pc:
                c.sb = mksb(s_pc)
                emit_wo(c)
            fence(c, waits=[c.wo_sp_done])
            stk_oT.close()
            c.xT_ln1 = stk_xT1.enter_context(nc.sbuf_tensor("xT_ln1", [128, NKT, M], BF16))
            with ExitStack() as s_pc2:
                c.sb = mksb(s_pc2)
                emit_postln(c, "ln1")
            fence(c, waits=[c.x1_done])

        if mode == "attn":
            emit_attn_out(c)
            stk_xT1.close()
        else:
            with ExitStack() as s_p4:
                c.sb = mksb(s_p4)
                emit_gate_up(c)
            fence(c, waits=[c.h_done, c.h_done2, c.H_done])
            stk_xT1.close()
            with ExitStack() as s_p5:
                c.sb = mksb(s_p5)
                emit_down(c)
            fence(c, waits=[c.dn_done])
            with ExitStack() as s_p6:
                c.sb = mksb(s_p6)
                emit_postln(c, "ln2")

        blk = es.enter_context(nc.Block())
        p.replay(blk)
    return nc


def emit_consts(c):
    T, p = c.T, c.p
    s_cg = c.sem("s_cg")
    s_cy = c.sem("s_cy")

    @p.on("gpsimd")
    def _(g):
        g.dma_start(out=c.ident[:], in_=T["identc"]).then_inc(s_cg.h, 16)
        g.dma_start(out=c.maskPT[:], in_=T["maskPTc"]).then_inc(s_cg.h, 16)
        g.dma_start(out=c.G32[:], in_=T["G32c"]).then_inc(s_cg.h, 16)
        g.dma_start(out=c.ones1[:], in_=T["ones1c"]).then_inc(s_cg.h, 16)
    s_cg.inc(64)

    @p.on("sync")
    def _(sy):
        sy.dma_start(out=c.Uc[:], in_=T["Uc"]).then_inc(s_cy.h, 16)
        sy.dma_start(out=c.identf[:], in_=T["identc"]).then_inc(s_cy.h, 16)
        sy.dma_start(out=c.onesr[:], in_=T["onesrc"]).then_inc(s_cy.h, 16)
        sy.dma_start(out=c.maskTn[:], in_=T["maskTn"]).then_inc(s_cy.h, 16)
        sy.dma_start(out=c.maskTf[:], in_=T["maskTf"]).then_inc(s_cy.h, 16)
    s_cy.inc(80)
    c.s_cg = s_cg
    c.s_cy = s_cy


def wait_consts(h, c):
    h.wait_ge(c.s_cg.h, c.s_cg.v)
    h.wait_ge(c.s_cy.h, c.s_cy.v)


def emit_zeros_prep(c):
    T, sb, p = c.T, c.sb, c.p
    chv = c.chv
    s_z = c.sem("s_z")
    s_zs = c.sem("s_zs")
    zq = sb("zq", [NFT, NFF // 8], I32)
    zsc = sb("zsc", [NFT, NFF], F32)
    znib = sb("znib", [NFT, NFF], I16)
    zout = sb("zout", [NFT, NFF], BF16)

    off = 0
    for zn, scn, rows, cols in (("gz", "gsc", 32, NFF), ("uz", "usc", 32, NFF),
                                ("dz", "dsc", NFT, HID)):
        @p.on("sync")
        def _(sy, zn=zn, scn=scn, rows=rows, cols=cols, prev=c.get("z_used", 0),
              prev_sp=c.get("z_spv", 0)):
            sy.wait_ge(chv.h, prev)
            if prev_sp:
                sy.wait_ge(s_zs.h, prev_sp)
            sy.dma_start(out=zq[0:rows, 0:cols // 8], in_=T[zn]).then_inc(s_z.h, 16)
            sy.dma_start(out=zsc[0:rows, 0:cols], in_=T[scn]).then_inc(s_z.h, 16)
        s_z.inc(32)
        tgt = s_z.v

        @p.on("vector")
        def _(v, rows=rows, cols=cols, tgt=tgt):
            v.wait_ge(s_z.h, tgt)
            z16 = zq[0:rows, 0:cols // 8].bitcast(I16).rearrange("r (w h) -> r w h", h=2)
            ov = znib[0:rows, 0:cols].rearrange("r (w e h) -> r w e h", e=4, h=2)
            for jb in range(4):
                v.tensor_scalar(out=ov[:, :, jb, :], in0=z16, scalar1=4 * jb, scalar2=15,
                                op0=ALU.logical_shift_right, op1=ALU.bitwise_and)
            v.tensor_scalar(out=zsc[0:rows, 0:cols], in0=zsc[0:rows, 0:cols],
                            scalar1=-1.0, scalar2=0.0, op0=ALU.mult, op1=ALU.add)
            v.tensor_tensor(out=zout[0:rows, 0:cols], in0=znib[0:rows, 0:cols],
                            in1=zsc[0:rows, 0:cols], op=ALU.mult)
        c["z_used"] = chv.v
        tgv = chv.v

        @p.on("sync")
        def _(sy, rows=rows, cols=cols, off=off, tgv=tgv):
            sy.wait_ge(chv.h, tgv)
            sy.dma_start(out=T["zs_sp"][0:rows, off:off + cols],
                         in_=zout[0:rows, 0:cols]).then_inc(s_zs.h, 16)
        s_zs.inc(16)
        c["z_spv"] = s_zs.v
        off += cols
    c.zs_done = (s_zs.h, s_zs.v)


def emit_proj(c):
    nc, T, sb, p, bank = c.nc, c.T, c.sb, c.p, c.bank
    cht, chv, chs = c.cht, c.chv, c.chs
    s_xl = [c.sem("s_xl0"), c.sem("s_xl1")]
    s_wl = [c.sem("s_wl0"), c.sem("s_wl1")]
    s_sp = c.sem("s_sp")
    s_spr = c.sem("s_spr")
    s_spg = c.sem("s_spg")
    s_cl = c.sem("s_cl")

    xbuf = [sb(f"xbuf{i}", [128, HID], F32) for i in range(2)]
    xT = sb("xT", [128, NKT, M], BF16)
    wsl = [sb(f"wsl{i}", [128, NKT, 512], BF16) for i in range(2)]
    expb = sb("expb", [128, 512], F32)
    sums = sb("sums", [128, 4], F32)
    recs = sb("recs", [128, 4], F32)
    pstage = sb("pstage", [128, 512], BF16)
    rstage = sb("rstage", [128, 512], BF16)
    gstage = sb("gstage", [128, 512], F32)
    ta = sb("ta", [128, 256], F32)
    tb = sb("tb", [128, 256], F32)
    cosb = sb("cosb", [128, 4, NTT, 64], F32)
    cosh = sb("cosh", [64, 128], F32)
    Bf = dict(expb=expb, sums=sums, recs=recs, pstage=pstage, rstage=rstage,
              gstage=gstage, ta=ta, tb=tb, cosb=cosb, cosh=cosh)
    Sm = dict(s_sp=s_sp, s_spr=s_spr, s_spg=s_spg, s_cl=s_cl)

    @p.on("scalar")
    def _(s):
        for i, nm in enumerate(("cosq", "sinq", "cosk", "sink")):
            s.dma_start(out=cosb[:, i],
                        in_=T[nm].rearrange("(tt p) f -> p tt f", p=128)).then_inc(s_cl.h, 16)
        s.dma_start(out=cosh[:, 0:64], in_=T["coskh"]).then_inc(s_cl.h, 16)
        s.dma_start(out=cosh[:, 64:128], in_=T["sinkh"]).then_inc(s_cl.h, 16)
    s_cl.inc(96)

    for phase in ("pre", "main"):
        xsrc = T["x_pre"] if phase == "pre" else T["x_main"]
        for tt in range(NTT):
            buf = xbuf[tt % 2]
            sxl = s_xl[tt % 2]

            @p.on("sync")
            def _(sy, tt=tt, buf=buf, prev_ev=c.get(f"pa_xtc_{tt - 2}", 0),
                  mmv=c.get("pa_mm_all", 0), sxl=sxl):
                sy.wait_ge(chv.h, prev_ev)
                sy.wait_ge(cht.h, mmv)
                sy.dma_start(out=buf[:], in_=xsrc[tt * 128:(tt + 1) * 128, :]).then_inc(sxl.h, 16)
            sxl.inc(16)
            xl_tgt = sxl.v
            v0 = chv.v

            @p.on("tensor")
            def _(t, buf=buf, xl_tgt=xl_tgt, v0=v0, first=(phase == "pre" and tt == 0), sxl=sxl):
                t.wait_ge(sxl.h, xl_tgt)
                if first:
                    wait_consts(t, c)
                for j in range(NKT):
                    if v0 + j - 1 > 0:
                        t.wait_ge(chv.h, v0 + j - 1)
                    xv = buf[:].rearrange("p (a b e) -> p a b e", a=32, b=4, e=32)[:, :, :, j]
                    pv = bank[j % 2].ap()
                    t.transpose(pv[:, 0:128], xv, c.identf[:])
            pe0 = cht.v - NKT

            @p.on("vector")
            def _(v, tt=tt, pe0=pe0):
                for j in range(NKT):
                    v.wait_ge(cht.h, pe0 + j + 1)
                    pv = bank[j % 2].ap()
                    v.tensor_copy(out=xT[:, j, tt * 128:(tt + 1) * 128], in_=pv[:, 0:128])
            c[f"pa_xtc_{tt}"] = chv.v
        xT_done = chv.v

        if phase == "main":
            groups = ([("q", fg) for fg in range(8)] +
                      [("k", 0), ("k", 1), ("v", 0), ("v", 1)])
        else:
            groups = [("k", 0), ("k", 1), ("v", 0), ("v", 1)]

        for gi, (kind, fg) in enumerate(groups):
            wslb = wsl[gi % 2]
            wsrc = {"q": "wq_il", "k": "wk_il", "v": "wv_il"}[kind]
            col0 = fg * 512

            swl = s_wl[gi % 2]

            @p.on("gpsimd")
            def _(g, wsrc=wsrc, col0=col0, wslb=wslb,
                  prev_mm=c.get(f"pa_mm_slot{gi % 2}", 0), xT_done=xT_done, swl=swl):
                g.wait_ge(cht.h, max(prev_mm, 0))
                g.wait_ge(chv.h, xT_done)
                g.dma_start(out=wslb[:],
                            in_=T[wsrc].rearrange("(j p) f -> p j f", p=128)[:, :, col0:col0 + 512]
                            ).then_inc(swl.h, 16)
            swl.inc(16)
            wl_tgt = swl.v

            for tt in range(NTT):
                bnk = bank[tt]
                ev_prev = c.get(f"pa_ev_{tt}", (0, 0))

                @p.on("tensor")
                def _(t, wslb=wslb, wl_tgt=wl_tgt, bnk=bnk, tt=tt,
                      ev_prev=ev_prev, xT_done=xT_done, swl=swl):
                    t.wait_ge(swl.h, wl_tgt)
                    t.wait_ge(chv.h, max(xT_done, ev_prev[0]))
                    if ev_prev[1]:
                        t.wait_ge(chs.h, ev_prev[1])
                    for j in range(NKT):
                        t.matmul(bnk.ap(), xT[:, j, tt * 128:(tt + 1) * 128],
                                 wslb[:, j, :], start=(j == 0), stop=(j == NKT - 1))
                mm_tgt = cht.v
                emit_proj_evac(c, phase, kind, fg, tt, bnk, mm_tgt, Bf, Sm)
                c[f"pa_ev_{tt}"] = (chv.v, chs.v)
            c[f"pa_mm_slot{gi % 2}"] = cht.v
        c["pa_mm_all"] = cht.v
    c.pa_sp = (s_sp.h, s_sp.v)
    c.pa_spr = (s_spr.h, s_spr.v)
    c.pa_spg = (s_spg.h, s_spg.v)


def emit_proj_evac(c, phase, kind, fg, tt, bnk, mm_tgt, Bf, Sm):
    T, p = c.T, c.p
    cht, chv, chs = c.cht, c.chv, c.chs
    s_sp, s_spr, s_spg, s_cl = Sm["s_sp"], Sm["s_spr"], Sm["s_spg"], Sm["s_cl"]
    expb, sums, recs = Bf["expb"], Bf["sums"], Bf["recs"]
    pstage, rstage, gstage = Bf["pstage"], Bf["rstage"], Bf["gstage"]
    ta, tb, cosb, cosh = Bf["ta"], Bf["tb"], Bf["cosb"], Bf["cosh"]
    main = (phase == "main")
    it = (NTT if main else 0) + tt

    if kind == "v":
        @p.on("scalar")
        def _(s, bnk=bnk, mm_tgt=mm_tgt, sp0=s_sp.v, vprev=chv.v):
            s.wait_ge(cht.h, mm_tgt)
            s.wait_ge(chv.h, vprev)
            if sp0:
                s.wait_ge(s_sp.h, sp0)
            s.activation(out=pstage[:], in_=bnk.ap(), func=AF.Identity)
        es_tgt = chs.v

        @p.on("sync")
        def _(sy, it=it, fg=fg, es_tgt=es_tgt):
            sy.wait_ge(chs.h, es_tgt)
            sy.dma_start(out=T["v_sp"][it, :, fg * 512:(fg + 1) * 512],
                         in_=pstage[:]).then_inc(s_sp.h, 16)
        s_sp.inc(16)
        return

    rope_this = main or (phase == "pre" and tt == 7 and kind == "k")
    cl_tgt = s_cl.v

    @p.on("scalar")
    def _(s, bnk=bnk, mm_tgt=mm_tgt, kind=kind, spg_pre=s_spg.v, vprev=chv.v):
        s.wait_ge(cht.h, mm_tgt)
        s.wait_ge(chv.h, vprev)
        if kind == "k" and spg_pre:
            s.wait_ge(s_spg.h, spg_pre)
        s.activation(out=expb[:], in_=bnk.ap(), func=AF.Exp)
        if kind == "k":
            s.activation(out=gstage[:], in_=bnk.ap(), func=AF.Exp, scale=-1.0)
            s.activation(out=gstage[:], in_=gstage[:], func=AF.Ln, bias=1.0)
    es_exp = chs.v

    @p.on("vector")
    def _(v, bnk=bnk, kind=kind, es_exp=es_exp, sp_pre=s_sp.v, spr_pre=s_spr.v,
          rope_this=rope_this, cl_tgt=cl_tgt, main=main):
        v.wait_ge(chs.h, es_exp)
        if sp_pre:
            v.wait_ge(s_sp.h, sp_pre)
        if rope_this and spr_pre:
            v.wait_ge(s_spr.h, spr_pre)
        v.tensor_reduce(out=sums[:].unsqueeze(2),
                        in_=expb[:].rearrange("p (h n) -> p h n", h=4),
                        axis=AX.X, op=ALU.add)
        if kind == "q":
            v.tensor_scalar(out=sums[:], in0=sums[:], scalar1=2.0, scalar2=0.0,
                            op0=ALU.mult, op1=ALU.add)
        v.reciprocal(out=recs[:], in_=sums[:])
        for hh in range(4):
            v.tensor_scalar(out=pstage[:, hh * 128:(hh + 1) * 128],
                            in0=expb[:, hh * 128:(hh + 1) * 128],
                            scalar1=recs[:, hh:hh + 1], scalar2=0.0,
                            op0=ALU.mult, op1=ALU.add)
        if rope_this:
            v.wait_ge(s_cl.h, cl_tgt)
            rows = slice(0, 128) if main else slice(64, 128)
            if main:
                ci = 0 if kind == "q" else 2
                cv = cosb[:, ci, tt, :]
                sv = cosb[:, ci + 1, tt, :]
            else:
                cv = cosh[:, 0:64]
                sv = cosh[:, 64:128]
            cv = cv[rows, :] if main else cv
            sv = sv[rows, :] if main else sv
            pv = bnk.ap()
            nr = 128 if main else 64
            pv3 = pv[rows, :].rearrange("p (h e n) -> p h e n", h=4, e=2)
            rs3 = rstage[rows, :].rearrange("p (h e n) -> p h e n", h=4, e=2)
            ta3 = ta[rows, :].rearrange("p (h n) -> p h n", h=4)
            tb3 = tb[rows, :].rearrange("p (h n) -> p h n", h=4)
            cvb = cv.unsqueeze(1).broadcast_to([nr, 4, 64])
            svb = sv.unsqueeze(1).broadcast_to([nr, 4, 64])
            v.tensor_tensor(out=ta3, in0=pv3[:, :, 0, :], in1=cvb, op=ALU.mult)
            v.tensor_tensor(out=tb3, in0=pv3[:, :, 1, :], in1=svb, op=ALU.mult)
            v.tensor_tensor(out=rs3[:, :, 0, :], in0=ta3, in1=tb3, op=ALU.subtract)
            v.tensor_tensor(out=ta3, in0=pv3[:, :, 1, :], in1=cvb, op=ALU.mult)
            v.tensor_tensor(out=tb3, in0=pv3[:, :, 0, :], in1=svb, op=ALU.mult)
            v.tensor_tensor(out=rs3[:, :, 1, :], in0=ta3, in1=tb3, op=ALU.add)
    ev_done = chv.v

    @p.on("sync")
    def _(sy, kind=kind, fg=fg, it=it, tt=tt, ev_done=ev_done, rope_this=rope_this,
          main=main):
        sy.wait_ge(chv.h, ev_done)
        cols = slice(fg * 512, (fg + 1) * 512)
        if kind == "q":
            sy.dma_start(out=T["qs_sp"][tt, :, cols], in_=pstage[:]).then_inc(s_sp.h, 16)
            sy.dma_start(out=T["sq_sp"][tt, :, cols], in_=rstage[:]).then_inc(s_spr.h, 16)
        else:
            sy.dma_start(out=T["ks_sp"][it, :, cols], in_=pstage[:]).then_inc(s_sp.h, 16)
            sy.dma_start(out=T["gs_sp"][it, :, cols], in_=gstage[:]).then_inc(s_spg.h, 16)
            if main:
                sy.dma_start(out=T["sk_sp"][tt, :, cols], in_=rstage[:]).then_inc(s_spr.h, 16)
            elif rope_this:
                sy.dma_start(out=T["skh_sp"][:, cols], in_=rstage[64:128, :]).then_inc(s_spr.h, 16)
    s_sp.inc(16)
    if kind == "q" or (kind == "k" and (main or rope_this)):
        s_spr.inc(16)
    if kind == "k":
        s_spg.inc(16)


def emit_mixer(c):
    nc, T, sb, p, bank = c.nc, c.T, c.sb, c.p, c.bank
    cht, chv, chs = c.cht, c.chv, c.chs
    s_l = c.sem("s_bl")

    S = sb("S", [128, KVH * 128], F32)
    Sb = sb("Sb", [128, KVH * 128], BF16)
    skT = sb("skT", [128, KVH, 64 + M], BF16)
    oT = c.oT
    qs = sb("qs", [128, H * D], BF16)
    sq = sb("sq", [128, H * D], BF16)
    ks = sb("ks", [128, KVH * D], BF16)
    sk = sb("sk", [128, KVH * D], BF16)
    gs = sb("gs", [128, KVH * D], F32)
    vt = [sb(f"vt{i}", [128, KVH * D], BF16) for i in range(2)]
    qsT = sb("qsT", [128, H, 128], BF16)
    sqT = sb("sqT", [128, H, 128], BF16)
    koT = sb("koT", [128, 128], BF16)
    ko = sb("ko", [128, 128], BF16)
    eGT = sb("eGT", [128, 128], F32)
    enG = sb("enG", [128, 128], F32)
    qtT = sb("qtT", [128, 4, 128], BF16)
    PTg = sb("PTg", [128, 512], BF16)
    sws = sb("sws", [128, 512], F32)
    swe = sb("swe", [128, 512], BF16)
    swz = sb("swz", [1, 512], F32)
    PTs = sb("PTs", [128, 512], BF16)
    vwin = sb("vwin", [128, KVH * D], BF16)
    s_vw = c.sem("s_vw")
    c._s_vw = s_vw
    c._vwin = vwin

    @p.on("vector")
    def _(v):
        v.memset(S[:], 0.0)
        v.memset(Sb[:], 0.0)

    Bf = dict(S=S, Sb=Sb, skT=skT, oT=oT, qs=qs, sq=sq, ks=ks, sk=sk, gs=gs,
              qsT=qsT, sqT=sqT, koT=koT, ko=ko, eGT=eGT, enG=enG, qtT=qtT,
              PTg=PTg, sws=sws, swe=swe, swz=swz, PTs=PTs)

    for it in range(16):
        main = it >= NTT
        tt = it - NTT if main else it
        vtc = vt[it % 2]

        @p.on("sync")
        def _(sy, it=it, tt=tt, main=main, vtc=vtc, done=c.get("mix_done", 0)):
            sy.wait_ge(chv.h, done)
            sy.wait_ge(cht.h, c.get("mix_done_t", 0))
            sy.dma_start(out=ks[:], in_=T["ks_sp"][it]).then_inc(s_l.h, 16)
            sy.dma_start(out=gs[:], in_=T["gs_sp"][it]).then_inc(s_l.h, 16)
            sy.dma_start(out=vtc[:], in_=T["v_sp"][it]).then_inc(s_l.h, 16)
            if main:
                sy.dma_start(out=qs[:], in_=T["qs_sp"][tt]).then_inc(s_l.h, 16)
                sy.dma_start(out=sq[:], in_=T["sq_sp"][tt]).then_inc(s_l.h, 16)
                sy.dma_start(out=sk[:], in_=T["sk_sp"][tt]).then_inc(s_l.h, 16)
        s_l.inc(48 + (48 if main else 0))
        ld_tgt = s_l.v

        if main:
            vwin = c._vwin
            s_vw = c._s_vw

            @p.on("sync")
            def _(sy, ld_tgt=ld_tgt, vtc=vtc, vtp=vt[(it + 1) % 2]):
                sy.wait_ge(s_l.h, ld_tgt)
                sy.dma_start(out=vwin[0:64, :], in_=vtp[64:128, :]).then_inc(s_vw.h, 16)
                sy.dma_start(out=vwin[64:128, :], in_=vtc[0:64, :]).then_inc(s_vw.h, 16)
            s_vw.inc(32)
            c["vw_tgt"] = s_vw.v

            for grp, (srcb, dstb, nseg) in enumerate(((qs, qsT, H), (sq, sqT, H), (sk, None, KVH))):
                v0 = chv.v

                @p.on("tensor")
                def _(t, srcb=srcb, nseg=nseg, ld_tgt=ld_tgt, v0=v0, grp=grp):
                    if grp == 0:
                        t.wait_ge(s_l.h, ld_tgt)
                    for sg in range(nseg):
                        if v0 + sg - 1 > 0:
                            t.wait_ge(chv.h, v0 + sg - 1)
                        pv = bank[4 + sg % 2].ap().bitcast(BF16)
                        t.transpose(pv[:, 0:128], srcb[:, sg * 128:(sg + 1) * 128],
                                    c.ident[:])
                pe0 = cht.v - nseg

                @p.on("vector")
                def _(v, dstb=dstb, nseg=nseg, pe0=pe0, tt=tt, grp=grp):
                    for sg in range(nseg):
                        v.wait_ge(cht.h, pe0 + sg + 1)
                        pv = bank[4 + sg % 2].ap().bitcast(BF16)
                        if grp < 2:
                            v.tensor_copy(out=dstb[:, sg, :], in_=pv[:, 0:128])
                        else:
                            v.tensor_copy(out=skT[:, sg, 64 + tt * 128:64 + (tt + 1) * 128],
                                          in_=pv[:, 0:128])

        for kv in range(KVH):
            emit_chunk(c, it, tt, main, kv, Bf, vtc, vt[(it + 1) % 2], s_l, ld_tgt)
        c["mix_done"] = chv.v
        c["mix_done_t"] = cht.v


def emit_chunk(c, it, tt, main, kv, Bf, vtc, vtp, s_l, ld_tgt):
    p, bank = c.p, c.bank
    cht, chv, chs = c.cht, c.chv, c.chs
    S, Sb, skT, oT = Bf["S"], Bf["Sb"], Bf["skT"], Bf["oT"]
    qs, sq, ks, sk, gs = Bf["qs"], Bf["sq"], Bf["ks"], Bf["sk"], Bf["gs"]
    qsT, sqT, koT, ko = Bf["qsT"], Bf["sqT"], Bf["koT"], Bf["ko"]
    eGT, enG, qtT = Bf["eGT"], Bf["enG"], Bf["qtT"]
    PTg, sws, swe, swz, PTs = Bf["PTg"], Bf["sws"], Bf["swe"], Bf["swz"], Bf["PTs"]
    ksg = slice(kv * 128, (kv + 1) * 128)
    Sseg = slice(kv * 128, (kv + 1) * 128)

    bG = bank[6]
    bO = bank[2 if kv % 2 == 0 else 3]
    bW = bank[7]
    bD = bank[6]

    @p.on("tensor")
    def _(t, ld_tgt=ld_tgt, kv=kv, vpre=chv.v, spre=chs.v):
        if kv == 0 and not main:
            t.wait_ge(s_l.h, ld_tgt)
        t.wait_ge(chv.h, vpre)
        t.wait_ge(chs.h, spre)
        t.matmul(bG.ap()[:, 0:128], gs[:, ksg], c.Uc[:], start=True, stop=True)
        t.matmul(bG.ap()[:, 128:256], c.Uc[:], gs[:, ksg], start=True, stop=True)
    gmm = cht.v

    @p.on("scalar")
    def _(s, gmm=gmm):
        s.wait_ge(cht.h, gmm)
        s.activation(out=eGT[:], in_=bG.ap()[:, 0:128], func=AF.Exp)
        s.activation(out=enG[:], in_=bG.ap()[:, 128:256], func=AF.Exp, scale=-1.0)
    gex = chs.v

    @p.on("vector")
    def _(v, gex=gex, kv=kv):
        v.wait_ge(chs.h, gex)
        v.tensor_tensor(out=ko[:], in0=ks[:, ksg], in1=enG[:], op=ALU.mult)
    kov = chv.v

    if main:
        @p.on("tensor")
        def _(t, kov=kov):
            t.wait_ge(chv.h, kov)
            pv = bank[5].ap().bitcast(BF16)
            t.transpose(pv[:, 0:128], ko[:], c.ident[:])
        koT_mm = cht.v

        @p.on("vector")
        def _(v, tgt=koT_mm, kv=kv):
            v.wait_ge(cht.h, tgt)
            pv = bank[5].ap().bitcast(BF16)
            v.tensor_copy(out=koT[:], in_=pv[:, 0:128])
            v.tensor_tensor(out=qtT[:], in0=qsT[:, kv * 4:(kv + 1) * 4, :],
                            in1=eGT[:].unsqueeze(1).broadcast_to([128, 4, 128]),
                            op=ALU.mult)
        qt_done = chv.v

        @p.on("tensor")
        def _(t, kv=kv, tt=tt, qt_done=qt_done):
            t.wait_ge(chv.h, qt_done)
            for qi in range(4):
                qh = kv * 4 + qi
                for blk_ in range(2):
                    w0 = tt * 128 + blk_ * 64
                    t.matmul(bW.ap()[:, (qi * 2 + blk_) * 64:(qi * 2 + blk_ + 1) * 64],
                             skT[:, kv, w0:w0 + 128],
                             sqT[:, qh, blk_ * 64:(blk_ + 1) * 64],
                             start=True, stop=True)
        swmm = cht.v

        @p.on("vector")
        def _(v, swmm=swmm, it=it):
            v.wait_ge(cht.h, swmm)
            mask = c.maskTf if it == NTT else c.maskTn
            v.tensor_tensor(out=sws[:], in0=bW.ap(), in1=mask[:], op=ALU.add)
        swm = chv.v

        @p.on("scalar")
        def _(s, swm=swm):
            s.wait_ge(chv.h, swm)
            s.activation(out=swe[:], in_=sws[:], func=AF.Exp)
        swex = chs.v

        @p.on("tensor")
        def _(t, swex=swex):
            t.wait_ge(chs.h, swex)
            t.matmul(bW.ap()[0:1, 0:512], c.ones1[:], swe[:, 0:512],
                     start=True, stop=True)
        zmm = cht.v

        @p.on("vector")
        def _(v, zmm=zmm):
            v.wait_ge(cht.h, zmm)
            v.reciprocal(out=swz[:], in_=bW.ap()[0:1, 0:512])
            v.tensor_scalar(out=swz[:], in0=swz[:], scalar1=0.5, scalar2=0.0,
                            op0=ALU.mult, op1=ALU.add)
        zrec = chv.v

        @p.on("tensor")
        def _(t, zrec=zrec):
            t.wait_ge(chv.h, zrec)
            t.matmul(bW.ap()[:, 0:512], c.onesr[:], swz[:], start=True, stop=True)
        repmm = cht.v

        @p.on("vector")
        def _(v, repmm=repmm):
            v.wait_ge(cht.h, repmm)
            v.tensor_tensor(out=PTs[:], in0=swe[:], in1=bW.ap(), op=ALU.mult)
        pts_done = chv.v

        @p.on("tensor")
        def _(t, kv=kv, qt_done=qt_done):
            t.wait_ge(chv.h, qt_done)
            for qi in range(4):
                t.matmul(bank[5].ap()[:, qi * 128:(qi + 1) * 128],
                         koT[:], qtT[:, qi, :], start=True, stop=True)
        ptgmm = cht.v

        @p.on("vector")
        def _(v, ptgmm=ptgmm):
            v.wait_ge(cht.h, ptgmm)
            v.tensor_tensor(out=PTg[:].rearrange("p (q n) -> p q n", q=4),
                            in0=bank[5].ap().rearrange("p (q n) -> p q n", q=4),
                            in1=c.maskPT[:].unsqueeze(1).broadcast_to([128, 4, 128]),
                            op=ALU.mult)
        ptg_done = chv.v

        @p.on("tensor")
        def _(t, kv=kv, it=it, ptg_done=ptg_done, pts_done=pts_done,
              o_ev=c.get(f"oev_{kv % 2}", 0), vtc=vtc, vtp=vtp):
            t.wait_ge(chv.h, max(ptg_done, pts_done))
            if o_ev:
                t.wait_ge(chs.h, o_ev)
            t.wait_ge(c._s_vw.h, c.get("vw_tgt", 0))
            for qi in range(4):
                osl = bO.ap()[:, qi * 128:(qi + 1) * 128]
                t.matmul(osl, vtc[:, ksg], PTg[:, qi * 128:(qi + 1) * 128],
                         start=True, stop=False)
                t.matmul(osl, Sb[:, Sseg], qtT[:, qi, :], start=False, stop=False)
                pA = PTs[:, (qi * 2) * 64:(qi * 2 + 1) * 64]
                t.matmul(osl[:, 0:64], c._vwin[:, ksg], pA[:], start=False, stop=False)
                pB = PTs[:, (qi * 2 + 1) * 64:(qi * 2 + 2) * 64]
                t.matmul(osl[:, 64:128], vtc[:, ksg], pB[:], start=False, stop=True)
        omm = cht.v

        @p.on("scalar")
        def _(s, kv=kv, tt=tt, omm=omm):
            s.wait_ge(cht.h, omm)
            for qi in range(4):
                qh = kv * 4 + qi
                s.activation(out=oT[:, qh, tt * 128:(tt + 1) * 128],
                             in_=bO.ap()[:, qi * 128:(qi + 1) * 128],
                             func=AF.Identity)
        c[f"oev_{kv % 2}"] = chs.v

    @p.on("tensor")
    def _(t, kov=kov, vpre=chv.v, vtc=vtc):
        t.wait_ge(chv.h, max(kov, vpre))
        t.matmul(bD.ap()[:, 256:384], ko[:], vtc[:, ksg], start=True, stop=True)
    dmm = cht.v

    @p.on("vector")
    def _(v, dmm=dmm, kv=kv):
        v.wait_ge(cht.h, dmm)
        v.tensor_tensor(out=S[:, Sseg], in0=S[:, Sseg], in1=bD.ap()[:, 256:384], op=ALU.add)
        v.tensor_scalar(out=S[:, Sseg], in0=S[:, Sseg], scalar1=eGT[:, 127:128],
                        scalar2=0.0, op0=ALU.mult, op1=ALU.add)
        v.tensor_copy(out=Sb[:, Sseg], in_=S[:, Sseg])

    if it == NTT - 1 and kv == KVH - 1:
        s_halo = c.sem("s_halo")

        @p.on("sync")
        def _(sy, sp=chv.v):
            sy.wait_ge(chv.h, sp)
            sy.dma_start(out=sk[0:64, :], in_=c.T["skh_sp"]).then_inc(s_halo.h, 16)

        v0h = chv.v

        @p.on("tensor")
        def _(t, v0h=v0h):
            t.wait_ge(s_halo.h, 16)
            for kk in range(KVH):
                if v0h + kk - 1 > 0:
                    t.wait_ge(chv.h, v0h + kk - 1)
                pv = bank[4 + kk % 2].ap().bitcast(BF16)
                t.transpose(pv[:, 0:64], sk[0:64, kk * 128:(kk + 1) * 128], c.ident[0:64, 0:64])
        peh = cht.v - KVH

        @p.on("vector")
        def _(v, peh=peh):
            for kk in range(KVH):
                v.wait_ge(cht.h, peh + kk + 1)
                pv = bank[4 + kk % 2].ap().bitcast(BF16)
                v.tensor_copy(out=skT[:, kk, 0:64], in_=pv[:, 0:64])


def emit_wo(c):
    nc, T, sb, p, bank = c.nc, c.T, c.sb, c.p, c.bank
    cht, chv, chs = c.cht, c.chv, c.chs
    oT = c.oT
    s_wl = [c.sem("s_cwl0"), c.sem("s_cwl1")]
    s_sp = c.sem("s_csp")

    wsl = [sb(f"cwsl{i}", [128, NKT, 128], BF16) for i in range(2)]
    wo_st = sb("wo_st", [128, M], BF16)

    for ht in range(NKT):
        wslb = wsl[ht % 2]

        swl = s_wl[ht % 2]

        @p.on("gpsimd")
        def _(g, ht=ht, wslb=wslb, prev_mm=c.get(f"wo_mm_slot{ht % 2}", 0), swl=swl):
            g.wait_ge(cht.h, prev_mm)
            g.dma_start(out=wslb[:],
                        in_=T["wo"].rearrange("(o p) n -> p o n", p=128)[:, :, ht * 128:(ht + 1) * 128]
                        ).then_inc(swl.h, 16)
        swl.inc(16)
        wl_tgt = swl.v

        @p.on("tensor")
        def _(t, ht=ht, wslb=wslb, wl_tgt=wl_tgt, ev_prev=c.get("wo_ev_prev", 0), swl=swl):
            t.wait_ge(swl.h, wl_tgt)
            t.wait_ge(chs.h, ev_prev)
            for half in range(2):
                bnk = bank[half]
                for of in range(NKT):
                    t.matmul(bnk.ap(), wslb[:, of, :],
                             oT[:, of, half * 512:(half + 1) * 512],
                             start=(of == 0), stop=(of == NKT - 1))
        c[f"wo_mm_slot{ht % 2}"] = cht.v
        mm_tgt = cht.v

        @p.on("scalar")
        def _(s, mm_tgt=mm_tgt, spp=s_sp.v):
            s.wait_ge(cht.h, mm_tgt)
            if spp:
                s.wait_ge(s_sp.h, spp)
            s.activation(out=wo_st[:, 0:512], in_=bank[0].ap(), func=AF.Identity)
            s.activation(out=wo_st[:, 512:1024], in_=bank[1].ap(), func=AF.Identity)
        c["wo_ev_prev"] = chs.v
        ev_tgt = chs.v

        @p.on("sync")
        def _(sy, ht=ht, ev_tgt=ev_tgt):
            sy.wait_ge(chs.h, ev_tgt)
            sy.dma_start(out=T["wo_sp"][0, ht], in_=wo_st[:, 0:512]).then_inc(s_sp.h, 16)
            sy.dma_start(out=T["wo_sp"][1, ht], in_=wo_st[:, 512:1024]).then_inc(s_sp.h, 16)
        s_sp.inc(32)
    c.wo_sp_done = (s_sp.h, s_sp.v)


def emit_postln(c, which):
    nc, T, sb, p, bank = c.nc, c.T, c.sb, c.p, c.bank
    cht, chv, chs = c.cht, c.chv, c.chs
    s_l = [c.sem(f"s_{which}l0"), c.sem(f"s_{which}l1")]
    s_x = [c.sem(f"s_{which}x0"), c.sem(f"s_{which}x1")]
    s_o = c.sem(f"s_{which}o")
    src_sp = T["wo_sp"] if which == "ln1" else T["dn_sp"]
    x_src = T["x_main"] if which == "ln1" else T["x1_sp"]

    dsl = [sb(f"{which}dl{i}", [128, NKT, 128], BF16) for i in range(2)]
    xrb = [sb(f"{which}xr{i}", [128, HID], F32) for i in range(2)]
    dtm = sb(f"{which}dtm", [128, HID], BF16)
    lsum = sb(f"{which}sum", [128, HID], F32)
    lsq = sb(f"{which}sq", [128, HID], F32)
    stat = sb(f"{which}stat", [128, 4], F32)
    outb = sb(f"{which}out", [128, HID], F32)
    outbb = sb(f"{which}outb", [128, HID], BF16)

    for tt in range(NTT):
        half = tt // 4
        q = tt % 4
        dbuf = dsl[tt % 2]

        sl = s_l[tt % 2]
        sx = s_x[tt % 2]

        @p.on("sync")
        def _(sy, half=half, q=q, dbuf=dbuf, tr_prev=c.get(f"{which}_tr_slot{tt % 2}", 0), sl=sl):
            sy.wait_ge(cht.h, tr_prev)
            sy.dma_start(out=dbuf[:],
                         in_=src_sp[half, :, :, q * 128:(q + 1) * 128]
                         .rearrange("ht p t -> p ht t")).then_inc(sl.h, 16)
        sl.inc(16)
        l_tgt = sl.v

        @p.on("scalar")
        def _(s, tt=tt, xb=xrb[tt % 2], used=c.get(f"{which}_x_slot{tt % 2}", 0), sx=sx):
            s.wait_ge(chv.h, used)
            s.dma_start(out=xb[:], in_=x_src[tt * 128:(tt + 1) * 128, :]).then_inc(sx.h, 16)
        sx.inc(16)
        x_tgt = sx.v
        v0 = chv.v

        @p.on("tensor")
        def _(t, dbuf=dbuf, l_tgt=l_tgt, v0=v0, sl=sl):
            t.wait_ge(sl.h, l_tgt)
            for j in range(NKT):
                if v0 + j - 1 > 0:
                    t.wait_ge(chv.h, v0 + j - 1)
                pv = bank[j % 2].ap().bitcast(BF16)
                t.transpose(pv[:, 0:128], dbuf[:, j, :], c.ident[:])
        c[f"{which}_tr_slot{tt % 2}"] = cht.v
        pe0 = cht.v - NKT

        @p.on("vector")
        def _(v, pe0=pe0, xb=xrb[tt % 2], x_tgt=x_tgt, sx=sx):
            for j in range(NKT):
                v.wait_ge(cht.h, pe0 + j + 1)
                pv = bank[j % 2].ap().bitcast(BF16)
                v.tensor_copy(out=dtm[:, j * 128:(j + 1) * 128], in_=pv[:, 0:128])
            v.wait_ge(sx.h, x_tgt)
            v.tensor_tensor(out=lsum[:], in0=xb[:], in1=dtm[:], op=ALU.add)
            v.tensor_reduce(out=stat[:, 0:1], in_=lsum[:], axis=AX.X, op=ALU.add)
        c[f"{which}_x_slot{tt % 2}"] = chv.v
        sum_done = chv.v

        @p.on("scalar")
        def _(s, sum_done=sum_done):
            s.wait_ge(chv.h, sum_done)
            s.activation(out=lsq[:], in_=lsum[:], func=AF.Square)
        sq_done = chs.v

        @p.on("vector")
        def _(v, sq_done=sq_done):
            v.wait_ge(chs.h, sq_done)
            v.tensor_reduce(out=stat[:, 1:2], in_=lsq[:], axis=AX.X, op=ALU.add)
            v.tensor_scalar(out=stat[:, 0:2], in0=stat[:, 0:2], scalar1=1.0 / HID,
                            scalar2=0.0, op0=ALU.mult, op1=ALU.add)
            v.tensor_tensor(out=stat[:, 2:3], in0=stat[:, 0:1], in1=stat[:, 0:1], op=ALU.mult)
            v.tensor_tensor(out=stat[:, 2:3], in0=stat[:, 1:2], in1=stat[:, 2:3], op=ALU.subtract)
            v.tensor_scalar(out=stat[:, 2:3], in0=stat[:, 2:3], scalar1=1.0, scalar2=EPS,
                            op0=ALU.mult, op1=ALU.add)
        var_done = chv.v

        @p.on("scalar")
        def _(s, var_done=var_done):
            s.wait_ge(chv.h, var_done)
            s.activation(out=stat[:, 3:4], in_=stat[:, 2:3], func=AF.Ln)
        ln_done = chs.v

        @p.on("vector")
        def _(v, ln_done=ln_done):
            v.wait_ge(chs.h, ln_done)
            v.tensor_scalar(out=stat[:, 3:4], in0=stat[:, 3:4], scalar1=-0.5, scalar2=0.0,
                            op0=ALU.mult, op1=ALU.add)
        lnv_done = chv.v

        @p.on("scalar")
        def _(s, lnv_done=lnv_done):
            s.wait_ge(chv.h, lnv_done)
            s.activation(out=stat[:, 2:3], in_=stat[:, 3:4], func=AF.Exp)
        rstd_done = chs.v

        @p.on("vector")
        def _(v, rstd_done=rstd_done, o_prev=c.get(f"{which}_o_prev", 0), which2=which):
            v.wait_ge(chs.h, rstd_done)
            if o_prev:
                v.wait_ge(s_o.h, o_prev)
            v.tensor_scalar(out=outb[:], in0=lsum[:], scalar1=stat[:, 0:1],
                            scalar2=stat[:, 2:3], op0=ALU.subtract, op1=ALU.mult)
            if which2 == "ln1":
                v.tensor_copy(out=outbb[:], in_=outb[:])
        norm_done = chv.v

        @p.on("sync")
        def _(sy, tt=tt, norm_done=norm_done, which2=which):
            sy.wait_ge(chv.h, norm_done)
            dst = T["x1_sp"] if which2 == "ln1" else T["out"]
            sy.dma_start(out=dst[tt * 128:(tt + 1) * 128, :], in_=outb[:]).then_inc(s_o.h, 16)
        s_o.inc(16)
        c[f"{which}_o_prev"] = s_o.v

        if which == "ln1":
            xT_ln1 = c.xT_ln1
            v1 = chv.v

            @p.on("tensor")
            def _(t, norm_done=norm_done, v1=v1):
                t.wait_ge(chv.h, norm_done)
                for j in range(NKT):
                    if v1 + j - 1 > 0:
                        t.wait_ge(chv.h, v1 + j - 1)
                    xv = outbb[:].rearrange("p (a b e) -> p a b e", a=32, b=4, e=32)[:, :, :, j]
                    pv = bank[2 + j % 2].ap().bitcast(BF16)
                    t.transpose(pv[:, 0:128], xv, c.ident[:])
            c[f"{which}_tr_prev"] = cht.v
            pe1 = cht.v - NKT

            @p.on("vector")
            def _(v, tt=tt, pe1=pe1, xT_ln1=xT_ln1):
                for j in range(NKT):
                    v.wait_ge(cht.h, pe1 + j + 1)
                    pv = bank[2 + j % 2].ap().bitcast(BF16)
                    v.tensor_copy(out=xT_ln1[:, j, tt * 128:(tt + 1) * 128],
                                  in_=pv[:, 0:128])
            c[f"{which}_x_used"] = chv.v
    if which == "ln1":
        c.xT_done = chv.v
        c.x1_done = (s_o.h, s_o.v)
    else:
        @p.on("gpsimd")
        def _(g, tgt=s_o.v):
            g.wait_ge(s_o.h, tgt)


def emit_attn_out(c):
    T, p = c.T, c.p
    s_ao = c.sem("s_ao")

    @p.on("sync")
    def _(sy):
        sy.dma_start(out=T["out"], in_=T["x1_sp"]).then_inc(s_ao.h, 16)
    s_ao.inc(16)

    @p.on("gpsimd")
    def _(g, tgt=s_ao.v):
        g.wait_ge(s_ao.h, tgt)


def emit_x2_load(c):
    nc, T, sb, p, bank = c.nc, c.T, c.sb, c.p, c.bank
    cht, chv, chs = c.cht, c.chv, c.chs
    s_xl = [c.sem("s_fxl0"), c.sem("s_fxl1")]
    s_o = c.sem("s_fo")
    xbuf = [sb(f"fxb{i}", [128, HID], F32) for i in range(2)]
    xbb = sb("fxbb", [128, HID], BF16)
    xT_ln1 = c.xT_ln1

    for tt in range(NTT):
        buf = xbuf[tt % 2]

        sxl = s_xl[tt % 2]

        @p.on("sync")
        def _(sy, tt=tt, buf=buf, prev=c.get(f"fx_used_{tt - 2}", 0), sxl=sxl):
            sy.wait_ge(chv.h, prev)
            sy.dma_start(out=buf[:], in_=T["x2in"][tt * 128:(tt + 1) * 128, :]).then_inc(sxl.h, 16)
            sy.dma_start(out=T["x1_sp"][tt * 128:(tt + 1) * 128, :],
                         in_=T["x2in"][tt * 128:(tt + 1) * 128, :]).then_inc(s_o.h, 16)
        sxl.inc(16)
        s_o.inc(16)
        xl_tgt = sxl.v

        @p.on("vector")
        def _(v, buf=buf, xl_tgt=xl_tgt, prev_tr=c.get(f"fx_tr_{tt - 1}", 0), sxl=sxl):
            v.wait_ge(sxl.h, xl_tgt)
            v.wait_ge(cht.h, prev_tr)
            v.tensor_copy(out=xbb[:], in_=buf[:])
        cv_tgt = chv.v
        v1 = chv.v

        @p.on("tensor")
        def _(t, cv_tgt=cv_tgt, v1=v1, first=(tt == 0)):
            t.wait_ge(chv.h, cv_tgt)
            if first:
                wait_consts(t, c)
            for j in range(NKT):
                if v1 + j - 1 > 0:
                    t.wait_ge(chv.h, v1 + j - 1)
                xv = xbb[:].rearrange("p (a b e) -> p a b e", a=32, b=4, e=32)[:, :, :, j]
                pv = bank[j % 2].ap().bitcast(BF16)
                t.transpose(pv[:, 0:128], xv, c.ident[:])
        c[f"fx_tr_{tt}"] = cht.v
        pe0 = cht.v - NKT

        @p.on("vector")
        def _(v, tt=tt, pe0=pe0):
            for j in range(NKT):
                v.wait_ge(cht.h, pe0 + j + 1)
                pv = bank[j % 2].ap().bitcast(BF16)
                v.tensor_copy(out=xT_ln1[:, j, tt * 128:(tt + 1) * 128],
                              in_=pv[:, 0:128])
        c[f"fx_used_{tt}"] = chv.v
    c.xT_done = chv.v
    c.x1_done = (s_o.h, s_o.v)


def emit_gate_up(c):
    nc, T, sb, p, bank = c.nc, c.T, c.sb, c.p, c.bank
    cht, chv, chs = c.cht, c.chv, c.chs
    xT = c.xT_ln1
    s_pl = [c.sem("s_gpl0"), c.sem("s_gpl1")]
    s_sl = [c.sem("s_gsl0"), c.sem("s_gsl1")]
    s_zl = c.sem("s_gzl")
    s_hs = [c.sem("s_ghs0"), c.sem("s_ghs1")]
    s_Hsp = c.sem("s_gHsp")

    pkb = [sb(f"gpk{i}", [128, 2, NKT, 16], I32) for i in range(2)]
    nib = sb("gnib", [128, NKT * 128], I16)
    wsl = [sb(f"gws{i}", [128, 2, NKT, 128], BF16) for i in range(2)]
    ssl = [sb(f"gss{i}", [128, 2, 128], F32) for i in range(2)]
    negz = sb("gnegz", [32, 2 * NFF], BF16)
    Xs = sb("gXs", [32, M], BF16)
    est = sb("gest", [128, M], F32)
    sgt = sb("gsgt", [128, M], F32)
    hst = [sb(f"ghst{i}", [128, M], BF16) for i in range(2)]
    hrow = sb("ghrow", [1, M], BF16)

    @p.on("gpsimd")
    def _(g):
        g.dma_start(out=negz[:, 0:NFF], in_=T["zs_sp"][0:32, 0:NFF]).then_inc(s_zl.h, 16)
        g.dma_start(out=negz[:, NFF:2 * NFF], in_=T["zs_sp"][0:32, NFF:2 * NFF]).then_inc(s_zl.h, 16)
    s_zl.inc(32)

    @p.on("tensor")
    def _(t, xT_done=c.xT_done, zl=s_zl.v):
        t.wait_ge(chv.h, xT_done)
        for half in range(2):
            bnk = bank[6] if half == 0 else bank[7]
            for j in range(NKT):
                t.matmul(bnk.ap()[0:32, 0:512], c.G32[:],
                         xT[:, j, half * 512:(half + 1) * 512],
                         start=(j == 0), stop=(j == NKT - 1))
    Xmm = cht.v

    @p.on("vector")
    def _(v, tgt=Xmm):
        v.wait_ge(cht.h, tgt)
        v.tensor_copy(out=Xs[:, 0:512], in_=bank[6].ap()[0:32, 0:512])
        v.tensor_copy(out=Xs[:, 512:1024], in_=bank[7].ap()[0:32, 0:512])
    X_done = chv.v

    for f in range(NFT):
        pk = pkb[f % 2]
        ws = wsl[f % 2]
        ss = ssl[f % 2]
        dq_prev = c.get(f"g_dq_slot{f % 2}", 0)
        mm_prev = c.get(f"g_mm_slot{f % 2}", 0)

        spl = s_pl[f % 2]
        ssl_s = s_sl[f % 2]

        @p.on("sync")
        def _(sy, f=f, pk=pk, dq_prev=dq_prev, spl=spl):
            sy.wait_ge(chv.h, dq_prev)
            sy.dma_start(out=pk[:, 0], in_=T["gq_t"][f]).then_inc(spl.h, 16)
            sy.dma_start(out=pk[:, 1], in_=T["uq_t"][f]).then_inc(spl.h, 16)
        spl.inc(32)
        pl_tgt = spl.v

        @p.on("gpsimd")
        def _(g, f=f, ss=ss, dq_prev=dq_prev, ssl_s=ssl_s):
            g.wait_ge(chv.h, dq_prev)
            g.dma_start(out=ss[:, 0, :], in_=T["gsr"][:, f * 128:(f + 1) * 128]).then_inc(ssl_s.h, 16)
            g.dma_start(out=ss[:, 1, :], in_=T["usr"][:, f * 128:(f + 1) * 128]).then_inc(ssl_s.h, 16)
        ssl_s.inc(32)
        sl_tgt = ssl_s.v

        @p.on("vector")
        def _(v, f=f, pk=pk, ws=ws, ss=ss, pl_tgt=pl_tgt, sl_tgt=sl_tgt,
              mm_prev=mm_prev, spl=spl, ssl_s=ssl_s):
            v.wait_ge(spl.h, pl_tgt)
            v.wait_ge(ssl_s.h, sl_tgt)
            v.wait_ge(cht.h, mm_prev)
            for mat in range(2):
                z16 = pk[:, mat].bitcast(I16).rearrange("p j (w h) -> p j w h", h=2)
                ov = nib[:].rearrange("p (j w e h) -> p j w e h", j=NKT, e=4, h=2)
                for jb in range(4):
                    v.tensor_scalar(out=ov[:, :, :, jb, :], in0=z16, scalar1=4 * jb,
                                    scalar2=15, op0=ALU.logical_shift_right, op1=ALU.bitwise_and)
                nv = nib[:].rearrange("p (j n) -> p j n", j=NKT)
                v.tensor_tensor(out=ws[:, mat], in0=nv,
                                in1=ss[:, mat, :].unsqueeze(1).broadcast_to([128, NKT, 128]),
                                op=ALU.mult)
        c[f"g_dq_slot{f % 2}"] = chv.v
        dq_tgt = chv.v

        if f % 2 == 0:
            b_g, b_u = [bank[0], bank[1]], [bank[2], bank[3]]
        else:
            b_g, b_u = [bank[4], bank[5]], [bank[6], bank[7]]

        @p.on("tensor")
        def _(t, f=f, ws=ws, dq_tgt=dq_tgt, b_g=b_g, b_u=b_u, X_done=X_done,
              ev_prev2=c.get(f"g_ev_slot{f % 2}", 0), es_prev2=c.get(f"g_es_slot{f % 2}", 0),
              hv_prev=c.get(f"g_hv_slot{f % 2}", 0), zl=s_zl.v):
            t.wait_ge(chv.h, max(dq_tgt, X_done, ev_prev2, hv_prev))
            t.wait_ge(chs.h, es_prev2)
            if f == 0:
                t.wait_ge(s_zl.h, zl)
            for mat, bb in ((0, b_g), (1, b_u)):
                for half in range(2):
                    bnk = bb[half]
                    for j in range(NKT):
                        t.matmul(bnk.ap(), ws[:, mat, j, :],
                                 xT[:, j, half * 512:(half + 1) * 512],
                                 start=(j == 0), stop=False)
                    t.matmul(bnk.ap(), negz[:, mat * NFF + f * 128:mat * NFF + (f + 1) * 128],
                             Xs[:, half * 512:(half + 1) * 512], start=False, stop=True)
        c[f"g_mm_slot{f % 2}"] = cht.v
        mm_tgt = cht.v
        hb = hst[f % 2]

        @p.on("scalar")
        def _(s, mm_tgt=mm_tgt, b_g=b_g):
            s.wait_ge(cht.h, mm_tgt)
            for half in range(2):
                sl = slice(half * 512, (half + 1) * 512)
                s.activation(out=est[:, sl], in_=b_g[half].ap(), func=AF.Exp, scale=-1.0)
                s.activation(out=est[:, sl], in_=est[:, sl], func=AF.Ln, bias=1.0)
                s.activation(out=sgt[:, sl], in_=est[:, sl], func=AF.Exp, scale=-1.0)
        c[f"g_es_slot{f % 2}"] = chs.v
        es_tgt = chs.v

        shs = s_hs[f % 2]

        @p.on("vector")
        def _(v, es_tgt=es_tgt, b_g=b_g, b_u=b_u, hb=hb, hs_prev=c.get(f"g_hs_slot{f % 2}", 0), shs=shs):
            v.wait_ge(chs.h, es_tgt)
            if hs_prev:
                v.wait_ge(shs.h, hs_prev)
            for half in range(2):
                sl = slice(half * 512, (half + 1) * 512)
                v.tensor_tensor(out=sgt[:, sl], in0=b_g[half].ap(), in1=sgt[:, sl], op=ALU.mult)
                v.tensor_tensor(out=hb[:, sl], in0=sgt[:, sl], in1=b_u[half].ap(), op=ALU.mult)
        c[f"g_ev_slot{f % 2}"] = chv.v
        ev_tgt = chv.v

        @p.on("sync")
        def _(sy, f=f, hb=hb, ev_tgt=ev_tgt, shs=shs):
            sy.wait_ge(chv.h, ev_tgt)
            sy.dma_start(out=T["h_sp"][f], in_=hb[:]).then_inc(shs.h, 16)
        shs.inc(16)
        c[f"g_hs_slot{f % 2}"] = shs.v

        @p.on("tensor")
        def _(t, hb=hb, b_g=b_g, ev_tgt=ev_tgt):
            t.wait_ge(chv.h, ev_tgt)
            t.matmul(b_g[0].ap()[0:1, 0:512], c.ones1[:], hb[:, 0:512], start=True, stop=True)
            t.matmul(b_g[1].ap()[0:1, 0:512], c.ones1[:], hb[:, 512:1024], start=True, stop=True)
        Hm = cht.v

        @p.on("vector")
        def _(v, b_g=b_g, tgt=Hm, hsp_prev=s_Hsp.v):
            v.wait_ge(cht.h, tgt)
            if hsp_prev:
                v.wait_ge(s_Hsp.h, hsp_prev)
            v.tensor_copy(out=hrow[0:1, 0:512], in_=b_g[0].ap()[0:1, 0:512])
            v.tensor_copy(out=hrow[0:1, 512:1024], in_=b_g[1].ap()[0:1, 0:512])
        c[f"g_hv_slot{f % 2}"] = chv.v

        @p.on("scalar")
        def _(s, f=f, tgt=chv.v):
            s.wait_ge(chv.h, tgt)
            s.dma_start(out=T["H_sp"][f:f + 1, :], in_=hrow[:]).then_inc(s_Hsp.h, 16)
        s_Hsp.inc(16)
        c[f"g_hsp_slot{f % 2}"] = s_Hsp.v
    c.h_done = (s_hs[0].h, s_hs[0].v)
    c.h_done2 = (s_hs[1].h, s_hs[1].v)
    c.H_done = (s_Hsp.h, s_Hsp.v)


def emit_down(c):
    nc, T, sb, p, bank = c.nc, c.T, c.sb, c.p, c.bank
    cht, chv, chs = c.cht, c.chv, c.chs
    s_hl = c.sem("s_dhl")
    s_pl = [c.sem("s_dpl0"), c.sem("s_dpl1")]
    s_sl = c.sem("s_dsl")
    s_wl = [c.sem("s_dwl0"), c.sem("s_dwl1")]
    s_sp = c.sem("s_dsp")
    s_wsp = [c.sem("s_dwsp0"), c.sem("s_dwsp1")]
    s_zl = c.sem("s_dzl")
    s_H = c.sem("s_dH")

    hb = sb("dh", [128, NFT, 512], BF16)
    pkb = [sb(f"dpk{i}", [128, NFT, 16], I32) for i in range(2)]
    wd = [sb(f"dwd{i}", [128, NFT, 128], BF16) for i in range(2)]
    srd = sb("dsrd", [128, NFF], BF16)
    negzd = sb("dnegz", [NFT, HID], BF16)
    Hs = sb("dHs", [NFT, 512], BF16)
    dstage = sb("dstage", [128, 512], BF16)

    @p.on("gpsimd")
    def _(g):
        g.dma_start(out=negzd[:], in_=T["zs_sp"][0:NFT, 2 * NFF:2 * NFF + HID]).then_inc(s_zl.h, 16)
    s_zl.inc(16)

    for half in range(2):
        @p.on("sync")
        def _(sy, half=half, mm_prev=cht.v):
            sy.wait_ge(cht.h, mm_prev)
            sy.dma_start(out=hb[:],
                         in_=T["h_sp"].rearrange("f p t -> p f t")[:, :, half * 512:(half + 1) * 512]
                         ).then_inc(s_hl.h, 16)
        s_hl.inc(16)
        hl_tgt = s_hl.v

        @p.on("gpsimd")
        def _(g, half=half, H_used=c.get("d_H_used", 0)):
            g.wait_ge(cht.h, H_used)
            g.dma_start(out=Hs[:], in_=T["H_sp"][:, half * 512:(half + 1) * 512]
                        ).then_inc(s_H.h, 16)
        s_H.inc(16)
        H_done = s_H.v

        for ht in range(NKT):
            wdb = wd[ht % 2]
            if half == 0:
                pk = pkb[ht % 2]
                dq_prev = c.get(f"d_dq_slot{ht % 2}", 0)
                mm_slot = c.get(f"d_mm_slot{ht % 2}", 0)
                wsp_slot = c.get(f"d_wsp_slot{ht % 2}", 0)

                spl = s_pl[ht % 2]

                @p.on("sync")
                def _(sy, ht=ht, pk=pk, dq_prev=dq_prev, spl=spl):
                    sy.wait_ge(chv.h, dq_prev)
                    sy.dma_start(out=pk[:], in_=T["dq_t"][ht]).then_inc(spl.h, 16)
                spl.inc(16)
                pl_tgt = spl.v

                @p.on("gpsimd")
                def _(g, ht=ht, dq_all=c.get("d_dq_all", 0), zl=s_zl.v):
                    g.wait_ge(chv.h, dq_all)
                    if ht == 0:
                        g.wait_ge(s_zl.h, zl)
                    g.dma_start(out=srd[:], in_=T["dsr"][ht:ht + 1, :]
                                .broadcast_to([128, NFF])).then_inc(s_sl.h, 16)
                s_sl.inc(16)
                sl_tgt = s_sl.v

                swsp = s_wsp[ht % 2]

                @p.on("vector")
                def _(v, pk=pk, wdb=wdb, pl_tgt=pl_tgt, sl_tgt=sl_tgt,
                      mm_slot=mm_slot, wsp_slot=wsp_slot, spl=spl, swsp=swsp):
                    v.wait_ge(spl.h, pl_tgt)
                    v.wait_ge(s_sl.h, sl_tgt)
                    v.wait_ge(cht.h, mm_slot)
                    if wsp_slot:
                        v.wait_ge(swsp.h, wsp_slot)
                    z16 = pk[:].bitcast(I16).rearrange("p k (w h) -> p k w h", h=2)
                    wdi = wdb[:].rearrange("p k n -> p (k n)").bitcast(I16)
                    ov = wdi.rearrange("p (k w e h) -> p k w e h", k=NFT, e=4, h=2)
                    for jb in range(4):
                        v.tensor_scalar(out=ov[:, :, :, jb, :], in0=z16, scalar1=4 * jb,
                                        scalar2=15, op0=ALU.logical_shift_right, op1=ALU.bitwise_and)
                    v.tensor_tensor(out=wdb[:].rearrange("p k n -> p (k n)"),
                                    in0=wdi, in1=srd[:], op=ALU.mult)
                c[f"d_dq_slot{ht % 2}"] = chv.v
                c["d_dq_all"] = chv.v
                w_sem, w_tgt = None, chv.v

                @p.on("scalar")
                def _(s, ht=ht, wdb=wdb, tgt=chv.v, swsp=swsp):
                    s.wait_ge(chv.h, tgt)
                    s.dma_start(out=T["wd_sp"][ht], in_=wdb[:].rearrange("p k n -> p (k n)")
                                ).then_inc(swsp.h, 16)
                swsp.inc(16)
                c[f"d_wsp_slot{ht % 2}"] = swsp.v
            else:
                mm_slot = c.get(f"d_mm_slot{ht % 2}", 0)

                swl = s_wl[ht % 2]

                @p.on("sync")
                def _(sy, ht=ht, wdb=wdb, mm_slot=mm_slot, swl=swl,
                      wsp_fin=s_wsp[ht % 2].v):
                    sy.wait_ge(cht.h, mm_slot)
                    sy.wait_ge(s_wsp[ht % 2].h, wsp_fin)
                    sy.dma_start(out=wdb[:].rearrange("p k n -> p (k n)"),
                                 in_=T["wd_sp"][ht]).then_inc(swl.h, 16)
                swl.inc(16)
                w_sem, w_tgt = swl, swl.v

            bnk = bank[ht % 2]
            ev_prev = c.get(f"d_ev_slot{ht % 2}", 0)

            @p.on("tensor")
            def _(t, ht=ht, wdb=wdb, bnk=bnk, w_sem=w_sem, w_tgt=w_tgt, H_done=H_done,
                  ev_prev=ev_prev, hl_tgt=hl_tgt):
                t.wait_ge(s_hl.h, hl_tgt)
                if w_sem is None:
                    t.wait_ge(chv.h, w_tgt)
                else:
                    t.wait_ge(w_sem.h, w_tgt)
                t.wait_ge(s_H.h, H_done)
                t.wait_ge(chv.h, ev_prev)
                for kt in range(NFT):
                    t.matmul(bnk.ap(), wdb[:, kt, :], hb[:, kt, :],
                             start=(kt == 0), stop=False)
                t.matmul(bnk.ap(), negzd[:, ht * 128:(ht + 1) * 128], Hs[:],
                         start=False, stop=True)
            c[f"d_mm_slot{ht % 2}"] = cht.v
            mm_tgt = cht.v

            @p.on("vector")
            def _(v, mm_tgt=mm_tgt, bnk=bnk, sp_prev=c.get("d_sp_prev", 0)):
                v.wait_ge(cht.h, mm_tgt)
                if sp_prev:
                    v.wait_ge(s_sp.h, sp_prev)
                v.tensor_copy(out=dstage[:], in_=bnk.ap())
            c[f"d_ev_slot{ht % 2}"] = chv.v

            @p.on("sync")
            def _(sy, half=half, ht=ht, tgt=chv.v):
                sy.wait_ge(chv.h, tgt)
                sy.dma_start(out=T["dn_sp"][half, ht], in_=dstage[:]).then_inc(s_sp.h, 16)
            s_sp.inc(16)
            c["d_sp_prev"] = s_sp.v
        c["d_H_used"] = cht.v
    c.dn_done = (s_sp.h, s_sp.v)


_NC_CACHE = {}


def _prep_inputs(inputs, b, hf):
    x = inputs["L0_attn_hidden_states"]
    wq = np.asarray(inputs["L0_attn_Wq"])
    wk = np.asarray(inputs["L0_attn_Wk"])
    wv = np.asarray(inputs["L0_attn_Wv"])
    wo = np.asarray(inputs["L0_attn_Wo"])
    gq = np.asarray(inputs["L0_ffn_gate_qweight"])
    gsc = np.asarray(inputs["L0_ffn_gate_scales"])
    gz = np.asarray(inputs["L0_ffn_gate_qzeros"])
    uq = np.asarray(inputs["L0_ffn_up_qweight"])
    usc = np.asarray(inputs["L0_ffn_up_scales"])
    uz = np.asarray(inputs["L0_ffn_up_qzeros"])
    dq = np.asarray(inputs["L0_ffn_down_qweight"])
    dsc = np.asarray(inputs["L0_ffn_down_scales"])
    dz = np.asarray(inputs["L0_ffn_down_qzeros"])

    glob = _prep_inputs._cache.get(id(inputs))
    if glob is None:
        wq_il = np.ascontiguousarray(wq[IL_PERM])
        wk_il = np.ascontiguousarray(wk[IL_PERM])
        wv_il = np.ascontiguousarray(wv[IL_PERM])
        gq_r = gq[IL_PERM].reshape(NKT, 128, 1376).transpose(1, 0, 2)
        gq_t = np.ascontiguousarray(
            gq_r.reshape(128, NKT, NFT, 16).transpose(2, 0, 1, 3))
        uq_r = uq[IL_PERM].reshape(NKT, 128, 1376).transpose(1, 0, 2)
        uq_t = np.ascontiguousarray(
            uq_r.reshape(128, NKT, NFT, 16).transpose(2, 0, 1, 3))
        dq_t = np.ascontiguousarray(
            dq.reshape(NFT, 128, 512).reshape(NFT, 128, NKT, 16).transpose(2, 1, 0, 3))
        gsr = np.ascontiguousarray(np.repeat(gsc, 4, axis=0))
        usr = np.ascontiguousarray(np.repeat(usc, 4, axis=0))
        dsr = np.ascontiguousarray(
            dsc.reshape(NFT, NKT, 128).transpose(1, 0, 2).reshape(NKT, NFF))
        glob = dict(wq_il=wq_il, wk_il=wk_il, wv_il=wv_il, wo=np.ascontiguousarray(wo),
                    gq_t=gq_t, uq_t=uq_t, dq_t=dq_t, gsr=gsr, usr=usr, dsr=dsr,
                    gz=np.ascontiguousarray(gz), uz=np.ascontiguousarray(uz),
                    dz=np.ascontiguousarray(dz), gsc=np.ascontiguousarray(gsc),
                    usc=np.ascontiguousarray(usc), dsc=np.ascontiguousarray(dsc))
        _prep_inputs._cache[id(inputs)] = glob

    x_main = np.ascontiguousarray(x[b, hf * M:(hf + 1) * M])
    x_pre = (np.ascontiguousarray(x[b, 0:M]) if hf == 1
             else np.zeros((M, HID), np.float32))
    start = hf * M
    d = dict(glob)
    d.update(
        x_main=x_main, x_pre=x_pre,
        identc=HC["ident"], Uc=HC["U"], maskPTc=HC["maskPT"], G32c=HC["G32"],
        ones1c=HC["ones1"], onesrc=HC["onesr"], maskTn=HC["maskT_norm"],
        maskTf=(HC["maskT_norm"] if hf == 1 else HC["maskT_first0"]),
        cosq=HC["cosq"][start:start + M], sinq=HC["sinq"][start:start + M],
        cosk=HC["cos"][start:start + M], sink=HC["sin"][start:start + M],
        coskh=HC["cos"][M - 64:M], sinkh=HC["sin"][M - 64:M],
    )
    return d


_prep_inputs._cache = {}


def kernel(**inputs):
    if "nc" not in _NC_CACHE:
        _NC_CACHE["nc"] = build_program("full")
    nc = _NC_CACHE["nc"]
    maps = []
    for core in range(8):
        b, hf = core // 2, core % 2
        maps.append(_prep_inputs(inputs, b, hf))
    _prep_inputs._cache.clear()
    res = run_bass_kernel_spmd(nc, maps, core_ids=list(range(8)))
    out = np.empty((B, N, HID), np.float32)
    for core in range(8):
        b, hf = core // 2, core % 2
        out[b, hf * M:(hf + 1) * M] = res.results[core]["out"]
    return out


# version 07
# speedup vs baseline: 2.3023x; 1.1000x over previous
import numpy as np
import ml_dtypes
from contextlib import ExitStack
import concourse.bass as bass
import concourse.mybir as mybir
from concourse.bass_utils import run_bass_kernel_spmd

F32 = mybir.dt.float32
BF16 = mybir.dt.bfloat16
I32 = mybir.dt.int32
I16 = mybir.dt.int16
ALU = mybir.AluOpType
AF = mybir.ActivationFunctionType
AX = mybir.AxisListType

B, N, HID = 4, 2048, 4096
H, KVH, D = 32, 8, 128
M = 1024
NTT = 8
NFF = 11008
NFT = 86
NKT = 32
GN = 16.0
THETA = 10000.0
EPS = 1e-5
NEG = -30000.0

ENGS = ("sync", "tensor", "vector", "scalar", "gpsimd")


def _il_perm():
    perm = np.empty(HID, dtype=np.int64)
    for j in range(NKT):
        for pp in range(128):
            perm[j * 128 + pp] = (pp // 4) * 128 + (pp % 4) * 32 + j
    return perm


IL_PERM = _il_perm()


def host_consts():
    ident = np.eye(128, dtype=np.float32)
    U = (np.triu(np.ones((128, 128))) * (-1.0 / GN)).astype(np.float32)
    maskPT = np.triu(np.ones((128, 128))).astype(np.float32)
    G32 = np.zeros((128, 32), np.float32)
    for pp in range(128):
        G32[pp, pp // 4] = 1.0
    ones1 = np.ones((128, 1), np.float32)
    onesr = np.ones((1, 128), np.float32)
    s_ = np.arange(128)[:, None]
    t_ = np.arange(64)[None, :]
    mvalid = (s_ >= t_) & (s_ <= t_ + 64)
    mT = np.where(mvalid, 0.0, NEG).astype(np.float32)
    maskT_norm = np.tile(mT, (1, 8))
    mA0 = np.where(mvalid & (s_ >= 64), 0.0, NEG).astype(np.float32)
    maskT_first0 = np.tile(np.concatenate([mA0, mT], axis=1), (1, 4))
    half = np.arange(D // 2, dtype=np.float64)
    inv = 1.0 / THETA ** (2.0 * half / D)
    pos = np.arange(N, dtype=np.float64)
    fr = pos[:, None] * inv[None, :]
    cos = np.cos(fr)
    sin = np.sin(fr)
    sc = 1.0 / np.sqrt(np.float64(D))
    return dict(ident=ident, U=U, maskPT=maskPT, G32=G32, ones1=ones1, onesr=onesr,
                maskT_norm=maskT_norm, maskT_first0=maskT_first0,
                cos=cos.astype(np.float32), sin=sin.astype(np.float32),
                cosq=(cos * sc).astype(np.float32), sinq=(sin * sc).astype(np.float32))


HC = host_consts()


class Sem:
    def __init__(self, handle):
        self.h = handle
        self.v = 0

    def inc(self, n=1):
        self.v += n
        return self.v


DATA_OPS = {"tensor_tensor", "tensor_scalar", "tensor_copy", "tensor_reduce",
            "reciprocal", "memset", "activation", "matmul", "transpose",
            "ldweights", "iota", "bn_stats", "bn_aggr"}
CHAINED = ("tensor", "vector", "scalar")


class RecIns:
    def __init__(self, auto=None):
        self.posts = []
        if auto is not None:
            self.posts.append(auto)
        self.auto = auto

    def then_inc(self, sem, n=1):
        assert self.auto is None, "explicit then_inc on chained compute op"
        self.posts.append((sem, n))
        return self


class Recorder:
    def __init__(self, p, eng):
        self.p = p
        self.eng = eng

    def wait_ge(self, sem, val):
        if val:
            self.p.q[self.eng].append(("w", sem, val))

    def nop(self):
        r = RecIns()
        self.p.q[self.eng].append(("nop", r))
        return r

    def dma_start(self, **kw):
        r = RecIns()
        self.p.q[self.eng].append(("dma", kw, r))
        return r

    def __getattr__(self, name):
        def g(*a, **k):
            ch = self.p.chain.get(self.eng) if name in DATA_OPS else None
            pre = ch.v if (ch is not None and ch.v and self.eng != "tensor") else None
            r = RecIns(auto=(ch.h, 1) if ch is not None else None)
            self.p.q[self.eng].append(("op", name, a, k, r, pre, ch))
            if ch is not None:
                ch.v += 1
            return r
        return g


class Prog:
    def __init__(self):
        self.q = {e: [] for e in ENGS}
        self.chain = {}

    def on(self, eng):
        rec = Recorder(self, eng)

        def deco(fn):
            fn(rec)
            return None
        return deco

    def replay(self, blk):
        for e in ENGS:
            recs = self.q[e]

            def section(h, recs=recs):
                for r in recs:
                    kind = r[0]
                    if kind == "w":
                        h.wait_ge(r[1], r[2])
                    elif kind == "nop":
                        ins = h.nop()
                        for sem, n in r[1].posts:
                            ins.then_inc(sem, n)
                    elif kind == "dma":
                        ins = h.dma_start(**r[1])
                        for sem, n in r[2].posts:
                            ins.then_inc(sem, n)
                    else:
                        _, name, a, k, ri, pre, ch = r
                        if pre is not None:
                            h.wait_ge(ch.h, pre)
                        ins = getattr(h, name)(*a, **k)
                        for sem, n in ri.posts:
                            ins.then_inc(sem, n)
            getattr(blk, e)(section)


class Ctx(dict):
    __getattr__ = dict.__getitem__
    __setattr__ = dict.__setitem__


def fence(c, waits=()):
    snap = {e: c.p.chain[e].v for e in CHAINED}
    for e in ENGS:
        @c.p.on(e)
        def _(h, e=e):
            for e2 in CHAINED:
                if e2 != e and snap[e2]:
                    h.wait_ge(c.p.chain[e2].h, snap[e2])
            for sem, val in waits:
                if val:
                    h.wait_ge(sem, val)


def build_program(mode="full"):
    nc = bass.Bass("TRN2", target_bir_lowering=False)
    T = {}

    def dram(name, shape, dty, kind="ExternalInput"):
        T[name] = nc.dram_tensor(name, shape, dty, kind=kind).ap()
        return T[name]

    dram("x_main", [M, HID], F32)
    dram("x_pre", [M, HID], F32)
    dram("wq_il", [HID, H * D], F32)
    dram("wk_il", [HID, KVH * D], F32)
    dram("wv_il", [HID, KVH * D], F32)
    dram("wo", [H * D, HID], F32)
    dram("gq_t", [NFT, 128, NKT, 16], I32)
    dram("uq_t", [NFT, 128, NKT, 16], I32)
    dram("dq_t", [NKT, 128, NFT, 16], I32)
    dram("gsr", [128, NFF], F32)
    dram("usr", [128, NFF], F32)
    dram("dsr", [NKT, NFF], F32)
    dram("gz", [32, NFF // 8], I32)
    dram("uz", [32, NFF // 8], I32)
    dram("dz", [NFT, 512], I32)
    dram("gsc", [32, NFF], F32)
    dram("usc", [32, NFF], F32)
    dram("dsc", [NFT, HID], F32)
    dram("identc", [128, 128], F32)
    dram("Uc", [128, 128], F32)
    dram("maskPTc", [128, 128], F32)
    dram("G32c", [128, 32], F32)
    dram("ones1c", [128, 1], F32)
    dram("onesrc", [1, 128], F32)
    dram("maskTn", [128, 512], F32)
    dram("maskTf", [128, 512], F32)
    dram("cosq", [M, 64], F32)
    dram("sinq", [M, 64], F32)
    dram("cosk", [M, 64], F32)
    dram("sink", [M, 64], F32)
    dram("coskh", [64, 64], F32)
    dram("sinkh", [64, 64], F32)
    dram("out", [M, HID], F32, kind="ExternalOutput")
    if mode == "ffn":
        dram("x2in", [M, HID], F32)
    dram("x1_sp", [M, HID], F32, kind="Internal")
    dram("h_sp", [NFT, 128, M], BF16, kind="Internal")
    dram("wd_sp", [NKT, 128, NFF], BF16, kind="Internal")
    dram("zs_sp", [128, 2 * NFF + HID], BF16, kind="Internal")
    dram("qs_sp", [NTT, 128, H * D], BF16, kind="Internal")
    dram("sq_sp", [NTT, 128, H * D], BF16, kind="Internal")
    dram("ks_sp", [16, 128, KVH * D], BF16, kind="Internal")
    dram("sk_sp", [NTT, 128, KVH * D], BF16, kind="Internal")
    dram("skh_sp", [64, KVH * D], BF16, kind="Internal")
    dram("gs_sp", [16, 128, KVH * D], F32, kind="Internal")
    dram("v_sp", [16, 128, KVH * D], BF16, kind="Internal")
    dram("dn_sp", [2, NKT, 128, 512], BF16, kind="Internal")
    dram("wo_sp", [2, NKT, 128, 512], BF16, kind="Internal")
    dram("H_sp", [NFT, M], BF16, kind="Internal")

    p = Prog()
    c = Ctx(nc=nc, T=T, p=p, mode=mode)
    es = ExitStack()
    with es:
        def sem(name):
            return Sem(es.enter_context(nc.semaphore(name)))
        c.sem = sem
        p.chain = {"tensor": sem("ch_t"), "vector": sem("ch_v"), "scalar": sem("ch_s")}
        c.cht = p.chain["tensor"]
        c.chv = p.chain["vector"]
        c.chs = p.chain["scalar"]
        c.bank = [es.enter_context(nc.psum_tensor(f"bank{i}", [128, 512], F32)) for i in range(8)]

        def mksb(stack):
            def sb(name, shape, dty):
                return stack.enter_context(nc.sbuf_tensor(name, shape, dty))
            return sb

        sb0 = mksb(es)
        c.ident = sb0("ident", [128, 128], BF16)
        c.identf = sb0("identf", [128, 128], F32)
        c.Uc = sb0("Uc_sb", [128, 128], F32)
        c.maskPT = sb0("maskPT", [128, 128], BF16)
        c.G32 = sb0("G32", [128, 32], BF16)
        c.ones1 = sb0("ones1", [128, 1], BF16)
        c.onesr = sb0("onesr", [1, 128], F32)
        c.maskTn = sb0("maskTn_sb", [128, 512], F32)
        c.maskTf = sb0("maskTf_sb", [128, 512], F32)
        emit_consts(c)

        with ExitStack() as s_p0:
            c.sb = mksb(s_p0)
            emit_zeros_prep(c)
        fence(c, waits=[c.zs_done])

        stk_xT1 = ExitStack()
        if mode == "ffn":
            with ExitStack() as s_x2:
                c.sb = mksb(s_x2)
                c.xT_ln1 = stk_xT1.enter_context(nc.sbuf_tensor("xT_ln1", [128, NKT, M], BF16))
                emit_x2_load(c)
            fence(c, waits=[c.x1_done])
        else:
            with ExitStack() as s_pa:
                c.sb = mksb(s_pa)
                emit_proj(c)
            fence(c, waits=[c.pa_sp, c.pa_spr, c.pa_spg])
            stk_oT = ExitStack()
            c.oT = stk_oT.enter_context(nc.sbuf_tensor("oT", [128, H, M], BF16))
            with ExitStack() as s_pb:
                c.sb = mksb(s_pb)
                emit_mixer(c)
            fence(c)
            with ExitStack() as s_pc:
                c.sb = mksb(s_pc)
                emit_wo(c)
            fence(c, waits=[c.wo_sp_done])
            stk_oT.close()
            c.xT_ln1 = stk_xT1.enter_context(nc.sbuf_tensor("xT_ln1", [128, NKT, M], BF16))
            with ExitStack() as s_pc2:
                c.sb = mksb(s_pc2)
                emit_postln(c, "ln1")
            fence(c, waits=[c.x1_done])

        if mode == "attn":
            emit_attn_out(c)
            stk_xT1.close()
        else:
            with ExitStack() as s_p4:
                c.sb = mksb(s_p4)
                emit_gate_up(c)
            fence(c, waits=[c.h_done, c.h_done2, c.H_done])
            stk_xT1.close()
            with ExitStack() as s_p5:
                c.sb = mksb(s_p5)
                emit_down(c)
            fence(c, waits=[c.dn_done])
            with ExitStack() as s_p6:
                c.sb = mksb(s_p6)
                emit_postln(c, "ln2")

        blk = es.enter_context(nc.Block())
        p.replay(blk)
    return nc


def emit_consts(c):
    T, p = c.T, c.p
    s_cg = c.sem("s_cg")
    s_cy = c.sem("s_cy")

    @p.on("gpsimd")
    def _(g):
        g.dma_start(out=c.ident[:], in_=T["identc"]).then_inc(s_cg.h, 16)
        g.dma_start(out=c.maskPT[:], in_=T["maskPTc"]).then_inc(s_cg.h, 16)
        g.dma_start(out=c.G32[:], in_=T["G32c"]).then_inc(s_cg.h, 16)
        g.dma_start(out=c.ones1[:], in_=T["ones1c"]).then_inc(s_cg.h, 16)
    s_cg.inc(64)

    @p.on("sync")
    def _(sy):
        sy.dma_start(out=c.Uc[:], in_=T["Uc"]).then_inc(s_cy.h, 16)
        sy.dma_start(out=c.identf[:], in_=T["identc"]).then_inc(s_cy.h, 16)
        sy.dma_start(out=c.onesr[:], in_=T["onesrc"]).then_inc(s_cy.h, 16)
        sy.dma_start(out=c.maskTn[:], in_=T["maskTn"]).then_inc(s_cy.h, 16)
        sy.dma_start(out=c.maskTf[:], in_=T["maskTf"]).then_inc(s_cy.h, 16)
    s_cy.inc(80)
    c.s_cg = s_cg
    c.s_cy = s_cy


def wait_consts(h, c):
    h.wait_ge(c.s_cg.h, c.s_cg.v)
    h.wait_ge(c.s_cy.h, c.s_cy.v)


def emit_zeros_prep(c):
    T, sb, p = c.T, c.sb, c.p
    chv = c.chv
    s_z = c.sem("s_z")
    s_zs = c.sem("s_zs")
    zq = sb("zq", [NFT, NFF // 8], I32)
    zsc = sb("zsc", [NFT, NFF], F32)
    znib = sb("znib", [NFT, NFF], I16)
    zout = sb("zout", [NFT, NFF], BF16)

    off = 0
    for zn, scn, rows, cols in (("gz", "gsc", 32, NFF), ("uz", "usc", 32, NFF),
                                ("dz", "dsc", NFT, HID)):
        @p.on("sync")
        def _(sy, zn=zn, scn=scn, rows=rows, cols=cols, prev=c.get("z_used", 0),
              prev_sp=c.get("z_spv", 0)):
            sy.wait_ge(chv.h, prev)
            if prev_sp:
                sy.wait_ge(s_zs.h, prev_sp)
            sy.dma_start(out=zq[0:rows, 0:cols // 8], in_=T[zn]).then_inc(s_z.h, 16)
            sy.dma_start(out=zsc[0:rows, 0:cols], in_=T[scn]).then_inc(s_z.h, 16)
        s_z.inc(32)
        tgt = s_z.v

        @p.on("vector")
        def _(v, rows=rows, cols=cols, tgt=tgt):
            v.wait_ge(s_z.h, tgt)
            z16 = zq[0:rows, 0:cols // 8].bitcast(I16).rearrange("r (w h) -> r w h", h=2)
            ov = znib[0:rows, 0:cols].rearrange("r (w e h) -> r w e h", e=4, h=2)
            for jb in range(4):
                v.tensor_scalar(out=ov[:, :, jb, :], in0=z16, scalar1=4 * jb, scalar2=15,
                                op0=ALU.logical_shift_right, op1=ALU.bitwise_and)
            v.tensor_scalar(out=zsc[0:rows, 0:cols], in0=zsc[0:rows, 0:cols],
                            scalar1=-1.0, scalar2=0.0, op0=ALU.mult, op1=ALU.add)
            v.tensor_tensor(out=zout[0:rows, 0:cols], in0=znib[0:rows, 0:cols],
                            in1=zsc[0:rows, 0:cols], op=ALU.mult)
        c["z_used"] = chv.v
        tgv = chv.v

        @p.on("sync")
        def _(sy, rows=rows, cols=cols, off=off, tgv=tgv):
            sy.wait_ge(chv.h, tgv)
            sy.dma_start(out=T["zs_sp"][0:rows, off:off + cols],
                         in_=zout[0:rows, 0:cols]).then_inc(s_zs.h, 16)
        s_zs.inc(16)
        c["z_spv"] = s_zs.v
        off += cols
    c.zs_done = (s_zs.h, s_zs.v)


def emit_proj(c):
    nc, T, sb, p, bank = c.nc, c.T, c.sb, c.p, c.bank
    cht, chv, chs = c.cht, c.chv, c.chs
    s_xl = [c.sem("s_xl0"), c.sem("s_xl1")]
    s_wl = [c.sem("s_wl0"), c.sem("s_wl1")]
    s_sp = c.sem("s_sp")
    s_spr = c.sem("s_spr")
    s_spg = c.sem("s_spg")
    s_cl = c.sem("s_cl")

    xbuf = [sb(f"xbuf{i}", [128, HID], F32) for i in range(2)]
    xT = sb("xT", [128, NKT, M], BF16)
    wsl = [sb(f"wsl{i}", [128, NKT, 512], BF16) for i in range(2)]
    expb = sb("expb", [128, 512], F32)
    sums = sb("sums", [128, 4], F32)
    recs = sb("recs", [128, 4], F32)
    pstage = sb("pstage", [128, 512], BF16)
    rstage = sb("rstage", [128, 512], BF16)
    gstage = sb("gstage", [128, 512], F32)
    ta = sb("ta", [128, 256], F32)
    tb = sb("tb", [128, 256], F32)
    cosb = sb("cosb", [128, 4, NTT, 64], F32)
    cosh = sb("cosh", [64, 128], F32)
    Bf = dict(expb=expb, sums=sums, recs=recs, pstage=pstage, rstage=rstage,
              gstage=gstage, ta=ta, tb=tb, cosb=cosb, cosh=cosh)
    Sm = dict(s_sp=s_sp, s_spr=s_spr, s_spg=s_spg, s_cl=s_cl)

    @p.on("scalar")
    def _(s):
        for i, nm in enumerate(("cosq", "sinq", "cosk", "sink")):
            s.dma_start(out=cosb[:, i],
                        in_=T[nm].rearrange("(tt p) f -> p tt f", p=128)).then_inc(s_cl.h, 16)
        s.dma_start(out=cosh[:, 0:64], in_=T["coskh"]).then_inc(s_cl.h, 16)
        s.dma_start(out=cosh[:, 64:128], in_=T["sinkh"]).then_inc(s_cl.h, 16)
    s_cl.inc(96)

    for phase in ("pre", "main"):
        xsrc = T["x_pre"] if phase == "pre" else T["x_main"]
        for tt in range(NTT):
            buf = xbuf[tt % 2]
            sxl = s_xl[tt % 2]

            @p.on("sync")
            def _(sy, tt=tt, buf=buf, prev_ev=c.get(f"pa_xtc_{tt - 2}", 0),
                  mmv=c.get("pa_mm_all", 0), sxl=sxl):
                sy.wait_ge(chv.h, prev_ev)
                sy.wait_ge(cht.h, mmv)
                sy.dma_start(out=buf[:], in_=xsrc[tt * 128:(tt + 1) * 128, :]).then_inc(sxl.h, 16)
            sxl.inc(16)
            xl_tgt = sxl.v
            v0 = chv.v

            @p.on("tensor")
            def _(t, buf=buf, xl_tgt=xl_tgt, v0=v0, first=(phase == "pre" and tt == 0), sxl=sxl):
                t.wait_ge(sxl.h, xl_tgt)
                if first:
                    wait_consts(t, c)
                for j in range(NKT):
                    if v0 + j - 1 > 0:
                        t.wait_ge(chv.h, v0 + j - 1)
                    xv = buf[:].rearrange("p (a b e) -> p a b e", a=32, b=4, e=32)[:, :, :, j]
                    pv = bank[j % 2].ap()
                    t.transpose(pv[:, 0:128], xv, c.identf[:])
            pe0 = cht.v - NKT

            @p.on("vector")
            def _(v, tt=tt, pe0=pe0):
                for j in range(NKT):
                    v.wait_ge(cht.h, pe0 + j + 1)
                    pv = bank[j % 2].ap()
                    v.tensor_copy(out=xT[:, j, tt * 128:(tt + 1) * 128], in_=pv[:, 0:128])
            c[f"pa_xtc_{tt}"] = chv.v
        xT_done = chv.v

        if phase == "main":
            groups = ([("q", fg) for fg in range(8)] +
                      [("k", 0), ("k", 1), ("v", 0), ("v", 1)])
        else:
            groups = [("k", 0), ("k", 1), ("v", 0), ("v", 1)]

        for gi, (kind, fg) in enumerate(groups):
            wslb = wsl[gi % 2]
            wsrc = {"q": "wq_il", "k": "wk_il", "v": "wv_il"}[kind]
            col0 = fg * 512

            swl = s_wl[gi % 2]

            @p.on("gpsimd")
            def _(g, wsrc=wsrc, col0=col0, wslb=wslb,
                  prev_mm=c.get(f"pa_mm_slot{gi % 2}", 0), xT_done=xT_done, swl=swl):
                g.wait_ge(cht.h, max(prev_mm, 0))
                g.wait_ge(chv.h, xT_done)
                g.dma_start(out=wslb[:],
                            in_=T[wsrc].rearrange("(j p) f -> p j f", p=128)[:, :, col0:col0 + 512]
                            ).then_inc(swl.h, 16)
            swl.inc(16)
            wl_tgt = swl.v

            for tt in range(NTT):
                bnk = bank[tt]
                ev_prev = c.get(f"pa_ev_{tt}", (0, 0))

                @p.on("tensor")
                def _(t, wslb=wslb, wl_tgt=wl_tgt, bnk=bnk, tt=tt,
                      ev_prev=ev_prev, xT_done=xT_done, swl=swl):
                    t.wait_ge(swl.h, wl_tgt)
                    t.wait_ge(chv.h, max(xT_done, ev_prev[0]))
                    if ev_prev[1]:
                        t.wait_ge(chs.h, ev_prev[1])
                    for j in range(NKT):
                        t.matmul(bnk.ap(), xT[:, j, tt * 128:(tt + 1) * 128],
                                 wslb[:, j, :], start=(j == 0), stop=(j == NKT - 1))
                mm_tgt = cht.v
                emit_proj_evac(c, phase, kind, fg, tt, bnk, mm_tgt, Bf, Sm)
                c[f"pa_ev_{tt}"] = (chv.v, chs.v)
            c[f"pa_mm_slot{gi % 2}"] = cht.v
        c["pa_mm_all"] = cht.v
    c.pa_sp = (s_sp.h, s_sp.v)
    c.pa_spr = (s_spr.h, s_spr.v)
    c.pa_spg = (s_spg.h, s_spg.v)


def emit_proj_evac(c, phase, kind, fg, tt, bnk, mm_tgt, Bf, Sm):
    T, p = c.T, c.p
    cht, chv, chs = c.cht, c.chv, c.chs
    s_sp, s_spr, s_spg, s_cl = Sm["s_sp"], Sm["s_spr"], Sm["s_spg"], Sm["s_cl"]
    expb, sums, recs = Bf["expb"], Bf["sums"], Bf["recs"]
    pstage, rstage, gstage = Bf["pstage"], Bf["rstage"], Bf["gstage"]
    ta, tb, cosb, cosh = Bf["ta"], Bf["tb"], Bf["cosb"], Bf["cosh"]
    main = (phase == "main")
    it = (NTT if main else 0) + tt

    if kind == "v":
        @p.on("scalar")
        def _(s, bnk=bnk, mm_tgt=mm_tgt, sp0=s_sp.v, vprev=chv.v):
            s.wait_ge(cht.h, mm_tgt)
            s.wait_ge(chv.h, vprev)
            if sp0:
                s.wait_ge(s_sp.h, sp0)
            s.activation(out=pstage[:], in_=bnk.ap(), func=AF.Identity)
        es_tgt = chs.v

        @p.on("sync")
        def _(sy, it=it, fg=fg, es_tgt=es_tgt):
            sy.wait_ge(chs.h, es_tgt)
            sy.dma_start(out=T["v_sp"][it, :, fg * 512:(fg + 1) * 512],
                         in_=pstage[:]).then_inc(s_sp.h, 16)
        s_sp.inc(16)
        return

    rope_this = main or (phase == "pre" and tt == 7 and kind == "k")
    cl_tgt = s_cl.v

    @p.on("scalar")
    def _(s, bnk=bnk, mm_tgt=mm_tgt, kind=kind, spg_pre=s_spg.v, vprev=chv.v):
        s.wait_ge(cht.h, mm_tgt)
        s.wait_ge(chv.h, vprev)
        if kind == "k" and spg_pre:
            s.wait_ge(s_spg.h, spg_pre)
        s.activation(out=expb[:], in_=bnk.ap(), func=AF.Exp)
        if kind == "k":
            s.activation(out=gstage[:], in_=bnk.ap(), func=AF.Exp, scale=-1.0)
            s.activation(out=gstage[:], in_=gstage[:], func=AF.Ln, bias=1.0)
    es_exp = chs.v

    @p.on("vector")
    def _(v, bnk=bnk, kind=kind, es_exp=es_exp, sp_pre=s_sp.v, spr_pre=s_spr.v,
          rope_this=rope_this, cl_tgt=cl_tgt, main=main):
        v.wait_ge(chs.h, es_exp)
        if sp_pre:
            v.wait_ge(s_sp.h, sp_pre)
        if rope_this and spr_pre:
            v.wait_ge(s_spr.h, spr_pre)
        v.tensor_reduce(out=sums[:].unsqueeze(2),
                        in_=expb[:].rearrange("p (h n) -> p h n", h=4),
                        axis=AX.X, op=ALU.add)
        if kind == "q":
            v.tensor_scalar(out=sums[:], in0=sums[:], scalar1=2.0, scalar2=0.0,
                            op0=ALU.mult, op1=ALU.add)
        v.reciprocal(out=recs[:], in_=sums[:])
        for hh in range(4):
            v.tensor_scalar(out=pstage[:, hh * 128:(hh + 1) * 128],
                            in0=expb[:, hh * 128:(hh + 1) * 128],
                            scalar1=recs[:, hh:hh + 1], scalar2=0.0,
                            op0=ALU.mult, op1=ALU.add)
        if rope_this:
            v.wait_ge(s_cl.h, cl_tgt)
            rows = slice(0, 128) if main else slice(64, 128)
            if main:
                ci = 0 if kind == "q" else 2
                cv = cosb[:, ci, tt, :]
                sv = cosb[:, ci + 1, tt, :]
            else:
                cv = cosh[:, 0:64]
                sv = cosh[:, 64:128]
            cv = cv[rows, :] if main else cv
            sv = sv[rows, :] if main else sv
            pv = bnk.ap()
            nr = 128 if main else 64
            pv3 = pv[rows, :].rearrange("p (h e n) -> p h e n", h=4, e=2)
            rs3 = rstage[rows, :].rearrange("p (h e n) -> p h e n", h=4, e=2)
            ta3 = ta[rows, :].rearrange("p (h n) -> p h n", h=4)
            tb3 = tb[rows, :].rearrange("p (h n) -> p h n", h=4)
            cvb = cv.unsqueeze(1).broadcast_to([nr, 4, 64])
            svb = sv.unsqueeze(1).broadcast_to([nr, 4, 64])
            v.tensor_tensor(out=ta3, in0=pv3[:, :, 0, :], in1=cvb, op=ALU.mult)
            v.tensor_tensor(out=tb3, in0=pv3[:, :, 1, :], in1=svb, op=ALU.mult)
            v.tensor_tensor(out=rs3[:, :, 0, :], in0=ta3, in1=tb3, op=ALU.subtract)
            v.tensor_tensor(out=ta3, in0=pv3[:, :, 1, :], in1=cvb, op=ALU.mult)
            v.tensor_tensor(out=tb3, in0=pv3[:, :, 0, :], in1=svb, op=ALU.mult)
            v.tensor_tensor(out=rs3[:, :, 1, :], in0=ta3, in1=tb3, op=ALU.add)
    ev_done = chv.v

    @p.on("sync")
    def _(sy, kind=kind, fg=fg, it=it, tt=tt, ev_done=ev_done, rope_this=rope_this,
          main=main):
        sy.wait_ge(chv.h, ev_done)
        cols = slice(fg * 512, (fg + 1) * 512)
        if kind == "q":
            sy.dma_start(out=T["qs_sp"][tt, :, cols], in_=pstage[:]).then_inc(s_sp.h, 16)
            sy.dma_start(out=T["sq_sp"][tt, :, cols], in_=rstage[:]).then_inc(s_spr.h, 16)
        else:
            sy.dma_start(out=T["ks_sp"][it, :, cols], in_=pstage[:]).then_inc(s_sp.h, 16)
            sy.dma_start(out=T["gs_sp"][it, :, cols], in_=gstage[:]).then_inc(s_spg.h, 16)
            if main:
                sy.dma_start(out=T["sk_sp"][tt, :, cols], in_=rstage[:]).then_inc(s_spr.h, 16)
            elif rope_this:
                sy.dma_start(out=T["skh_sp"][:, cols], in_=rstage[64:128, :]).then_inc(s_spr.h, 16)
    s_sp.inc(16)
    if kind == "q" or (kind == "k" and (main or rope_this)):
        s_spr.inc(16)
    if kind == "k":
        s_spg.inc(16)


def emit_mixer(c):
    nc, T, sb, p, bank = c.nc, c.T, c.sb, c.p, c.bank
    cht, chv, chs = c.cht, c.chv, c.chs
    s_l = c.sem("s_bl")

    S = sb("S", [128, KVH * 128], F32)
    Sb = sb("Sb", [128, KVH * 128], BF16)
    skT = sb("skT", [128, KVH, 64 + M], BF16)
    oT = c.oT
    qs = sb("qs", [128, H * D], BF16)
    sq = sb("sq", [128, H * D], BF16)
    ks = sb("ks", [128, KVH * D], BF16)
    sk = sb("sk", [128, KVH * D], BF16)
    gs = sb("gs", [128, KVH * D], F32)
    vt = [sb(f"vt{i}", [128, KVH * D], BF16) for i in range(2)]
    qsT = sb("qsT", [128, H, 128], BF16)
    sqT = sb("sqT", [128, H, 128], BF16)
    koT = sb("koT", [128, 128], BF16)
    ko = sb("ko", [128, 128], BF16)
    eGT = sb("eGT", [128, 128], F32)
    enG = sb("enG", [128, 128], F32)
    qtT = sb("qtT", [128, 4, 128], BF16)
    PTg = sb("PTg", [128, 512], BF16)
    sws = sb("sws", [128, 512], F32)
    swe = sb("swe", [128, 512], BF16)
    swz = sb("swz", [1, 512], F32)
    PTs = sb("PTs", [128, 512], BF16)
    vwin = sb("vwin", [128, KVH * D], BF16)
    s_vw = c.sem("s_vw")
    c._s_vw = s_vw
    c._vwin = vwin

    @p.on("vector")
    def _(v):
        v.memset(S[:], 0.0)
        v.memset(Sb[:], 0.0)

    Bf = dict(S=S, Sb=Sb, skT=skT, oT=oT, qs=qs, sq=sq, ks=ks, sk=sk, gs=gs,
              qsT=qsT, sqT=sqT, koT=koT, ko=ko, eGT=eGT, enG=enG, qtT=qtT,
              PTg=PTg, sws=sws, swe=swe, swz=swz, PTs=PTs)

    for it in range(16):
        main = it >= NTT
        tt = it - NTT if main else it
        vtc = vt[it % 2]

        @p.on("sync")
        def _(sy, it=it, tt=tt, main=main, vtc=vtc, done=c.get("mix_done", 0)):
            sy.wait_ge(chv.h, done)
            sy.wait_ge(cht.h, c.get("mix_done_t", 0))
            sy.dma_start(out=ks[:], in_=T["ks_sp"][it]).then_inc(s_l.h, 16)
            sy.dma_start(out=gs[:], in_=T["gs_sp"][it]).then_inc(s_l.h, 16)
            sy.dma_start(out=vtc[:], in_=T["v_sp"][it]).then_inc(s_l.h, 16)
            if main:
                sy.dma_start(out=qs[:], in_=T["qs_sp"][tt]).then_inc(s_l.h, 16)
                sy.dma_start(out=sq[:], in_=T["sq_sp"][tt]).then_inc(s_l.h, 16)
                sy.dma_start(out=sk[:], in_=T["sk_sp"][tt]).then_inc(s_l.h, 16)
        s_l.inc(48 + (48 if main else 0))
        ld_tgt = s_l.v

        if main:
            vwin = c._vwin
            s_vw = c._s_vw

            @p.on("sync")
            def _(sy, ld_tgt=ld_tgt, vtc=vtc, vtp=vt[(it + 1) % 2]):
                sy.wait_ge(s_l.h, ld_tgt)
                sy.dma_start(out=vwin[0:64, :], in_=vtp[64:128, :]).then_inc(s_vw.h, 16)
                sy.dma_start(out=vwin[64:128, :], in_=vtc[0:64, :]).then_inc(s_vw.h, 16)
            s_vw.inc(32)
            c["vw_tgt"] = s_vw.v

            for grp, (srcb, dstb, nseg) in enumerate(((qs, qsT, H), (sq, sqT, H), (sk, None, KVH))):
                v0 = chv.v

                @p.on("tensor")
                def _(t, srcb=srcb, nseg=nseg, ld_tgt=ld_tgt, v0=v0, grp=grp):
                    if grp == 0:
                        t.wait_ge(s_l.h, ld_tgt)
                    for sg in range(nseg):
                        if v0 + sg - 1 > 0:
                            t.wait_ge(chv.h, v0 + sg - 1)
                        pv = bank[4 + sg % 2].ap().bitcast(BF16)
                        t.transpose(pv[:, 0:128], srcb[:, sg * 128:(sg + 1) * 128],
                                    c.ident[:])
                pe0 = cht.v - nseg

                @p.on("vector")
                def _(v, dstb=dstb, nseg=nseg, pe0=pe0, tt=tt, grp=grp):
                    for sg in range(nseg):
                        v.wait_ge(cht.h, pe0 + sg + 1)
                        pv = bank[4 + sg % 2].ap().bitcast(BF16)
                        if grp < 2:
                            v.tensor_copy(out=dstb[:, sg, :], in_=pv[:, 0:128])
                        else:
                            v.tensor_copy(out=skT[:, sg, 64 + tt * 128:64 + (tt + 1) * 128],
                                          in_=pv[:, 0:128])

        for kv in range(KVH):
            emit_chunk(c, it, tt, main, kv, Bf, vtc, vt[(it + 1) % 2], s_l, ld_tgt)
        c["mix_done"] = chv.v
        c["mix_done_t"] = cht.v


def emit_chunk(c, it, tt, main, kv, Bf, vtc, vtp, s_l, ld_tgt):
    p, bank = c.p, c.bank
    cht, chv, chs = c.cht, c.chv, c.chs
    S, Sb, skT, oT = Bf["S"], Bf["Sb"], Bf["skT"], Bf["oT"]
    qs, sq, ks, sk, gs = Bf["qs"], Bf["sq"], Bf["ks"], Bf["sk"], Bf["gs"]
    qsT, sqT, koT, ko = Bf["qsT"], Bf["sqT"], Bf["koT"], Bf["ko"]
    eGT, enG, qtT = Bf["eGT"], Bf["enG"], Bf["qtT"]
    PTg, sws, swe, swz, PTs = Bf["PTg"], Bf["sws"], Bf["swe"], Bf["swz"], Bf["PTs"]
    ksg = slice(kv * 128, (kv + 1) * 128)
    Sseg = slice(kv * 128, (kv + 1) * 128)

    bG = bank[6]
    bO = bank[2 if kv % 2 == 0 else 3]
    bW = bank[7]
    bD = bank[6]

    @p.on("tensor")
    def _(t, ld_tgt=ld_tgt, kv=kv, vpre=chv.v, spre=chs.v):
        if kv == 0 and not main:
            t.wait_ge(s_l.h, ld_tgt)
        t.wait_ge(chv.h, vpre)
        t.wait_ge(chs.h, spre)
        t.matmul(bG.ap()[:, 0:128], gs[:, ksg], c.Uc[:], start=True, stop=True)
        t.matmul(bG.ap()[:, 128:256], c.Uc[:], gs[:, ksg], start=True, stop=True)
    gmm = cht.v

    @p.on("scalar")
    def _(s, gmm=gmm):
        s.wait_ge(cht.h, gmm)
        s.activation(out=eGT[:], in_=bG.ap()[:, 0:128], func=AF.Exp)
        s.activation(out=enG[:], in_=bG.ap()[:, 128:256], func=AF.Exp, scale=-1.0)
    gex = chs.v

    @p.on("vector")
    def _(v, gex=gex, kv=kv):
        v.wait_ge(chs.h, gex)
        v.tensor_tensor(out=ko[:], in0=ks[:, ksg], in1=enG[:], op=ALU.mult)
    kov = chv.v

    if main:
        @p.on("tensor")
        def _(t, kov=kov):
            t.wait_ge(chv.h, kov)
            pv = bank[5].ap().bitcast(BF16)
            t.transpose(pv[:, 0:128], ko[:], c.ident[:])
        koT_mm = cht.v

        @p.on("vector")
        def _(v, tgt=koT_mm, kv=kv):
            v.wait_ge(cht.h, tgt)
            pv = bank[5].ap().bitcast(BF16)
            v.tensor_copy(out=koT[:], in_=pv[:, 0:128])
            v.tensor_tensor(out=qtT[:], in0=qsT[:, kv * 4:(kv + 1) * 4, :],
                            in1=eGT[:].unsqueeze(1).broadcast_to([128, 4, 128]),
                            op=ALU.mult)
        qt_done = chv.v

        @p.on("tensor")
        def _(t, kv=kv, tt=tt, qt_done=qt_done):
            t.wait_ge(chv.h, qt_done)
            for qi in range(4):
                qh = kv * 4 + qi
                for blk_ in range(2):
                    w0 = tt * 128 + blk_ * 64
                    t.matmul(bW.ap()[:, (qi * 2 + blk_) * 64:(qi * 2 + blk_ + 1) * 64],
                             skT[:, kv, w0:w0 + 128],
                             sqT[:, qh, blk_ * 64:(blk_ + 1) * 64],
                             start=True, stop=True)
        swmm = cht.v

        @p.on("vector")
        def _(v, swmm=swmm, it=it):
            v.wait_ge(cht.h, swmm)
            mask = c.maskTf if it == NTT else c.maskTn
            v.tensor_tensor(out=sws[:], in0=bW.ap(), in1=mask[:], op=ALU.add)
        swm = chv.v

        @p.on("scalar")
        def _(s, swm=swm):
            s.wait_ge(chv.h, swm)
            s.activation(out=swe[:], in_=sws[:], func=AF.Exp)
        swex = chs.v

        @p.on("tensor")
        def _(t, swex=swex):
            t.wait_ge(chs.h, swex)
            t.matmul(bW.ap()[0:1, 0:512], c.ones1[:], swe[:, 0:512],
                     start=True, stop=True)
        zmm = cht.v

        @p.on("vector")
        def _(v, zmm=zmm):
            v.wait_ge(cht.h, zmm)
            v.reciprocal(out=swz[:], in_=bW.ap()[0:1, 0:512])
            v.tensor_scalar(out=swz[:], in0=swz[:], scalar1=0.5, scalar2=0.0,
                            op0=ALU.mult, op1=ALU.add)
        zrec = chv.v

        @p.on("tensor")
        def _(t, zrec=zrec):
            t.wait_ge(chv.h, zrec)
            t.matmul(bW.ap()[:, 0:512], c.onesr[:], swz[:], start=True, stop=True)
        repmm = cht.v

        @p.on("vector")
        def _(v, repmm=repmm):
            v.wait_ge(cht.h, repmm)
            v.tensor_tensor(out=PTs[:], in0=swe[:], in1=bW.ap(), op=ALU.mult)
        pts_done = chv.v

        @p.on("tensor")
        def _(t, kv=kv, qt_done=qt_done):
            t.wait_ge(chv.h, qt_done)
            for qi in range(4):
                t.matmul(bank[5].ap()[:, qi * 128:(qi + 1) * 128],
                         koT[:], qtT[:, qi, :], start=True, stop=True)
        ptgmm = cht.v

        @p.on("vector")
        def _(v, ptgmm=ptgmm):
            v.wait_ge(cht.h, ptgmm)
            v.tensor_tensor(out=PTg[:].rearrange("p (q n) -> p q n", q=4),
                            in0=bank[5].ap().rearrange("p (q n) -> p q n", q=4),
                            in1=c.maskPT[:].unsqueeze(1).broadcast_to([128, 4, 128]),
                            op=ALU.mult)
        ptg_done = chv.v

        @p.on("tensor")
        def _(t, kv=kv, it=it, ptg_done=ptg_done, pts_done=pts_done,
              o_ev=c.get(f"oev_{kv % 2}", 0), vtc=vtc, vtp=vtp):
            t.wait_ge(chv.h, max(ptg_done, pts_done))
            if o_ev:
                t.wait_ge(chs.h, o_ev)
            t.wait_ge(c._s_vw.h, c.get("vw_tgt", 0))
            for qi in range(4):
                osl = bO.ap()[:, qi * 128:(qi + 1) * 128]
                t.matmul(osl, vtc[:, ksg], PTg[:, qi * 128:(qi + 1) * 128],
                         start=True, stop=False)
                t.matmul(osl, Sb[:, Sseg], qtT[:, qi, :], start=False, stop=False)
                pA = PTs[:, (qi * 2) * 64:(qi * 2 + 1) * 64]
                t.matmul(osl[:, 0:64], c._vwin[:, ksg], pA[:], start=False, stop=False)
                pB = PTs[:, (qi * 2 + 1) * 64:(qi * 2 + 2) * 64]
                t.matmul(osl[:, 64:128], vtc[:, ksg], pB[:], start=False, stop=True)
        omm = cht.v

        @p.on("scalar")
        def _(s, kv=kv, tt=tt, omm=omm):
            s.wait_ge(cht.h, omm)
            for qi in range(4):
                qh = kv * 4 + qi
                s.activation(out=oT[:, qh, tt * 128:(tt + 1) * 128],
                             in_=bO.ap()[:, qi * 128:(qi + 1) * 128],
                             func=AF.Identity)
        c[f"oev_{kv % 2}"] = chs.v

    @p.on("tensor")
    def _(t, kov=kov, vpre=chv.v, vtc=vtc):
        t.wait_ge(chv.h, max(kov, vpre))
        t.matmul(bD.ap()[:, 256:384], ko[:], vtc[:, ksg], start=True, stop=True)
    dmm = cht.v

    @p.on("vector")
    def _(v, dmm=dmm, kv=kv):
        v.wait_ge(cht.h, dmm)
        v.tensor_tensor(out=S[:, Sseg], in0=S[:, Sseg], in1=bD.ap()[:, 256:384], op=ALU.add)
        v.tensor_scalar(out=S[:, Sseg], in0=S[:, Sseg], scalar1=eGT[:, 127:128],
                        scalar2=0.0, op0=ALU.mult, op1=ALU.add)
        v.tensor_copy(out=Sb[:, Sseg], in_=S[:, Sseg])

    if it == NTT - 1 and kv == KVH - 1:
        s_halo = c.sem("s_halo")

        @p.on("sync")
        def _(sy, sp=chv.v):
            sy.wait_ge(chv.h, sp)
            sy.dma_start(out=sk[0:64, :], in_=c.T["skh_sp"]).then_inc(s_halo.h, 16)

        v0h = chv.v

        @p.on("tensor")
        def _(t, v0h=v0h):
            t.wait_ge(s_halo.h, 16)
            for kk in range(KVH):
                if v0h + kk - 1 > 0:
                    t.wait_ge(chv.h, v0h + kk - 1)
                pv = bank[4 + kk % 2].ap().bitcast(BF16)
                t.transpose(pv[:, 0:64], sk[0:64, kk * 128:(kk + 1) * 128], c.ident[0:64, 0:64])
        peh = cht.v - KVH

        @p.on("vector")
        def _(v, peh=peh):
            for kk in range(KVH):
                v.wait_ge(cht.h, peh + kk + 1)
                pv = bank[4 + kk % 2].ap().bitcast(BF16)
                v.tensor_copy(out=skT[:, kk, 0:64], in_=pv[:, 0:64])


def emit_wo(c):
    nc, T, sb, p, bank = c.nc, c.T, c.sb, c.p, c.bank
    cht, chv, chs = c.cht, c.chv, c.chs
    oT = c.oT
    s_wl = [c.sem("s_cwl0"), c.sem("s_cwl1")]
    s_sp = c.sem("s_csp")

    wsl = [sb(f"cwsl{i}", [128, NKT, 128], BF16) for i in range(2)]
    wo_st = sb("wo_st", [128, M], BF16)

    for ht in range(NKT):
        wslb = wsl[ht % 2]

        swl = s_wl[ht % 2]

        @p.on("gpsimd")
        def _(g, ht=ht, wslb=wslb, prev_mm=c.get(f"wo_mm_slot{ht % 2}", 0), swl=swl):
            g.wait_ge(cht.h, prev_mm)
            g.dma_start(out=wslb[:],
                        in_=T["wo"].rearrange("(o p) n -> p o n", p=128)[:, :, ht * 128:(ht + 1) * 128]
                        ).then_inc(swl.h, 16)
        swl.inc(16)
        wl_tgt = swl.v

        @p.on("tensor")
        def _(t, ht=ht, wslb=wslb, wl_tgt=wl_tgt, ev_prev=c.get("wo_ev_prev", 0), swl=swl):
            t.wait_ge(swl.h, wl_tgt)
            t.wait_ge(chs.h, ev_prev)
            for half in range(2):
                bnk = bank[half]
                for of in range(NKT):
                    t.matmul(bnk.ap(), wslb[:, of, :],
                             oT[:, of, half * 512:(half + 1) * 512],
                             start=(of == 0), stop=(of == NKT - 1))
        c[f"wo_mm_slot{ht % 2}"] = cht.v
        mm_tgt = cht.v

        @p.on("scalar")
        def _(s, mm_tgt=mm_tgt, spp=s_sp.v):
            s.wait_ge(cht.h, mm_tgt)
            if spp:
                s.wait_ge(s_sp.h, spp)
            s.activation(out=wo_st[:, 0:512], in_=bank[0].ap(), func=AF.Identity)
            s.activation(out=wo_st[:, 512:1024], in_=bank[1].ap(), func=AF.Identity)
        c["wo_ev_prev"] = chs.v
        ev_tgt = chs.v

        @p.on("sync")
        def _(sy, ht=ht, ev_tgt=ev_tgt):
            sy.wait_ge(chs.h, ev_tgt)
            sy.dma_start(out=T["wo_sp"][0, ht], in_=wo_st[:, 0:512]).then_inc(s_sp.h, 16)
            sy.dma_start(out=T["wo_sp"][1, ht], in_=wo_st[:, 512:1024]).then_inc(s_sp.h, 16)
        s_sp.inc(32)
    c.wo_sp_done = (s_sp.h, s_sp.v)


def emit_postln(c, which):
    nc, T, sb, p, bank = c.nc, c.T, c.sb, c.p, c.bank
    cht, chv, chs = c.cht, c.chv, c.chs
    s_l = [c.sem(f"s_{which}l0"), c.sem(f"s_{which}l1")]
    s_x = [c.sem(f"s_{which}x0"), c.sem(f"s_{which}x1")]
    s_o = c.sem(f"s_{which}o")
    src_sp = T["wo_sp"] if which == "ln1" else T["dn_sp"]
    x_src = T["x_main"] if which == "ln1" else T["x1_sp"]

    dsl = [sb(f"{which}dl{i}", [128, NKT, 128], BF16) for i in range(2)]
    xrb = [sb(f"{which}xr{i}", [128, HID], F32) for i in range(2)]
    dtm = sb(f"{which}dtm", [128, HID], BF16)
    lsum = sb(f"{which}sum", [128, HID], F32)
    lsq = sb(f"{which}sq", [128, HID], F32)
    stat = sb(f"{which}stat", [128, 4], F32)
    outb = sb(f"{which}out", [128, HID], F32)
    outbb = sb(f"{which}outb", [128, HID], BF16)

    for tt in range(NTT):
        half = tt // 4
        q = tt % 4
        dbuf = dsl[tt % 2]

        sl = s_l[tt % 2]
        sx = s_x[tt % 2]

        @p.on("sync")
        def _(sy, half=half, q=q, dbuf=dbuf, tr_prev=c.get(f"{which}_tr_slot{tt % 2}", 0), sl=sl):
            sy.wait_ge(cht.h, tr_prev)
            sy.dma_start(out=dbuf[:],
                         in_=src_sp[half, :, :, q * 128:(q + 1) * 128]
                         .rearrange("ht p t -> p ht t")).then_inc(sl.h, 16)
        sl.inc(16)
        l_tgt = sl.v

        @p.on("scalar")
        def _(s, tt=tt, xb=xrb[tt % 2], used=c.get(f"{which}_x_slot{tt % 2}", 0), sx=sx):
            s.wait_ge(chv.h, used)
            s.dma_start(out=xb[:], in_=x_src[tt * 128:(tt + 1) * 128, :]).then_inc(sx.h, 16)
        sx.inc(16)
        x_tgt = sx.v
        v0 = chv.v

        @p.on("tensor")
        def _(t, dbuf=dbuf, l_tgt=l_tgt, v0=v0, sl=sl):
            t.wait_ge(sl.h, l_tgt)
            for j in range(NKT):
                if v0 + j - 1 > 0:
                    t.wait_ge(chv.h, v0 + j - 1)
                pv = bank[j % 2].ap().bitcast(BF16)
                t.transpose(pv[:, 0:128], dbuf[:, j, :], c.ident[:])
        c[f"{which}_tr_slot{tt % 2}"] = cht.v
        pe0 = cht.v - NKT

        @p.on("vector")
        def _(v, pe0=pe0, xb=xrb[tt % 2], x_tgt=x_tgt, sx=sx):
            for j in range(NKT):
                v.wait_ge(cht.h, pe0 + j + 1)
                pv = bank[j % 2].ap().bitcast(BF16)
                v.tensor_copy(out=dtm[:, j * 128:(j + 1) * 128], in_=pv[:, 0:128])
            v.wait_ge(sx.h, x_tgt)
            v.tensor_tensor(out=lsum[:], in0=xb[:], in1=dtm[:], op=ALU.add)
            v.tensor_reduce(out=stat[:, 0:1], in_=lsum[:], axis=AX.X, op=ALU.add)
        c[f"{which}_x_slot{tt % 2}"] = chv.v
        sum_done = chv.v

        @p.on("scalar")
        def _(s, sum_done=sum_done):
            s.wait_ge(chv.h, sum_done)
            s.activation(out=lsq[:], in_=lsum[:], func=AF.Square)
        sq_done = chs.v

        @p.on("vector")
        def _(v, sq_done=sq_done):
            v.wait_ge(chs.h, sq_done)
            v.tensor_reduce(out=stat[:, 1:2], in_=lsq[:], axis=AX.X, op=ALU.add)
            v.tensor_scalar(out=stat[:, 0:2], in0=stat[:, 0:2], scalar1=1.0 / HID,
                            scalar2=0.0, op0=ALU.mult, op1=ALU.add)
            v.tensor_tensor(out=stat[:, 2:3], in0=stat[:, 0:1], in1=stat[:, 0:1], op=ALU.mult)
            v.tensor_tensor(out=stat[:, 2:3], in0=stat[:, 1:2], in1=stat[:, 2:3], op=ALU.subtract)
            v.tensor_scalar(out=stat[:, 2:3], in0=stat[:, 2:3], scalar1=1.0, scalar2=EPS,
                            op0=ALU.mult, op1=ALU.add)
        var_done = chv.v

        @p.on("scalar")
        def _(s, var_done=var_done):
            s.wait_ge(chv.h, var_done)
            s.activation(out=stat[:, 3:4], in_=stat[:, 2:3], func=AF.Ln)
        ln_done = chs.v

        @p.on("vector")
        def _(v, ln_done=ln_done):
            v.wait_ge(chs.h, ln_done)
            v.tensor_scalar(out=stat[:, 3:4], in0=stat[:, 3:4], scalar1=-0.5, scalar2=0.0,
                            op0=ALU.mult, op1=ALU.add)
        lnv_done = chv.v

        @p.on("scalar")
        def _(s, lnv_done=lnv_done):
            s.wait_ge(chv.h, lnv_done)
            s.activation(out=stat[:, 2:3], in_=stat[:, 3:4], func=AF.Exp)
        rstd_done = chs.v

        @p.on("vector")
        def _(v, rstd_done=rstd_done, o_prev=c.get(f"{which}_o_prev", 0), which2=which):
            v.wait_ge(chs.h, rstd_done)
            if o_prev:
                v.wait_ge(s_o.h, o_prev)
            v.tensor_scalar(out=outb[:], in0=lsum[:], scalar1=stat[:, 0:1],
                            scalar2=stat[:, 2:3], op0=ALU.subtract, op1=ALU.mult)
            if which2 == "ln1":
                v.tensor_copy(out=outbb[:], in_=outb[:])
        norm_done = chv.v

        @p.on("sync")
        def _(sy, tt=tt, norm_done=norm_done, which2=which):
            sy.wait_ge(chv.h, norm_done)
            dst = T["x1_sp"] if which2 == "ln1" else T["out"]
            sy.dma_start(out=dst[tt * 128:(tt + 1) * 128, :], in_=outb[:]).then_inc(s_o.h, 16)
        s_o.inc(16)
        c[f"{which}_o_prev"] = s_o.v

        if which == "ln1":
            xT_ln1 = c.xT_ln1
            v1 = chv.v

            @p.on("tensor")
            def _(t, norm_done=norm_done, v1=v1):
                t.wait_ge(chv.h, norm_done)
                for j in range(NKT):
                    if v1 + j - 1 > 0:
                        t.wait_ge(chv.h, v1 + j - 1)
                    xv = outbb[:].rearrange("p (a b e) -> p a b e", a=32, b=4, e=32)[:, :, :, j]
                    pv = bank[2 + j % 2].ap().bitcast(BF16)
                    t.transpose(pv[:, 0:128], xv, c.ident[:])
            c[f"{which}_tr_prev"] = cht.v
            pe1 = cht.v - NKT

            @p.on("vector")
            def _(v, tt=tt, pe1=pe1, xT_ln1=xT_ln1):
                for j in range(NKT):
                    v.wait_ge(cht.h, pe1 + j + 1)
                    pv = bank[2 + j % 2].ap().bitcast(BF16)
                    v.tensor_copy(out=xT_ln1[:, j, tt * 128:(tt + 1) * 128],
                                  in_=pv[:, 0:128])
            c[f"{which}_x_used"] = chv.v
    if which == "ln1":
        c.xT_done = chv.v
        c.x1_done = (s_o.h, s_o.v)
    else:
        @p.on("gpsimd")
        def _(g, tgt=s_o.v):
            g.wait_ge(s_o.h, tgt)


def emit_attn_out(c):
    T, p = c.T, c.p
    s_ao = c.sem("s_ao")

    @p.on("sync")
    def _(sy):
        sy.dma_start(out=T["out"], in_=T["x1_sp"]).then_inc(s_ao.h, 16)
    s_ao.inc(16)

    @p.on("gpsimd")
    def _(g, tgt=s_ao.v):
        g.wait_ge(s_ao.h, tgt)


def emit_x2_load(c):
    nc, T, sb, p, bank = c.nc, c.T, c.sb, c.p, c.bank
    cht, chv, chs = c.cht, c.chv, c.chs
    s_xl = [c.sem("s_fxl0"), c.sem("s_fxl1")]
    s_o = c.sem("s_fo")
    xbuf = [sb(f"fxb{i}", [128, HID], F32) for i in range(2)]
    xbb = sb("fxbb", [128, HID], BF16)
    xT_ln1 = c.xT_ln1

    for tt in range(NTT):
        buf = xbuf[tt % 2]

        sxl = s_xl[tt % 2]

        @p.on("sync")
        def _(sy, tt=tt, buf=buf, prev=c.get(f"fx_used_{tt - 2}", 0), sxl=sxl):
            sy.wait_ge(chv.h, prev)
            sy.dma_start(out=buf[:], in_=T["x2in"][tt * 128:(tt + 1) * 128, :]).then_inc(sxl.h, 16)
            sy.dma_start(out=T["x1_sp"][tt * 128:(tt + 1) * 128, :],
                         in_=T["x2in"][tt * 128:(tt + 1) * 128, :]).then_inc(s_o.h, 16)
        sxl.inc(16)
        s_o.inc(16)
        xl_tgt = sxl.v

        @p.on("vector")
        def _(v, buf=buf, xl_tgt=xl_tgt, prev_tr=c.get(f"fx_tr_{tt - 1}", 0), sxl=sxl):
            v.wait_ge(sxl.h, xl_tgt)
            v.wait_ge(cht.h, prev_tr)
            v.tensor_copy(out=xbb[:], in_=buf[:])
        cv_tgt = chv.v
        v1 = chv.v

        @p.on("tensor")
        def _(t, cv_tgt=cv_tgt, v1=v1, first=(tt == 0)):
            t.wait_ge(chv.h, cv_tgt)
            if first:
                wait_consts(t, c)
            for j in range(NKT):
                if v1 + j - 1 > 0:
                    t.wait_ge(chv.h, v1 + j - 1)
                xv = xbb[:].rearrange("p (a b e) -> p a b e", a=32, b=4, e=32)[:, :, :, j]
                pv = bank[j % 2].ap().bitcast(BF16)
                t.transpose(pv[:, 0:128], xv, c.ident[:])
        c[f"fx_tr_{tt}"] = cht.v
        pe0 = cht.v - NKT

        @p.on("vector")
        def _(v, tt=tt, pe0=pe0):
            for j in range(NKT):
                v.wait_ge(cht.h, pe0 + j + 1)
                pv = bank[j % 2].ap().bitcast(BF16)
                v.tensor_copy(out=xT_ln1[:, j, tt * 128:(tt + 1) * 128],
                              in_=pv[:, 0:128])
        c[f"fx_used_{tt}"] = chv.v
    c.xT_done = chv.v
    c.x1_done = (s_o.h, s_o.v)


def emit_gate_up(c):
    nc, T, sb, p, bank = c.nc, c.T, c.sb, c.p, c.bank
    cht, chv, chs = c.cht, c.chv, c.chs
    xT = c.xT_ln1
    s_pl = [c.sem("s_gpl0"), c.sem("s_gpl1")]
    s_sl = [c.sem("s_gsl0"), c.sem("s_gsl1")]
    s_zl = c.sem("s_gzl")
    s_hs = [c.sem("s_ghs0"), c.sem("s_ghs1")]
    s_Hsp = c.sem("s_gHsp")

    pkb = [sb(f"gpk{i}", [128, 2, NKT, 16], I32) for i in range(2)]
    nib = sb("gnib", [128, NKT * 128], I16)
    wsl = [sb(f"gws{i}", [128, 2, NKT, 128], BF16) for i in range(2)]
    ssl = [sb(f"gss{i}", [128, 2, 128], F32) for i in range(2)]
    negz = sb("gnegz", [32, 2 * NFF], BF16)
    Xs = sb("gXs", [32, M], BF16)
    est = sb("gest", [128, M], F32)
    sgt = sb("gsgt", [128, M], F32)
    hst = [sb(f"ghst{i}", [128, M], BF16) for i in range(2)]
    hrow = sb("ghrow", [1, M], BF16)

    @p.on("gpsimd")
    def _(g):
        g.dma_start(out=negz[:, 0:NFF], in_=T["zs_sp"][0:32, 0:NFF]).then_inc(s_zl.h, 16)
        g.dma_start(out=negz[:, NFF:2 * NFF], in_=T["zs_sp"][0:32, NFF:2 * NFF]).then_inc(s_zl.h, 16)
    s_zl.inc(32)

    @p.on("tensor")
    def _(t, xT_done=c.xT_done, zl=s_zl.v):
        t.wait_ge(chv.h, xT_done)
        for half in range(2):
            bnk = bank[6] if half == 0 else bank[7]
            for j in range(NKT):
                t.matmul(bnk.ap()[0:32, 0:512], c.G32[:],
                         xT[:, j, half * 512:(half + 1) * 512],
                         start=(j == 0), stop=(j == NKT - 1))
    Xmm = cht.v

    @p.on("vector")
    def _(v, tgt=Xmm):
        v.wait_ge(cht.h, tgt)
        v.tensor_copy(out=Xs[:, 0:512], in_=bank[6].ap()[0:32, 0:512])
        v.tensor_copy(out=Xs[:, 512:1024], in_=bank[7].ap()[0:32, 0:512])
    X_done = chv.v

    def emit_gu_dequant(f):
        pk = pkb[f % 2]
        ws = wsl[f % 2]
        ss = ssl[f % 2]
        dq_prev = c.get(f"g_dq_slot{f % 2}", 0)
        mm_prev = c.get(f"g_mm_slot{f % 2}", 0)
        spl = s_pl[f % 2]
        ssl_s = s_sl[f % 2]

        @p.on("sync")
        def _(sy, f=f, pk=pk, dq_prev=dq_prev, spl=spl):
            sy.wait_ge(chv.h, dq_prev)
            sy.dma_start(out=pk[:, 0], in_=T["gq_t"][f]).then_inc(spl.h, 16)
            sy.dma_start(out=pk[:, 1], in_=T["uq_t"][f]).then_inc(spl.h, 16)
        spl.inc(32)
        pl_tgt = spl.v

        @p.on("gpsimd")
        def _(g, f=f, ss=ss, dq_prev=dq_prev, ssl_s=ssl_s):
            g.wait_ge(chv.h, dq_prev)
            g.dma_start(out=ss[:, 0, :], in_=T["gsr"][:, f * 128:(f + 1) * 128]).then_inc(ssl_s.h, 16)
            g.dma_start(out=ss[:, 1, :], in_=T["usr"][:, f * 128:(f + 1) * 128]).then_inc(ssl_s.h, 16)
        ssl_s.inc(32)
        sl_tgt = ssl_s.v

        @p.on("vector")
        def _(v, f=f, pk=pk, ws=ws, ss=ss, pl_tgt=pl_tgt, sl_tgt=sl_tgt,
              mm_prev=mm_prev, spl=spl, ssl_s=ssl_s):
            v.wait_ge(spl.h, pl_tgt)
            v.wait_ge(ssl_s.h, sl_tgt)
            v.wait_ge(cht.h, mm_prev)
            for mat in range(2):
                z16 = pk[:, mat].bitcast(I16).rearrange("p j (w h) -> p j w h", h=2)
                ov = nib[:].rearrange("p (j w e h) -> p j w e h", j=NKT, e=4, h=2)
                for jb in range(4):
                    v.tensor_scalar(out=ov[:, :, :, jb, :], in0=z16, scalar1=4 * jb,
                                    scalar2=15, op0=ALU.logical_shift_right, op1=ALU.bitwise_and)
                nv = nib[:].rearrange("p (j n) -> p j n", j=NKT)
                v.tensor_tensor(out=ws[:, mat], in0=nv,
                                in1=ss[:, mat, :].unsqueeze(1).broadcast_to([128, NKT, 128]),
                                op=ALU.mult)
        c[f"g_dq_slot{f % 2}"] = chv.v

    emit_gu_dequant(0)
    for f in range(NFT):
        ws = wsl[f % 2]
        dq_tgt = c[f"g_dq_slot{f % 2}"]
        if f % 2 == 0:
            b_g, b_u = [bank[0], bank[1]], [bank[2], bank[3]]
        else:
            b_g, b_u = [bank[4], bank[5]], [bank[6], bank[7]]

        @p.on("tensor")
        def _(t, f=f, ws=ws, dq_tgt=dq_tgt, b_g=b_g, b_u=b_u, X_done=X_done,
              ev_prev2=c.get(f"g_ev_slot{f % 2}", 0), es_prev2=c.get(f"g_es_slot{f % 2}", 0),
              hv_prev=c.get(f"g_hv_slot{f % 2}", 0), zl=s_zl.v):
            t.wait_ge(chv.h, max(dq_tgt, X_done, ev_prev2, hv_prev))
            t.wait_ge(chs.h, es_prev2)
            if f == 0:
                t.wait_ge(s_zl.h, zl)
            for mat, bb in ((0, b_g), (1, b_u)):
                for half in range(2):
                    bnk = bb[half]
                    for j in range(NKT):
                        t.matmul(bnk.ap(), ws[:, mat, j, :],
                                 xT[:, j, half * 512:(half + 1) * 512],
                                 start=(j == 0), stop=False)
                    t.matmul(bnk.ap(), negz[:, mat * NFF + f * 128:mat * NFF + (f + 1) * 128],
                             Xs[:, half * 512:(half + 1) * 512], start=False, stop=True)
        c[f"g_mm_slot{f % 2}"] = cht.v
        mm_tgt = cht.v
        hb = hst[f % 2]

        if "g_pend_H" in c:
            c.pop("g_pend_H")()

        if f + 1 < NFT:
            emit_gu_dequant(f + 1)

        @p.on("scalar")
        def _(s, mm_tgt=mm_tgt, b_g=b_g):
            s.wait_ge(cht.h, mm_tgt)
            for half in range(2):
                sl = slice(half * 512, (half + 1) * 512)
                s.activation(out=est[:, sl], in_=b_g[half].ap(), func=AF.Exp, scale=-1.0)
                s.activation(out=est[:, sl], in_=est[:, sl], func=AF.Ln, bias=1.0)
                s.activation(out=sgt[:, sl], in_=est[:, sl], func=AF.Exp, scale=-1.0)
        c[f"g_es_slot{f % 2}"] = chs.v
        es_tgt = chs.v
        shs = s_hs[f % 2]

        @p.on("vector")
        def _(v, es_tgt=es_tgt, b_g=b_g, b_u=b_u, hb=hb, hs_prev=c.get(f"g_hs_slot{f % 2}", 0), shs=shs):
            v.wait_ge(chs.h, es_tgt)
            if hs_prev:
                v.wait_ge(shs.h, hs_prev)
            for half in range(2):
                sl = slice(half * 512, (half + 1) * 512)
                v.tensor_tensor(out=sgt[:, sl], in0=b_g[half].ap(), in1=sgt[:, sl], op=ALU.mult)
                v.tensor_tensor(out=hb[:, sl], in0=sgt[:, sl], in1=b_u[half].ap(), op=ALU.mult)
        c[f"g_ev_slot{f % 2}"] = chv.v
        ev_tgt = chv.v

        @p.on("sync")
        def _(sy, f=f, hb=hb, ev_tgt=ev_tgt, shs=shs):
            sy.wait_ge(chv.h, ev_tgt)
            sy.dma_start(out=T["h_sp"][f], in_=hb[:]).then_inc(shs.h, 16)
        shs.inc(16)
        c[f"g_hs_slot{f % 2}"] = shs.v

        def emit_H(f=f, hb=hb, b_g=b_g, ev_tgt=ev_tgt):
            @p.on("tensor")
            def _(t, hb=hb, b_g=b_g, ev_tgt=ev_tgt):
                t.wait_ge(chv.h, ev_tgt)
                t.matmul(b_g[0].ap()[0:1, 0:512], c.ones1[:], hb[:, 0:512], start=True, stop=True)
                t.matmul(b_g[1].ap()[0:1, 0:512], c.ones1[:], hb[:, 512:1024], start=True, stop=True)
            Hm = cht.v

            @p.on("vector")
            def _(v, b_g=b_g, tgt=Hm, hsp_prev=s_Hsp.v):
                v.wait_ge(cht.h, tgt)
                if hsp_prev:
                    v.wait_ge(s_Hsp.h, hsp_prev)
                v.tensor_copy(out=hrow[0:1, 0:512], in_=b_g[0].ap()[0:1, 0:512])
                v.tensor_copy(out=hrow[0:1, 512:1024], in_=b_g[1].ap()[0:1, 0:512])
            c[f"g_hv_slot{f % 2}"] = chv.v

            @p.on("scalar")
            def _(s, f=f, tgt=chv.v):
                s.wait_ge(chv.h, tgt)
                s.dma_start(out=T["H_sp"][f:f + 1, :], in_=hrow[:]).then_inc(s_Hsp.h, 16)
            s_Hsp.inc(16)
        c["g_pend_H"] = emit_H
    c.pop("g_pend_H")()
    c.h_done = (s_hs[0].h, s_hs[0].v)
    c.h_done2 = (s_hs[1].h, s_hs[1].v)
    c.H_done = (s_Hsp.h, s_Hsp.v)


def emit_down(c):
    nc, T, sb, p, bank = c.nc, c.T, c.sb, c.p, c.bank
    cht, chv, chs = c.cht, c.chv, c.chs
    s_hl = c.sem("s_dhl")
    s_pl = [c.sem("s_dpl0"), c.sem("s_dpl1")]
    s_sl = c.sem("s_dsl")
    s_wl = [c.sem("s_dwl0"), c.sem("s_dwl1")]
    s_sp = c.sem("s_dsp")
    s_wsp = [c.sem("s_dwsp0"), c.sem("s_dwsp1")]
    s_zl = c.sem("s_dzl")
    s_H = c.sem("s_dH")

    hb = sb("dh", [128, NFT, 512], BF16)
    pkb = [sb(f"dpk{i}", [128, NFT, 16], I32) for i in range(2)]
    wd = [sb(f"dwd{i}", [128, NFT, 128], BF16) for i in range(2)]
    srd = sb("dsrd", [128, NFF], BF16)
    negzd = sb("dnegz", [NFT, HID], BF16)
    Hs = sb("dHs", [NFT, 512], BF16)
    dstage = sb("dstage", [128, 512], BF16)

    @p.on("gpsimd")
    def _(g):
        g.dma_start(out=negzd[:], in_=T["zs_sp"][0:NFT, 2 * NFF:2 * NFF + HID]).then_inc(s_zl.h, 16)
    s_zl.inc(16)

    for half in range(2):
        @p.on("sync")
        def _(sy, half=half, mm_prev=cht.v):
            sy.wait_ge(cht.h, mm_prev)
            sy.dma_start(out=hb[:],
                         in_=T["h_sp"].rearrange("f p t -> p f t")[:, :, half * 512:(half + 1) * 512]
                         ).then_inc(s_hl.h, 16)
        s_hl.inc(16)
        hl_tgt = s_hl.v

        @p.on("gpsimd")
        def _(g, half=half, H_used=c.get("d_H_used", 0)):
            g.wait_ge(cht.h, H_used)
            g.dma_start(out=Hs[:], in_=T["H_sp"][:, half * 512:(half + 1) * 512]
                        ).then_inc(s_H.h, 16)
        s_H.inc(16)
        H_done = s_H.v

        def emit_d_dequant(ht):
            pk = pkb[ht % 2]
            wdb = wd[ht % 2]
            dq_prev = c.get(f"d_dq_slot{ht % 2}", 0)
            mm_slot = c.get(f"d_mm_slot{ht % 2}", 0)
            wsp_slot = c.get(f"d_wsp_slot{ht % 2}", 0)
            spl = s_pl[ht % 2]
            swsp = s_wsp[ht % 2]

            @p.on("sync")
            def _(sy, ht=ht, pk=pk, dq_prev=dq_prev, spl=spl):
                sy.wait_ge(chv.h, dq_prev)
                sy.dma_start(out=pk[:], in_=T["dq_t"][ht]).then_inc(spl.h, 16)
            spl.inc(16)
            pl_tgt = spl.v

            @p.on("gpsimd")
            def _(g, ht=ht, dq_all=c.get("d_dq_all", 0), zl=s_zl.v):
                g.wait_ge(chv.h, dq_all)
                if ht == 0:
                    g.wait_ge(s_zl.h, zl)
                g.dma_start(out=srd[:], in_=T["dsr"][ht:ht + 1, :]
                            .broadcast_to([128, NFF])).then_inc(s_sl.h, 16)
            s_sl.inc(16)
            sl_tgt = s_sl.v

            @p.on("vector")
            def _(v, pk=pk, wdb=wdb, pl_tgt=pl_tgt, sl_tgt=sl_tgt,
                  mm_slot=mm_slot, wsp_slot=wsp_slot, spl=spl, swsp=swsp):
                v.wait_ge(spl.h, pl_tgt)
                v.wait_ge(cht.h, mm_slot)
                if wsp_slot:
                    v.wait_ge(swsp.h, wsp_slot)
                z16 = pk[:].bitcast(I16).rearrange("p k (w h) -> p k w h", h=2)
                wdi = wdb[:].rearrange("p k n -> p (k n)").bitcast(I16)
                ov = wdi.rearrange("p (k w e h) -> p k w e h", k=NFT, e=4, h=2)
                for jb in range(4):
                    v.tensor_scalar(out=ov[:, :, :, jb, :], in0=z16, scalar1=4 * jb,
                                    scalar2=15, op0=ALU.logical_shift_right, op1=ALU.bitwise_and)
                v.wait_ge(s_sl.h, sl_tgt)
                v.tensor_tensor(out=wdb[:].rearrange("p k n -> p (k n)"),
                                in0=wdi, in1=srd[:], op=ALU.mult)
            c[f"d_dq_slot{ht % 2}"] = chv.v
            c["d_dq_all"] = chv.v

            @p.on("scalar")
            def _(s, ht=ht, wdb=wdb, tgt=chv.v, swsp=swsp):
                s.wait_ge(chv.h, tgt)
                s.dma_start(out=T["wd_sp"][ht], in_=wdb[:].rearrange("p k n -> p (k n)")
                            ).then_inc(swsp.h, 16)
            swsp.inc(16)
            c[f"d_wsp_slot{ht % 2}"] = swsp.v

        def emit_d_load(ht):
            wdb = wd[ht % 2]
            mm_slot = c.get(f"d_mm_slot{ht % 2}", 0)
            swl = s_wl[ht % 2]

            @p.on("sync")
            def _(sy, ht=ht, wdb=wdb, mm_slot=mm_slot, swl=swl,
                  wsp_fin=s_wsp[ht % 2].v):
                sy.wait_ge(cht.h, mm_slot)
                sy.wait_ge(s_wsp[ht % 2].h, wsp_fin)
                sy.dma_start(out=wdb[:].rearrange("p k n -> p (k n)"),
                             in_=T["wd_sp"][ht]).then_inc(swl.h, 16)
            swl.inc(16)
            c[f"d_wl_slot{ht % 2}"] = swl.v

        if half == 0:
            emit_d_dequant(0)
        else:
            emit_d_load(0)
        for ht in range(NKT):
            wdb = wd[ht % 2]
            if half == 0:
                w_sem, w_tgt = None, c[f"d_dq_slot{ht % 2}"]
            else:
                w_sem, w_tgt = s_wl[ht % 2], c[f"d_wl_slot{ht % 2}"]

            bnk = bank[ht % 2]
            ev_prev = c.get(f"d_ev_slot{ht % 2}", 0)

            @p.on("tensor")
            def _(t, ht=ht, wdb=wdb, bnk=bnk, w_sem=w_sem, w_tgt=w_tgt, H_done=H_done,
                  ev_prev=ev_prev, hl_tgt=hl_tgt):
                t.wait_ge(s_hl.h, hl_tgt)
                if w_sem is None:
                    t.wait_ge(chv.h, w_tgt)
                else:
                    t.wait_ge(w_sem.h, w_tgt)
                t.wait_ge(s_H.h, H_done)
                t.wait_ge(chv.h, ev_prev)
                for kt in range(NFT):
                    t.matmul(bnk.ap(), wdb[:, kt, :], hb[:, kt, :],
                             start=(kt == 0), stop=False)
                t.matmul(bnk.ap(), negzd[:, ht * 128:(ht + 1) * 128], Hs[:],
                         start=False, stop=True)
            c[f"d_mm_slot{ht % 2}"] = cht.v
            mm_tgt = cht.v

            if ht + 1 < NKT:
                if half == 0:
                    emit_d_dequant(ht + 1)
                else:
                    emit_d_load(ht + 1)

            @p.on("vector")
            def _(v, mm_tgt=mm_tgt, bnk=bnk, sp_prev=c.get("d_sp_prev", 0)):
                v.wait_ge(cht.h, mm_tgt)
                if sp_prev:
                    v.wait_ge(s_sp.h, sp_prev)
                v.tensor_copy(out=dstage[:], in_=bnk.ap())
            c[f"d_ev_slot{ht % 2}"] = chv.v

            @p.on("sync")
            def _(sy, half=half, ht=ht, tgt=chv.v):
                sy.wait_ge(chv.h, tgt)
                sy.dma_start(out=T["dn_sp"][half, ht], in_=dstage[:]).then_inc(s_sp.h, 16)
            s_sp.inc(16)
            c["d_sp_prev"] = s_sp.v
        c["d_H_used"] = cht.v
    c.dn_done = (s_sp.h, s_sp.v)


_NC_CACHE = {}


def _prep_inputs(inputs, b, hf):
    x = inputs["L0_attn_hidden_states"]
    wq = np.asarray(inputs["L0_attn_Wq"])
    wk = np.asarray(inputs["L0_attn_Wk"])
    wv = np.asarray(inputs["L0_attn_Wv"])
    wo = np.asarray(inputs["L0_attn_Wo"])
    gq = np.asarray(inputs["L0_ffn_gate_qweight"])
    gsc = np.asarray(inputs["L0_ffn_gate_scales"])
    gz = np.asarray(inputs["L0_ffn_gate_qzeros"])
    uq = np.asarray(inputs["L0_ffn_up_qweight"])
    usc = np.asarray(inputs["L0_ffn_up_scales"])
    uz = np.asarray(inputs["L0_ffn_up_qzeros"])
    dq = np.asarray(inputs["L0_ffn_down_qweight"])
    dsc = np.asarray(inputs["L0_ffn_down_scales"])
    dz = np.asarray(inputs["L0_ffn_down_qzeros"])

    glob = _prep_inputs._cache.get(id(inputs))
    if glob is None:
        wq_il = np.ascontiguousarray(wq[IL_PERM])
        wk_il = np.ascontiguousarray(wk[IL_PERM])
        wv_il = np.ascontiguousarray(wv[IL_PERM])
        gq_r = gq[IL_PERM].reshape(NKT, 128, 1376).transpose(1, 0, 2)
        gq_t = np.ascontiguousarray(
            gq_r.reshape(128, NKT, NFT, 16).transpose(2, 0, 1, 3))
        uq_r = uq[IL_PERM].reshape(NKT, 128, 1376).transpose(1, 0, 2)
        uq_t = np.ascontiguousarray(
            uq_r.reshape(128, NKT, NFT, 16).transpose(2, 0, 1, 3))
        dq_t = np.ascontiguousarray(
            dq.reshape(NFT, 128, 512).reshape(NFT, 128, NKT, 16).transpose(2, 1, 0, 3))
        gsr = np.ascontiguousarray(np.repeat(gsc, 4, axis=0))
        usr = np.ascontiguousarray(np.repeat(usc, 4, axis=0))
        dsr = np.ascontiguousarray(
            dsc.reshape(NFT, NKT, 128).transpose(1, 0, 2).reshape(NKT, NFF))
        glob = dict(wq_il=wq_il, wk_il=wk_il, wv_il=wv_il, wo=np.ascontiguousarray(wo),
                    gq_t=gq_t, uq_t=uq_t, dq_t=dq_t, gsr=gsr, usr=usr, dsr=dsr,
                    gz=np.ascontiguousarray(gz), uz=np.ascontiguousarray(uz),
                    dz=np.ascontiguousarray(dz), gsc=np.ascontiguousarray(gsc),
                    usc=np.ascontiguousarray(usc), dsc=np.ascontiguousarray(dsc))
        _prep_inputs._cache[id(inputs)] = glob

    x_main = np.ascontiguousarray(x[b, hf * M:(hf + 1) * M])
    x_pre = (np.ascontiguousarray(x[b, 0:M]) if hf == 1
             else np.zeros((M, HID), np.float32))
    start = hf * M
    d = dict(glob)
    d.update(
        x_main=x_main, x_pre=x_pre,
        identc=HC["ident"], Uc=HC["U"], maskPTc=HC["maskPT"], G32c=HC["G32"],
        ones1c=HC["ones1"], onesrc=HC["onesr"], maskTn=HC["maskT_norm"],
        maskTf=(HC["maskT_norm"] if hf == 1 else HC["maskT_first0"]),
        cosq=HC["cosq"][start:start + M], sinq=HC["sinq"][start:start + M],
        cosk=HC["cos"][start:start + M], sink=HC["sin"][start:start + M],
        coskh=HC["cos"][M - 64:M], sinkh=HC["sin"][M - 64:M],
    )
    return d


_prep_inputs._cache = {}


def kernel(**inputs):
    if "nc" not in _NC_CACHE:
        _NC_CACHE["nc"] = build_program("full")
    nc = _NC_CACHE["nc"]
    maps = []
    for core in range(8):
        b, hf = core // 2, core % 2
        maps.append(_prep_inputs(inputs, b, hf))
    _prep_inputs._cache.clear()
    res = run_bass_kernel_spmd(nc, maps, core_ids=list(range(8)))
    out = np.empty((B, N, HID), np.float32)
    for core in range(8):
        b, hf = core // 2, core % 2
        out[b, hf * M:(hf + 1) * M] = res.results[core]["out"]
    return out


# version 08 (final)
# speedup vs baseline: 2.4527x; 1.0653x over previous
import numpy as np
import ml_dtypes
from contextlib import ExitStack
import concourse.bass as bass
import concourse.mybir as mybir
from concourse.bass_utils import run_bass_kernel_spmd

F32 = mybir.dt.float32
BF16 = mybir.dt.bfloat16
I32 = mybir.dt.int32
I16 = mybir.dt.int16
ALU = mybir.AluOpType
AF = mybir.ActivationFunctionType
AX = mybir.AxisListType

B, N, HID = 4, 2048, 4096
H, KVH, D = 32, 8, 128
M = 1024
NTT = 8
NFF = 11008
NFT = 86
NKT = 32
GN = 16.0
THETA = 10000.0
EPS = 1e-5
NEG = -30000.0

ENGS = ("sync", "tensor", "vector", "scalar", "gpsimd")


def _il_perm():
    perm = np.empty(HID, dtype=np.int64)
    for j in range(NKT):
        for pp in range(128):
            perm[j * 128 + pp] = (pp // 4) * 128 + (pp % 4) * 32 + j
    return perm


IL_PERM = _il_perm()


def host_consts():
    ident = np.eye(128, dtype=np.float32)
    U = (np.triu(np.ones((128, 128))) * (-1.0 / GN)).astype(np.float32)
    maskPT = np.triu(np.ones((128, 128))).astype(np.float32)
    G32 = np.zeros((128, 32), np.float32)
    for pp in range(128):
        G32[pp, pp // 4] = 1.0
    ones1 = np.ones((128, 1), np.float32)
    onesr = np.ones((1, 128), np.float32)
    s_ = np.arange(128)[:, None]
    t_ = np.arange(64)[None, :]
    mvalid = (s_ >= t_) & (s_ <= t_ + 64)
    mT = np.where(mvalid, 0.0, NEG).astype(np.float32)
    maskT_norm = np.tile(mT, (1, 8))
    mA0 = np.where(mvalid & (s_ >= 64), 0.0, NEG).astype(np.float32)
    maskT_first0 = np.tile(np.concatenate([mA0, mT], axis=1), (1, 4))
    half = np.arange(D // 2, dtype=np.float64)
    inv = 1.0 / THETA ** (2.0 * half / D)
    pos = np.arange(N, dtype=np.float64)
    fr = pos[:, None] * inv[None, :]
    cos = np.cos(fr)
    sin = np.sin(fr)
    sc = 1.0 / np.sqrt(np.float64(D))
    return dict(ident=ident, U=U, maskPT=maskPT, G32=G32, ones1=ones1, onesr=onesr,
                maskT_norm=maskT_norm, maskT_first0=maskT_first0,
                cos=cos.astype(np.float32), sin=sin.astype(np.float32),
                cosq=(cos * sc).astype(np.float32), sinq=(sin * sc).astype(np.float32))


HC = host_consts()


class Sem:
    def __init__(self, handle):
        self.h = handle
        self.v = 0

    def inc(self, n=1):
        self.v += n
        return self.v


DATA_OPS = {"tensor_tensor", "tensor_scalar", "tensor_copy", "tensor_reduce",
            "reciprocal", "memset", "activation", "matmul", "transpose",
            "ldweights", "iota", "bn_stats", "bn_aggr"}
CHAINED = ("tensor", "vector", "scalar")


class RecIns:
    def __init__(self, auto=None):
        self.posts = []
        if auto is not None:
            self.posts.append(auto)
        self.auto = auto

    def then_inc(self, sem, n=1):
        assert self.auto is None, "explicit then_inc on chained compute op"
        self.posts.append((sem, n))
        return self


class Recorder:
    def __init__(self, p, eng):
        self.p = p
        self.eng = eng

    def wait_ge(self, sem, val):
        if val:
            self.p.q[self.eng].append(("w", sem, val))

    def nop(self):
        r = RecIns()
        self.p.q[self.eng].append(("nop", r))
        return r

    def dma_start(self, **kw):
        r = RecIns()
        self.p.q[self.eng].append(("dma", kw, r))
        return r

    def __getattr__(self, name):
        def g(*a, **k):
            ch = self.p.chain.get(self.eng) if name in DATA_OPS else None
            pre = ch.v if (ch is not None and ch.v and self.eng != "tensor") else None
            r = RecIns(auto=(ch.h, 1) if ch is not None else None)
            self.p.q[self.eng].append(("op", name, a, k, r, pre, ch))
            if ch is not None:
                ch.v += 1
            return r
        return g


class Prog:
    def __init__(self):
        self.q = {e: [] for e in ENGS}
        self.chain = {}

    def on(self, eng):
        rec = Recorder(self, eng)

        def deco(fn):
            fn(rec)
            return None
        return deco

    def replay(self, blk):
        for e in ENGS:
            recs = self.q[e]

            def section(h, recs=recs):
                for r in recs:
                    kind = r[0]
                    if kind == "w":
                        h.wait_ge(r[1], r[2])
                    elif kind == "nop":
                        ins = h.nop()
                        for sem, n in r[1].posts:
                            ins.then_inc(sem, n)
                    elif kind == "dma":
                        ins = h.dma_start(**r[1])
                        for sem, n in r[2].posts:
                            ins.then_inc(sem, n)
                    else:
                        _, name, a, k, ri, pre, ch = r
                        if pre is not None:
                            h.wait_ge(ch.h, pre)
                        ins = getattr(h, name)(*a, **k)
                        for sem, n in ri.posts:
                            ins.then_inc(sem, n)
            getattr(blk, e)(section)


class Ctx(dict):
    __getattr__ = dict.__getitem__
    __setattr__ = dict.__setitem__


def fence(c, waits=()):
    snap = {e: c.p.chain[e].v for e in CHAINED}
    for e in ENGS:
        @c.p.on(e)
        def _(h, e=e):
            for e2 in CHAINED:
                if e2 != e and snap[e2]:
                    h.wait_ge(c.p.chain[e2].h, snap[e2])
            for sem, val in waits:
                if val:
                    h.wait_ge(sem, val)


def build_program(mode="full"):
    nc = bass.Bass("TRN2", target_bir_lowering=False)
    T = {}

    def dram(name, shape, dty, kind="ExternalInput"):
        T[name] = nc.dram_tensor(name, shape, dty, kind=kind).ap()
        return T[name]

    dram("x_main", [M, HID], F32)
    dram("x_pre", [M, HID], F32)
    dram("wq_il", [HID, H * D], F32)
    dram("wk_il", [HID, KVH * D], F32)
    dram("wv_il", [HID, KVH * D], F32)
    dram("wo", [H * D, HID], F32)
    dram("gq_t", [NFT, 128, NKT, 16], I32)
    dram("uq_t", [NFT, 128, NKT, 16], I32)
    dram("dq_t", [NKT, 128, NFT, 16], I32)
    dram("gsr", [128, NFF], F32)
    dram("usr", [128, NFF], F32)
    dram("dsr", [NKT, NFF], F32)
    dram("gz", [32, NFF // 8], I32)
    dram("uz", [32, NFF // 8], I32)
    dram("dz", [NFT, 512], I32)
    dram("gsc", [32, NFF], F32)
    dram("usc", [32, NFF], F32)
    dram("dsc", [NFT, HID], F32)
    dram("identc", [128, 128], F32)
    dram("Uc", [128, 128], F32)
    dram("maskPTc", [128, 128], F32)
    dram("G32c", [128, 32], F32)
    dram("ones1c", [128, 1], F32)
    dram("onesrc", [1, 128], F32)
    dram("maskTn", [128, 512], F32)
    dram("maskTf", [128, 512], F32)
    dram("cosq", [M, 64], F32)
    dram("sinq", [M, 64], F32)
    dram("cosk", [M, 64], F32)
    dram("sink", [M, 64], F32)
    dram("coskh", [64, 64], F32)
    dram("sinkh", [64, 64], F32)
    dram("out", [M, HID], F32, kind="ExternalOutput")
    if mode == "ffn":
        dram("x2in", [M, HID], F32)
    dram("x1_sp", [M, HID], F32, kind="Internal")
    dram("h_sp", [NFT, 128, M], BF16, kind="Internal")
    dram("wd_sp", [NKT, 128, NFF], BF16, kind="Internal")
    dram("zs_sp", [128, 2 * NFF + HID], BF16, kind="Internal")
    dram("qs_sp", [NTT, 128, H * D], BF16, kind="Internal")
    dram("sq_sp", [NTT, 128, H * D], BF16, kind="Internal")
    dram("ks_sp", [16, 128, KVH * D], BF16, kind="Internal")
    dram("sk_sp", [NTT, 128, KVH * D], BF16, kind="Internal")
    dram("skh_sp", [64, KVH * D], BF16, kind="Internal")
    dram("gs_sp", [16, 128, KVH * D], F32, kind="Internal")
    dram("v_sp", [16, 128, KVH * D], BF16, kind="Internal")
    dram("dn_sp", [2, NKT, 128, 512], BF16, kind="Internal")
    dram("wo_sp", [2, NKT, 128, 512], BF16, kind="Internal")
    dram("H_sp", [NFT, M], BF16, kind="Internal")

    p = Prog()
    c = Ctx(nc=nc, T=T, p=p, mode=mode)
    es = ExitStack()
    with es:
        def sem(name):
            return Sem(es.enter_context(nc.semaphore(name)))
        c.sem = sem
        p.chain = {"tensor": sem("ch_t"), "vector": sem("ch_v"), "scalar": sem("ch_s")}
        c.cht = p.chain["tensor"]
        c.chv = p.chain["vector"]
        c.chs = p.chain["scalar"]
        c.bank = [es.enter_context(nc.psum_tensor(f"bank{i}", [128, 512], F32)) for i in range(8)]

        def mksb(stack):
            def sb(name, shape, dty):
                return stack.enter_context(nc.sbuf_tensor(name, shape, dty))
            return sb

        sb0 = mksb(es)
        c.ident = sb0("ident", [128, 128], BF16)
        c.identf = sb0("identf", [128, 128], F32)
        c.Uc = sb0("Uc_sb", [128, 128], F32)
        c.maskPT = sb0("maskPT", [128, 128], BF16)
        c.G32 = sb0("G32", [128, 32], BF16)
        c.ones1 = sb0("ones1", [128, 1], BF16)
        c.onesr = sb0("onesr", [1, 128], F32)
        c.maskTn = sb0("maskTn_sb", [128, 512], F32)
        c.maskTf = sb0("maskTf_sb", [128, 512], F32)
        emit_consts(c)

        with ExitStack() as s_p0:
            c.sb = mksb(s_p0)
            emit_zeros_prep(c)
        fence(c, waits=[c.zs_done])

        stk_xT1 = ExitStack()
        if mode == "ffn":
            with ExitStack() as s_x2:
                c.sb = mksb(s_x2)
                c.xT_ln1 = stk_xT1.enter_context(nc.sbuf_tensor("xT_ln1", [128, NKT, M], BF16))
                emit_x2_load(c)
            fence(c, waits=[c.x1_done])
        else:
            with ExitStack() as s_pa:
                c.sb = mksb(s_pa)
                emit_proj(c)
            fence(c, waits=[c.pa_sp, c.pa_spr, c.pa_spg])
            stk_oT = ExitStack()
            c.oT = stk_oT.enter_context(nc.sbuf_tensor("oT", [128, H, M], BF16))
            with ExitStack() as s_pb:
                c.sb = mksb(s_pb)
                emit_mixer(c)
            fence(c)
            with ExitStack() as s_pc:
                c.sb = mksb(s_pc)
                emit_wo(c)
            fence(c, waits=[c.wo_sp_done])
            stk_oT.close()
            c.xT_ln1 = stk_xT1.enter_context(nc.sbuf_tensor("xT_ln1", [128, NKT, M], BF16))
            with ExitStack() as s_pc2:
                c.sb = mksb(s_pc2)
                emit_postln(c, "ln1")
            fence(c, waits=[c.x1_done])

        if mode == "attn":
            emit_attn_out(c)
            stk_xT1.close()
        else:
            with ExitStack() as s_p4:
                c.sb = mksb(s_p4)
                emit_gate_up(c)
            fence(c, waits=[c.h_done, c.h_done2, c.H_done])
            stk_xT1.close()
            with ExitStack() as s_p5:
                c.sb = mksb(s_p5)
                emit_down(c)
            fence(c, waits=[c.dn_done])
            with ExitStack() as s_p6:
                c.sb = mksb(s_p6)
                emit_postln(c, "ln2")

        blk = es.enter_context(nc.Block())
        p.replay(blk)
    return nc


def emit_consts(c):
    T, p = c.T, c.p
    s_cg = c.sem("s_cg")
    s_cy = c.sem("s_cy")

    @p.on("gpsimd")
    def _(g):
        g.dma_start(out=c.ident[:], in_=T["identc"]).then_inc(s_cg.h, 16)
        g.dma_start(out=c.maskPT[:], in_=T["maskPTc"]).then_inc(s_cg.h, 16)
        g.dma_start(out=c.G32[:], in_=T["G32c"]).then_inc(s_cg.h, 16)
        g.dma_start(out=c.ones1[:], in_=T["ones1c"]).then_inc(s_cg.h, 16)
    s_cg.inc(64)

    @p.on("sync")
    def _(sy):
        sy.dma_start(out=c.Uc[:], in_=T["Uc"]).then_inc(s_cy.h, 16)
        sy.dma_start(out=c.identf[:], in_=T["identc"]).then_inc(s_cy.h, 16)
        sy.dma_start(out=c.onesr[:], in_=T["onesrc"]).then_inc(s_cy.h, 16)
        sy.dma_start(out=c.maskTn[:], in_=T["maskTn"]).then_inc(s_cy.h, 16)
        sy.dma_start(out=c.maskTf[:], in_=T["maskTf"]).then_inc(s_cy.h, 16)
    s_cy.inc(80)
    c.s_cg = s_cg
    c.s_cy = s_cy


def wait_consts(h, c):
    h.wait_ge(c.s_cg.h, c.s_cg.v)
    h.wait_ge(c.s_cy.h, c.s_cy.v)


def emit_zeros_prep(c):
    T, sb, p = c.T, c.sb, c.p
    chv = c.chv
    s_z = c.sem("s_z")
    s_zs = c.sem("s_zs")
    zq = sb("zq", [NFT, NFF // 8], I32)
    zsc = sb("zsc", [NFT, NFF], F32)
    znib = sb("znib", [NFT, NFF], I16)
    zout = sb("zout", [NFT, NFF], BF16)

    off = 0
    for zn, scn, rows, cols in (("gz", "gsc", 32, NFF), ("uz", "usc", 32, NFF),
                                ("dz", "dsc", NFT, HID)):
        @p.on("sync")
        def _(sy, zn=zn, scn=scn, rows=rows, cols=cols, prev=c.get("z_used", 0),
              prev_sp=c.get("z_spv", 0)):
            sy.wait_ge(chv.h, prev)
            if prev_sp:
                sy.wait_ge(s_zs.h, prev_sp)
            sy.dma_start(out=zq[0:rows, 0:cols // 8], in_=T[zn]).then_inc(s_z.h, 16)
            sy.dma_start(out=zsc[0:rows, 0:cols], in_=T[scn]).then_inc(s_z.h, 16)
        s_z.inc(32)
        tgt = s_z.v

        @p.on("vector")
        def _(v, rows=rows, cols=cols, tgt=tgt):
            v.wait_ge(s_z.h, tgt)
            z16 = zq[0:rows, 0:cols // 8].bitcast(I16).rearrange("r (w h) -> r w h", h=2)
            ov = znib[0:rows, 0:cols].rearrange("r (w e h) -> r w e h", e=4, h=2)
            for jb in range(4):
                v.tensor_scalar(out=ov[:, :, jb, :], in0=z16, scalar1=4 * jb, scalar2=15,
                                op0=ALU.logical_shift_right, op1=ALU.bitwise_and)
            v.tensor_scalar(out=zsc[0:rows, 0:cols], in0=zsc[0:rows, 0:cols],
                            scalar1=-1.0, scalar2=0.0, op0=ALU.mult, op1=ALU.add)
            v.tensor_tensor(out=zout[0:rows, 0:cols], in0=znib[0:rows, 0:cols],
                            in1=zsc[0:rows, 0:cols], op=ALU.mult)
        c["z_used"] = chv.v
        tgv = chv.v

        @p.on("sync")
        def _(sy, rows=rows, cols=cols, off=off, tgv=tgv):
            sy.wait_ge(chv.h, tgv)
            sy.dma_start(out=T["zs_sp"][0:rows, off:off + cols],
                         in_=zout[0:rows, 0:cols]).then_inc(s_zs.h, 16)
        s_zs.inc(16)
        c["z_spv"] = s_zs.v
        off += cols
    c.zs_done = (s_zs.h, s_zs.v)


def emit_proj(c):
    nc, T, sb, p, bank = c.nc, c.T, c.sb, c.p, c.bank
    cht, chv, chs = c.cht, c.chv, c.chs
    s_xl = [c.sem("s_xl0"), c.sem("s_xl1")]
    s_wl = [c.sem("s_wl0"), c.sem("s_wl1")]
    s_sp = c.sem("s_sp")
    s_spr = c.sem("s_spr")
    s_spg = c.sem("s_spg")
    s_cl = c.sem("s_cl")

    xbuf = [sb(f"xbuf{i}", [128, HID], F32) for i in range(2)]
    xT = sb("xT", [128, NKT, M], BF16)
    wsl = [sb(f"wsl{i}", [128, NKT, 512], BF16) for i in range(2)]
    expb = sb("expb", [128, 512], F32)
    sums = sb("sums", [128, 4], F32)
    recs = sb("recs", [128, 4], F32)
    pstage = sb("pstage", [128, 512], BF16)
    rstage = sb("rstage", [128, 512], BF16)
    gstage = sb("gstage", [128, 512], F32)
    ta = sb("ta", [128, 256], F32)
    tb = sb("tb", [128, 256], F32)
    cosb = sb("cosb", [128, 4, NTT, 64], F32)
    cosh = sb("cosh", [64, 128], F32)
    Bf = dict(expb=expb, sums=sums, recs=recs, pstage=pstage, rstage=rstage,
              gstage=gstage, ta=ta, tb=tb, cosb=cosb, cosh=cosh)
    Sm = dict(s_sp=s_sp, s_spr=s_spr, s_spg=s_spg, s_cl=s_cl)

    @p.on("scalar")
    def _(s):
        for i, nm in enumerate(("cosq", "sinq", "cosk", "sink")):
            s.dma_start(out=cosb[:, i],
                        in_=T[nm].rearrange("(tt p) f -> p tt f", p=128)).then_inc(s_cl.h, 16)
        s.dma_start(out=cosh[:, 0:64], in_=T["coskh"]).then_inc(s_cl.h, 16)
        s.dma_start(out=cosh[:, 64:128], in_=T["sinkh"]).then_inc(s_cl.h, 16)
    s_cl.inc(96)

    for phase in ("pre", "main"):
        xsrc = T["x_pre"] if phase == "pre" else T["x_main"]
        for tt in range(NTT):
            buf = xbuf[tt % 2]
            sxl = s_xl[tt % 2]

            @p.on("sync")
            def _(sy, tt=tt, buf=buf, prev_ev=c.get(f"pa_xtc_{tt - 2}", 0),
                  mmv=c.get("pa_mm_all", 0), sxl=sxl):
                sy.wait_ge(chv.h, prev_ev)
                sy.wait_ge(cht.h, mmv)
                sy.dma_start(out=buf[:], in_=xsrc[tt * 128:(tt + 1) * 128, :]).then_inc(sxl.h, 16)
            sxl.inc(16)
            xl_tgt = sxl.v
            v0 = chv.v

            @p.on("tensor")
            def _(t, buf=buf, xl_tgt=xl_tgt, v0=v0, first=(phase == "pre" and tt == 0), sxl=sxl):
                t.wait_ge(sxl.h, xl_tgt)
                if first:
                    wait_consts(t, c)
                for j in range(NKT):
                    if v0 + j - 1 > 0:
                        t.wait_ge(chv.h, v0 + j - 1)
                    xv = buf[:].rearrange("p (a b e) -> p a b e", a=32, b=4, e=32)[:, :, :, j]
                    pv = bank[j % 2].ap()
                    t.transpose(pv[:, 0:128], xv, c.identf[:])
            pe0 = cht.v - NKT

            @p.on("vector")
            def _(v, tt=tt, pe0=pe0):
                for j in range(NKT):
                    v.wait_ge(cht.h, pe0 + j + 1)
                    pv = bank[j % 2].ap()
                    v.tensor_copy(out=xT[:, j, tt * 128:(tt + 1) * 128], in_=pv[:, 0:128])
            c[f"pa_xtc_{tt}"] = chv.v
        xT_done = chv.v

        if phase == "main":
            groups = ([("q", fg) for fg in range(8)] +
                      [("k", 0), ("k", 1), ("v", 0), ("v", 1)])
        else:
            groups = [("k", 0), ("k", 1), ("v", 0), ("v", 1)]

        for gi, (kind, fg) in enumerate(groups):
            wslb = wsl[gi % 2]
            wsrc = {"q": "wq_il", "k": "wk_il", "v": "wv_il"}[kind]
            col0 = fg * 512

            swl = s_wl[gi % 2]

            @p.on("gpsimd")
            def _(g, wsrc=wsrc, col0=col0, wslb=wslb,
                  prev_mm=c.get(f"pa_mm_slot{gi % 2}", 0), xT_done=xT_done, swl=swl):
                g.wait_ge(cht.h, max(prev_mm, 0))
                g.wait_ge(chv.h, xT_done)
                g.dma_start(out=wslb[:],
                            in_=T[wsrc].rearrange("(j p) f -> p j f", p=128)[:, :, col0:col0 + 512]
                            ).then_inc(swl.h, 16)
            swl.inc(16)
            wl_tgt = swl.v

            for tt in range(NTT):
                bnk = bank[tt]
                ev_prev = c.get(f"pa_ev_{tt}", (0, 0))

                @p.on("tensor")
                def _(t, wslb=wslb, wl_tgt=wl_tgt, bnk=bnk, tt=tt,
                      ev_prev=ev_prev, xT_done=xT_done, swl=swl):
                    t.wait_ge(swl.h, wl_tgt)
                    t.wait_ge(chv.h, max(xT_done, ev_prev[0]))
                    if ev_prev[1]:
                        t.wait_ge(chs.h, ev_prev[1])
                    for j in range(NKT):
                        t.matmul(bnk.ap(), xT[:, j, tt * 128:(tt + 1) * 128],
                                 wslb[:, j, :], start=(j == 0), stop=(j == NKT - 1))
                mm_tgt = cht.v
                emit_proj_evac(c, phase, kind, fg, tt, bnk, mm_tgt, Bf, Sm)
                c[f"pa_ev_{tt}"] = (chv.v, chs.v)
            c[f"pa_mm_slot{gi % 2}"] = cht.v
        c["pa_mm_all"] = cht.v
    c.pa_sp = (s_sp.h, s_sp.v)
    c.pa_spr = (s_spr.h, s_spr.v)
    c.pa_spg = (s_spg.h, s_spg.v)


def emit_proj_evac(c, phase, kind, fg, tt, bnk, mm_tgt, Bf, Sm):
    T, p = c.T, c.p
    cht, chv, chs = c.cht, c.chv, c.chs
    s_sp, s_spr, s_spg, s_cl = Sm["s_sp"], Sm["s_spr"], Sm["s_spg"], Sm["s_cl"]
    expb, sums, recs = Bf["expb"], Bf["sums"], Bf["recs"]
    pstage, rstage, gstage = Bf["pstage"], Bf["rstage"], Bf["gstage"]
    ta, tb, cosb, cosh = Bf["ta"], Bf["tb"], Bf["cosb"], Bf["cosh"]
    main = (phase == "main")
    it = (NTT if main else 0) + tt

    if kind == "v":
        @p.on("scalar")
        def _(s, bnk=bnk, mm_tgt=mm_tgt, sp0=s_sp.v, vprev=chv.v):
            s.wait_ge(cht.h, mm_tgt)
            s.wait_ge(chv.h, vprev)
            if sp0:
                s.wait_ge(s_sp.h, sp0)
            s.activation(out=pstage[:], in_=bnk.ap(), func=AF.Identity)
        es_tgt = chs.v

        @p.on("sync")
        def _(sy, it=it, fg=fg, es_tgt=es_tgt):
            sy.wait_ge(chs.h, es_tgt)
            sy.dma_start(out=T["v_sp"][it, :, fg * 512:(fg + 1) * 512],
                         in_=pstage[:]).then_inc(s_sp.h, 16)
        s_sp.inc(16)
        return

    rope_this = main or (phase == "pre" and tt == 7 and kind == "k")
    cl_tgt = s_cl.v

    @p.on("scalar")
    def _(s, bnk=bnk, mm_tgt=mm_tgt, kind=kind, spg_pre=s_spg.v, vprev=chv.v):
        s.wait_ge(cht.h, mm_tgt)
        s.wait_ge(chv.h, vprev)
        if kind == "k" and spg_pre:
            s.wait_ge(s_spg.h, spg_pre)
        s.activation(out=expb[:], in_=bnk.ap(), func=AF.Exp)
        if kind == "k":
            s.activation(out=gstage[:], in_=bnk.ap(), func=AF.Exp, scale=-1.0)
            s.activation(out=gstage[:], in_=gstage[:], func=AF.Ln, bias=1.0)
    es_exp = chs.v

    @p.on("vector")
    def _(v, bnk=bnk, kind=kind, es_exp=es_exp, sp_pre=s_sp.v, spr_pre=s_spr.v,
          rope_this=rope_this, cl_tgt=cl_tgt, main=main):
        v.wait_ge(chs.h, es_exp)
        if sp_pre:
            v.wait_ge(s_sp.h, sp_pre)
        if rope_this and spr_pre:
            v.wait_ge(s_spr.h, spr_pre)
        v.tensor_reduce(out=sums[:].unsqueeze(2),
                        in_=expb[:].rearrange("p (h n) -> p h n", h=4),
                        axis=AX.X, op=ALU.add)
        if kind == "q":
            v.tensor_scalar(out=sums[:], in0=sums[:], scalar1=2.0, scalar2=0.0,
                            op0=ALU.mult, op1=ALU.add)
        v.reciprocal(out=recs[:], in_=sums[:])
        for hh in range(4):
            v.tensor_scalar(out=pstage[:, hh * 128:(hh + 1) * 128],
                            in0=expb[:, hh * 128:(hh + 1) * 128],
                            scalar1=recs[:, hh:hh + 1], scalar2=0.0,
                            op0=ALU.mult, op1=ALU.add)
        if rope_this:
            v.wait_ge(s_cl.h, cl_tgt)
            rows = slice(0, 128) if main else slice(64, 128)
            if main:
                ci = 0 if kind == "q" else 2
                cv = cosb[:, ci, tt, :]
                sv = cosb[:, ci + 1, tt, :]
            else:
                cv = cosh[:, 0:64]
                sv = cosh[:, 64:128]
            cv = cv[rows, :] if main else cv
            sv = sv[rows, :] if main else sv
            pv = bnk.ap()
            nr = 128 if main else 64
            pv3 = pv[rows, :].rearrange("p (h e n) -> p h e n", h=4, e=2)
            rs3 = rstage[rows, :].rearrange("p (h e n) -> p h e n", h=4, e=2)
            ta3 = ta[rows, :].rearrange("p (h n) -> p h n", h=4)
            tb3 = tb[rows, :].rearrange("p (h n) -> p h n", h=4)
            cvb = cv.unsqueeze(1).broadcast_to([nr, 4, 64])
            svb = sv.unsqueeze(1).broadcast_to([nr, 4, 64])
            v.tensor_tensor(out=ta3, in0=pv3[:, :, 0, :], in1=cvb, op=ALU.mult)
            v.tensor_tensor(out=tb3, in0=pv3[:, :, 1, :], in1=svb, op=ALU.mult)
            v.tensor_tensor(out=rs3[:, :, 0, :], in0=ta3, in1=tb3, op=ALU.subtract)
            v.tensor_tensor(out=ta3, in0=pv3[:, :, 1, :], in1=cvb, op=ALU.mult)
            v.tensor_tensor(out=tb3, in0=pv3[:, :, 0, :], in1=svb, op=ALU.mult)
            v.tensor_tensor(out=rs3[:, :, 1, :], in0=ta3, in1=tb3, op=ALU.add)
    ev_done = chv.v

    @p.on("sync")
    def _(sy, kind=kind, fg=fg, it=it, tt=tt, ev_done=ev_done, rope_this=rope_this,
          main=main):
        sy.wait_ge(chv.h, ev_done)
        cols = slice(fg * 512, (fg + 1) * 512)
        if kind == "q":
            sy.dma_start(out=T["qs_sp"][tt, :, cols], in_=pstage[:]).then_inc(s_sp.h, 16)
            sy.dma_start(out=T["sq_sp"][tt, :, cols], in_=rstage[:]).then_inc(s_spr.h, 16)
        else:
            sy.dma_start(out=T["ks_sp"][it, :, cols], in_=pstage[:]).then_inc(s_sp.h, 16)
            sy.dma_start(out=T["gs_sp"][it, :, cols], in_=gstage[:]).then_inc(s_spg.h, 16)
            if main:
                sy.dma_start(out=T["sk_sp"][tt, :, cols], in_=rstage[:]).then_inc(s_spr.h, 16)
            elif rope_this:
                sy.dma_start(out=T["skh_sp"][:, cols], in_=rstage[64:128, :]).then_inc(s_spr.h, 16)
    s_sp.inc(16)
    if kind == "q" or (kind == "k" and (main or rope_this)):
        s_spr.inc(16)
    if kind == "k":
        s_spg.inc(16)


def emit_mixer(c):
    nc, T, sb, p, bank = c.nc, c.T, c.sb, c.p, c.bank
    cht, chv, chs = c.cht, c.chv, c.chs
    s_l = c.sem("s_bl")

    S = sb("S", [128, KVH * 128], F32)
    Sb = sb("Sb", [128, KVH * 128], BF16)
    skT = sb("skT", [128, KVH, 64 + M], BF16)
    oT = c.oT
    qs = sb("qs", [128, H * D], BF16)
    sq = sb("sq", [128, H * D], BF16)
    ks = sb("ks", [128, KVH * D], BF16)
    sk = sb("sk", [128, KVH * D], BF16)
    gs = sb("gs", [128, KVH * D], F32)
    vt = [sb(f"vt{i}", [128, KVH * D], BF16) for i in range(2)]
    qsT = sb("qsT", [128, H, 128], BF16)
    sqT = sb("sqT", [128, H, 128], BF16)
    koT = sb("koT", [128, 128], BF16)
    ko = sb("ko", [128, 128], BF16)
    eGT = sb("eGT", [128, 128], F32)
    enG = sb("enG", [128, 128], F32)
    qtT = sb("qtT", [128, 4, 128], BF16)
    PTg = sb("PTg", [128, 512], BF16)
    sws = sb("sws", [128, 512], F32)
    swe = sb("swe", [128, 512], BF16)
    swz = sb("swz", [1, 512], F32)
    PTs = sb("PTs", [128, 512], BF16)
    vwin = sb("vwin", [128, KVH * D], BF16)
    s_vw = c.sem("s_vw")
    c._s_vw = s_vw
    c._vwin = vwin

    @p.on("vector")
    def _(v):
        v.memset(S[:], 0.0)
        v.memset(Sb[:], 0.0)

    Bf = dict(S=S, Sb=Sb, skT=skT, oT=oT, qs=qs, sq=sq, ks=ks, sk=sk, gs=gs,
              qsT=qsT, sqT=sqT, koT=koT, ko=ko, eGT=eGT, enG=enG, qtT=qtT,
              PTg=PTg, sws=sws, swe=swe, swz=swz, PTs=PTs)

    for it in range(16):
        main = it >= NTT
        tt = it - NTT if main else it
        vtc = vt[it % 2]

        @p.on("sync")
        def _(sy, it=it, tt=tt, main=main, vtc=vtc, done=c.get("mix_done", 0)):
            sy.wait_ge(chv.h, done)
            sy.wait_ge(cht.h, c.get("mix_done_t", 0))
            sy.dma_start(out=ks[:], in_=T["ks_sp"][it]).then_inc(s_l.h, 16)
            sy.dma_start(out=gs[:], in_=T["gs_sp"][it]).then_inc(s_l.h, 16)
            sy.dma_start(out=vtc[:], in_=T["v_sp"][it]).then_inc(s_l.h, 16)
            if main:
                sy.dma_start(out=qs[:], in_=T["qs_sp"][tt]).then_inc(s_l.h, 16)
                sy.dma_start(out=sq[:], in_=T["sq_sp"][tt]).then_inc(s_l.h, 16)
                sy.dma_start(out=sk[:], in_=T["sk_sp"][tt]).then_inc(s_l.h, 16)
        s_l.inc(48 + (48 if main else 0))
        ld_tgt = s_l.v

        if main:
            vwin = c._vwin
            s_vw = c._s_vw

            @p.on("sync")
            def _(sy, ld_tgt=ld_tgt, vtc=vtc, vtp=vt[(it + 1) % 2]):
                sy.wait_ge(s_l.h, ld_tgt)
                sy.dma_start(out=vwin[0:64, :], in_=vtp[64:128, :]).then_inc(s_vw.h, 16)
                sy.dma_start(out=vwin[64:128, :], in_=vtc[0:64, :]).then_inc(s_vw.h, 16)
            s_vw.inc(32)
            c["vw_tgt"] = s_vw.v

            for grp, (srcb, dstb, nseg) in enumerate(((qs, qsT, H), (sq, sqT, H), (sk, None, KVH))):
                v0 = chv.v

                @p.on("tensor")
                def _(t, srcb=srcb, nseg=nseg, ld_tgt=ld_tgt, v0=v0, grp=grp):
                    if grp == 0:
                        t.wait_ge(s_l.h, ld_tgt)
                    for sg in range(nseg):
                        if v0 + sg - 1 > 0:
                            t.wait_ge(chv.h, v0 + sg - 1)
                        pv = bank[4 + sg % 2].ap().bitcast(BF16)
                        t.transpose(pv[:, 0:128], srcb[:, sg * 128:(sg + 1) * 128],
                                    c.ident[:])
                pe0 = cht.v - nseg

                @p.on("vector")
                def _(v, dstb=dstb, nseg=nseg, pe0=pe0, tt=tt, grp=grp):
                    for sg in range(nseg):
                        v.wait_ge(cht.h, pe0 + sg + 1)
                        pv = bank[4 + sg % 2].ap().bitcast(BF16)
                        if grp < 2:
                            v.tensor_copy(out=dstb[:, sg, :], in_=pv[:, 0:128])
                        else:
                            v.tensor_copy(out=skT[:, sg, 64 + tt * 128:64 + (tt + 1) * 128],
                                          in_=pv[:, 0:128])

        for kv in range(KVH):
            emit_chunk(c, it, tt, main, kv, Bf, vtc, vt[(it + 1) % 2], s_l, ld_tgt)
        c["mix_done"] = chv.v
        c["mix_done_t"] = cht.v


def emit_chunk(c, it, tt, main, kv, Bf, vtc, vtp, s_l, ld_tgt):
    p, bank = c.p, c.bank
    cht, chv, chs = c.cht, c.chv, c.chs
    S, Sb, skT, oT = Bf["S"], Bf["Sb"], Bf["skT"], Bf["oT"]
    qs, sq, ks, sk, gs = Bf["qs"], Bf["sq"], Bf["ks"], Bf["sk"], Bf["gs"]
    qsT, sqT, koT, ko = Bf["qsT"], Bf["sqT"], Bf["koT"], Bf["ko"]
    eGT, enG, qtT = Bf["eGT"], Bf["enG"], Bf["qtT"]
    PTg, sws, swe, swz, PTs = Bf["PTg"], Bf["sws"], Bf["swe"], Bf["swz"], Bf["PTs"]
    ksg = slice(kv * 128, (kv + 1) * 128)
    Sseg = slice(kv * 128, (kv + 1) * 128)

    bG = bank[6]
    bO = bank[2 if kv % 2 == 0 else 3]
    bW = bank[7]
    bD = bank[6]

    @p.on("tensor")
    def _(t, ld_tgt=ld_tgt, kv=kv, vpre=chv.v, spre=chs.v):
        if kv == 0 and not main:
            t.wait_ge(s_l.h, ld_tgt)
        t.wait_ge(chv.h, vpre)
        t.wait_ge(chs.h, spre)
        t.matmul(bG.ap()[:, 0:128], gs[:, ksg], c.Uc[:], start=True, stop=True)
        t.matmul(bG.ap()[:, 128:256], c.Uc[:], gs[:, ksg], start=True, stop=True)
    gmm = cht.v

    @p.on("scalar")
    def _(s, gmm=gmm):
        s.wait_ge(cht.h, gmm)
        s.activation(out=eGT[:], in_=bG.ap()[:, 0:128], func=AF.Exp)
        s.activation(out=enG[:], in_=bG.ap()[:, 128:256], func=AF.Exp, scale=-1.0)
    gex = chs.v

    @p.on("vector")
    def _(v, gex=gex, kv=kv):
        v.wait_ge(chs.h, gex)
        v.tensor_tensor(out=ko[:], in0=ks[:, ksg], in1=enG[:], op=ALU.mult)
    kov = chv.v

    if main:
        @p.on("tensor")
        def _(t, kov=kov):
            t.wait_ge(chv.h, kov)
            pv = bank[5].ap().bitcast(BF16)
            t.transpose(pv[:, 0:128], ko[:], c.ident[:])
        koT_mm = cht.v

        @p.on("vector")
        def _(v, tgt=koT_mm, kv=kv):
            v.wait_ge(cht.h, tgt)
            pv = bank[5].ap().bitcast(BF16)
            v.tensor_copy(out=koT[:], in_=pv[:, 0:128])
            v.tensor_tensor(out=qtT[:], in0=qsT[:, kv * 4:(kv + 1) * 4, :],
                            in1=eGT[:].unsqueeze(1).broadcast_to([128, 4, 128]),
                            op=ALU.mult)
        qt_done = chv.v

        @p.on("tensor")
        def _(t, kv=kv, tt=tt, qt_done=qt_done):
            t.wait_ge(chv.h, qt_done)
            for qi in range(4):
                qh = kv * 4 + qi
                for blk_ in range(2):
                    w0 = tt * 128 + blk_ * 64
                    t.matmul(bW.ap()[:, (qi * 2 + blk_) * 64:(qi * 2 + blk_ + 1) * 64],
                             skT[:, kv, w0:w0 + 128],
                             sqT[:, qh, blk_ * 64:(blk_ + 1) * 64],
                             start=True, stop=True)
        swmm = cht.v

        @p.on("vector")
        def _(v, swmm=swmm, it=it):
            v.wait_ge(cht.h, swmm)
            mask = c.maskTf if it == NTT else c.maskTn
            v.tensor_tensor(out=sws[:], in0=bW.ap(), in1=mask[:], op=ALU.add)
        swm = chv.v

        @p.on("scalar")
        def _(s, swm=swm):
            s.wait_ge(chv.h, swm)
            s.activation(out=swe[:], in_=sws[:], func=AF.Exp)
        swex = chs.v

        @p.on("tensor")
        def _(t, swex=swex):
            t.wait_ge(chs.h, swex)
            t.matmul(bW.ap()[0:1, 0:512], c.ones1[:], swe[:, 0:512],
                     start=True, stop=True)
        zmm = cht.v

        @p.on("vector")
        def _(v, zmm=zmm):
            v.wait_ge(cht.h, zmm)
            v.reciprocal(out=swz[:], in_=bW.ap()[0:1, 0:512])
            v.tensor_scalar(out=swz[:], in0=swz[:], scalar1=0.5, scalar2=0.0,
                            op0=ALU.mult, op1=ALU.add)
        zrec = chv.v

        @p.on("tensor")
        def _(t, zrec=zrec):
            t.wait_ge(chv.h, zrec)
            t.matmul(bW.ap()[:, 0:512], c.onesr[:], swz[:], start=True, stop=True)
        repmm = cht.v

        @p.on("vector")
        def _(v, repmm=repmm):
            v.wait_ge(cht.h, repmm)
            v.tensor_tensor(out=PTs[:], in0=swe[:], in1=bW.ap(), op=ALU.mult)
        pts_done = chv.v

        @p.on("tensor")
        def _(t, kv=kv, qt_done=qt_done):
            t.wait_ge(chv.h, qt_done)
            for qi in range(4):
                t.matmul(bank[5].ap()[:, qi * 128:(qi + 1) * 128],
                         koT[:], qtT[:, qi, :], start=True, stop=True)
        ptgmm = cht.v

        @p.on("vector")
        def _(v, ptgmm=ptgmm):
            v.wait_ge(cht.h, ptgmm)
            v.tensor_tensor(out=PTg[:].rearrange("p (q n) -> p q n", q=4),
                            in0=bank[5].ap().rearrange("p (q n) -> p q n", q=4),
                            in1=c.maskPT[:].unsqueeze(1).broadcast_to([128, 4, 128]),
                            op=ALU.mult)
        ptg_done = chv.v

        @p.on("tensor")
        def _(t, kv=kv, it=it, ptg_done=ptg_done, pts_done=pts_done,
              o_ev=c.get(f"oev_{kv % 2}", 0), vtc=vtc, vtp=vtp):
            t.wait_ge(chv.h, max(ptg_done, pts_done))
            if o_ev:
                t.wait_ge(chs.h, o_ev)
            t.wait_ge(c._s_vw.h, c.get("vw_tgt", 0))
            for qi in range(4):
                osl = bO.ap()[:, qi * 128:(qi + 1) * 128]
                t.matmul(osl, vtc[:, ksg], PTg[:, qi * 128:(qi + 1) * 128],
                         start=True, stop=False)
                t.matmul(osl, Sb[:, Sseg], qtT[:, qi, :], start=False, stop=False)
                pA = PTs[:, (qi * 2) * 64:(qi * 2 + 1) * 64]
                t.matmul(osl[:, 0:64], c._vwin[:, ksg], pA[:], start=False, stop=False)
                pB = PTs[:, (qi * 2 + 1) * 64:(qi * 2 + 2) * 64]
                t.matmul(osl[:, 64:128], vtc[:, ksg], pB[:], start=False, stop=True)
        omm = cht.v

        @p.on("scalar")
        def _(s, kv=kv, tt=tt, omm=omm):
            s.wait_ge(cht.h, omm)
            for qi in range(4):
                qh = kv * 4 + qi
                s.activation(out=oT[:, qh, tt * 128:(tt + 1) * 128],
                             in_=bO.ap()[:, qi * 128:(qi + 1) * 128],
                             func=AF.Identity)
        c[f"oev_{kv % 2}"] = chs.v

    @p.on("tensor")
    def _(t, kov=kov, vpre=chv.v, vtc=vtc):
        t.wait_ge(chv.h, max(kov, vpre))
        t.matmul(bD.ap()[:, 256:384], ko[:], vtc[:, ksg], start=True, stop=True)
    dmm = cht.v

    @p.on("vector")
    def _(v, dmm=dmm, kv=kv):
        v.wait_ge(cht.h, dmm)
        v.tensor_tensor(out=S[:, Sseg], in0=S[:, Sseg], in1=bD.ap()[:, 256:384], op=ALU.add)
        v.tensor_scalar(out=S[:, Sseg], in0=S[:, Sseg], scalar1=eGT[:, 127:128],
                        scalar2=0.0, op0=ALU.mult, op1=ALU.add)
        v.tensor_copy(out=Sb[:, Sseg], in_=S[:, Sseg])

    if it == NTT - 1 and kv == KVH - 1:
        s_halo = c.sem("s_halo")

        @p.on("sync")
        def _(sy, sp=chv.v):
            sy.wait_ge(chv.h, sp)
            sy.dma_start(out=sk[0:64, :], in_=c.T["skh_sp"]).then_inc(s_halo.h, 16)

        v0h = chv.v

        @p.on("tensor")
        def _(t, v0h=v0h):
            t.wait_ge(s_halo.h, 16)
            for kk in range(KVH):
                if v0h + kk - 1 > 0:
                    t.wait_ge(chv.h, v0h + kk - 1)
                pv = bank[4 + kk % 2].ap().bitcast(BF16)
                t.transpose(pv[:, 0:64], sk[0:64, kk * 128:(kk + 1) * 128], c.ident[0:64, 0:64])
        peh = cht.v - KVH

        @p.on("vector")
        def _(v, peh=peh):
            for kk in range(KVH):
                v.wait_ge(cht.h, peh + kk + 1)
                pv = bank[4 + kk % 2].ap().bitcast(BF16)
                v.tensor_copy(out=skT[:, kk, 0:64], in_=pv[:, 0:64])


def emit_wo(c):
    nc, T, sb, p, bank = c.nc, c.T, c.sb, c.p, c.bank
    cht, chv, chs = c.cht, c.chv, c.chs
    oT = c.oT
    s_wl = [c.sem("s_cwl0"), c.sem("s_cwl1")]
    s_sp = c.sem("s_csp")

    wsl = [sb(f"cwsl{i}", [128, NKT, 128], BF16) for i in range(2)]
    wo_st = sb("wo_st", [128, M], BF16)

    for ht in range(NKT):
        wslb = wsl[ht % 2]

        swl = s_wl[ht % 2]

        @p.on("gpsimd")
        def _(g, ht=ht, wslb=wslb, prev_mm=c.get(f"wo_mm_slot{ht % 2}", 0), swl=swl):
            g.wait_ge(cht.h, prev_mm)
            g.dma_start(out=wslb[:],
                        in_=T["wo"].rearrange("(o p) n -> p o n", p=128)[:, :, ht * 128:(ht + 1) * 128]
                        ).then_inc(swl.h, 16)
        swl.inc(16)
        wl_tgt = swl.v

        @p.on("tensor")
        def _(t, ht=ht, wslb=wslb, wl_tgt=wl_tgt, ev_prev=c.get("wo_ev_prev", 0), swl=swl):
            t.wait_ge(swl.h, wl_tgt)
            t.wait_ge(chs.h, ev_prev)
            for half in range(2):
                bnk = bank[half]
                for of in range(NKT):
                    t.matmul(bnk.ap(), wslb[:, of, :],
                             oT[:, of, half * 512:(half + 1) * 512],
                             start=(of == 0), stop=(of == NKT - 1))
        c[f"wo_mm_slot{ht % 2}"] = cht.v
        mm_tgt = cht.v

        @p.on("scalar")
        def _(s, mm_tgt=mm_tgt, spp=s_sp.v):
            s.wait_ge(cht.h, mm_tgt)
            if spp:
                s.wait_ge(s_sp.h, spp)
            s.activation(out=wo_st[:, 0:512], in_=bank[0].ap(), func=AF.Identity)
            s.activation(out=wo_st[:, 512:1024], in_=bank[1].ap(), func=AF.Identity)
        c["wo_ev_prev"] = chs.v
        ev_tgt = chs.v

        @p.on("sync")
        def _(sy, ht=ht, ev_tgt=ev_tgt):
            sy.wait_ge(chs.h, ev_tgt)
            sy.dma_start(out=T["wo_sp"][0, ht], in_=wo_st[:, 0:512]).then_inc(s_sp.h, 16)
            sy.dma_start(out=T["wo_sp"][1, ht], in_=wo_st[:, 512:1024]).then_inc(s_sp.h, 16)
        s_sp.inc(32)
    c.wo_sp_done = (s_sp.h, s_sp.v)


def emit_postln(c, which):
    nc, T, sb, p, bank = c.nc, c.T, c.sb, c.p, c.bank
    cht, chv, chs = c.cht, c.chv, c.chs
    s_l = [c.sem(f"s_{which}l0"), c.sem(f"s_{which}l1")]
    s_x = [c.sem(f"s_{which}x0"), c.sem(f"s_{which}x1")]
    s_o = c.sem(f"s_{which}o")
    src_sp = T["wo_sp"] if which == "ln1" else T["dn_sp"]
    x_src = T["x_main"] if which == "ln1" else T["x1_sp"]

    dsl = [sb(f"{which}dl{i}", [128, NKT, 128], BF16) for i in range(2)]
    xrb = [sb(f"{which}xr{i}", [128, HID], F32) for i in range(2)]
    dtm = sb(f"{which}dtm", [128, HID], BF16)
    lsum = sb(f"{which}sum", [128, HID], F32)
    lsq = sb(f"{which}sq", [128, HID], F32)
    stat = sb(f"{which}stat", [128, 4], F32)
    outb = sb(f"{which}out", [128, HID], F32)
    outbb = sb(f"{which}outb", [128, HID], BF16)

    for tt in range(NTT):
        half = tt // 4
        q = tt % 4
        dbuf = dsl[tt % 2]

        sl = s_l[tt % 2]
        sx = s_x[tt % 2]

        @p.on("sync")
        def _(sy, half=half, q=q, dbuf=dbuf, tr_prev=c.get(f"{which}_tr_slot{tt % 2}", 0), sl=sl):
            sy.wait_ge(cht.h, tr_prev)
            sy.dma_start(out=dbuf[:],
                         in_=src_sp[half, :, :, q * 128:(q + 1) * 128]
                         .rearrange("ht p t -> p ht t")).then_inc(sl.h, 16)
        sl.inc(16)
        l_tgt = sl.v

        @p.on("scalar")
        def _(s, tt=tt, xb=xrb[tt % 2], used=c.get(f"{which}_x_slot{tt % 2}", 0), sx=sx):
            s.wait_ge(chv.h, used)
            s.dma_start(out=xb[:], in_=x_src[tt * 128:(tt + 1) * 128, :]).then_inc(sx.h, 16)
        sx.inc(16)
        x_tgt = sx.v
        v0 = chv.v

        @p.on("tensor")
        def _(t, dbuf=dbuf, l_tgt=l_tgt, v0=v0, sl=sl):
            t.wait_ge(sl.h, l_tgt)
            for j in range(NKT):
                if v0 + j - 1 > 0:
                    t.wait_ge(chv.h, v0 + j - 1)
                pv = bank[j % 2].ap().bitcast(BF16)
                t.transpose(pv[:, 0:128], dbuf[:, j, :], c.ident[:])
        c[f"{which}_tr_slot{tt % 2}"] = cht.v
        pe0 = cht.v - NKT

        @p.on("vector")
        def _(v, pe0=pe0, xb=xrb[tt % 2], x_tgt=x_tgt, sx=sx):
            for j in range(NKT):
                v.wait_ge(cht.h, pe0 + j + 1)
                pv = bank[j % 2].ap().bitcast(BF16)
                v.tensor_copy(out=dtm[:, j * 128:(j + 1) * 128], in_=pv[:, 0:128])
            v.wait_ge(sx.h, x_tgt)
            v.tensor_tensor(out=lsum[:], in0=xb[:], in1=dtm[:], op=ALU.add)
            v.tensor_reduce(out=stat[:, 0:1], in_=lsum[:], axis=AX.X, op=ALU.add)
        c[f"{which}_x_slot{tt % 2}"] = chv.v
        sum_done = chv.v

        @p.on("scalar")
        def _(s, sum_done=sum_done):
            s.wait_ge(chv.h, sum_done)
            s.activation(out=lsq[:], in_=lsum[:], func=AF.Square)
        sq_done = chs.v

        @p.on("vector")
        def _(v, sq_done=sq_done):
            v.wait_ge(chs.h, sq_done)
            v.tensor_reduce(out=stat[:, 1:2], in_=lsq[:], axis=AX.X, op=ALU.add)
            v.tensor_scalar(out=stat[:, 0:2], in0=stat[:, 0:2], scalar1=1.0 / HID,
                            scalar2=0.0, op0=ALU.mult, op1=ALU.add)
            v.tensor_tensor(out=stat[:, 2:3], in0=stat[:, 0:1], in1=stat[:, 0:1], op=ALU.mult)
            v.tensor_tensor(out=stat[:, 2:3], in0=stat[:, 1:2], in1=stat[:, 2:3], op=ALU.subtract)
            v.tensor_scalar(out=stat[:, 2:3], in0=stat[:, 2:3], scalar1=1.0, scalar2=EPS,
                            op0=ALU.mult, op1=ALU.add)
        var_done = chv.v

        @p.on("scalar")
        def _(s, var_done=var_done):
            s.wait_ge(chv.h, var_done)
            s.activation(out=stat[:, 3:4], in_=stat[:, 2:3], func=AF.Ln)
        ln_done = chs.v

        @p.on("vector")
        def _(v, ln_done=ln_done):
            v.wait_ge(chs.h, ln_done)
            v.tensor_scalar(out=stat[:, 3:4], in0=stat[:, 3:4], scalar1=-0.5, scalar2=0.0,
                            op0=ALU.mult, op1=ALU.add)
        lnv_done = chv.v

        @p.on("scalar")
        def _(s, lnv_done=lnv_done):
            s.wait_ge(chv.h, lnv_done)
            s.activation(out=stat[:, 2:3], in_=stat[:, 3:4], func=AF.Exp)
        rstd_done = chs.v

        @p.on("vector")
        def _(v, rstd_done=rstd_done, o_prev=c.get(f"{which}_o_prev", 0), which2=which):
            v.wait_ge(chs.h, rstd_done)
            if o_prev:
                v.wait_ge(s_o.h, o_prev)
            v.tensor_scalar(out=outb[:], in0=lsum[:], scalar1=stat[:, 0:1],
                            scalar2=stat[:, 2:3], op0=ALU.subtract, op1=ALU.mult)
            if which2 == "ln1":
                v.tensor_copy(out=outbb[:], in_=outb[:])
        norm_done = chv.v

        @p.on("sync")
        def _(sy, tt=tt, norm_done=norm_done, which2=which):
            sy.wait_ge(chv.h, norm_done)
            dst = T["x1_sp"] if which2 == "ln1" else T["out"]
            sy.dma_start(out=dst[tt * 128:(tt + 1) * 128, :], in_=outb[:]).then_inc(s_o.h, 16)
        s_o.inc(16)
        c[f"{which}_o_prev"] = s_o.v

        if which == "ln1":
            xT_ln1 = c.xT_ln1
            v1 = chv.v

            @p.on("tensor")
            def _(t, norm_done=norm_done, v1=v1):
                t.wait_ge(chv.h, norm_done)
                for j in range(NKT):
                    if v1 + j - 1 > 0:
                        t.wait_ge(chv.h, v1 + j - 1)
                    xv = outbb[:].rearrange("p (a b e) -> p a b e", a=32, b=4, e=32)[:, :, :, j]
                    pv = bank[2 + j % 2].ap().bitcast(BF16)
                    t.transpose(pv[:, 0:128], xv, c.ident[:])
            c[f"{which}_tr_prev"] = cht.v
            pe1 = cht.v - NKT

            @p.on("vector")
            def _(v, tt=tt, pe1=pe1, xT_ln1=xT_ln1):
                for j in range(NKT):
                    v.wait_ge(cht.h, pe1 + j + 1)
                    pv = bank[2 + j % 2].ap().bitcast(BF16)
                    v.tensor_copy(out=xT_ln1[:, j, tt * 128:(tt + 1) * 128],
                                  in_=pv[:, 0:128])
            c[f"{which}_x_used"] = chv.v
    if which == "ln1":
        c.xT_done = chv.v
        c.x1_done = (s_o.h, s_o.v)
    else:
        @p.on("gpsimd")
        def _(g, tgt=s_o.v):
            g.wait_ge(s_o.h, tgt)


def emit_attn_out(c):
    T, p = c.T, c.p
    s_ao = c.sem("s_ao")

    @p.on("sync")
    def _(sy):
        sy.dma_start(out=T["out"], in_=T["x1_sp"]).then_inc(s_ao.h, 16)
    s_ao.inc(16)

    @p.on("gpsimd")
    def _(g, tgt=s_ao.v):
        g.wait_ge(s_ao.h, tgt)


def emit_x2_load(c):
    nc, T, sb, p, bank = c.nc, c.T, c.sb, c.p, c.bank
    cht, chv, chs = c.cht, c.chv, c.chs
    s_xl = [c.sem("s_fxl0"), c.sem("s_fxl1")]
    s_o = c.sem("s_fo")
    xbuf = [sb(f"fxb{i}", [128, HID], F32) for i in range(2)]
    xbb = sb("fxbb", [128, HID], BF16)
    xT_ln1 = c.xT_ln1

    for tt in range(NTT):
        buf = xbuf[tt % 2]

        sxl = s_xl[tt % 2]

        @p.on("sync")
        def _(sy, tt=tt, buf=buf, prev=c.get(f"fx_used_{tt - 2}", 0), sxl=sxl):
            sy.wait_ge(chv.h, prev)
            sy.dma_start(out=buf[:], in_=T["x2in"][tt * 128:(tt + 1) * 128, :]).then_inc(sxl.h, 16)
            sy.dma_start(out=T["x1_sp"][tt * 128:(tt + 1) * 128, :],
                         in_=T["x2in"][tt * 128:(tt + 1) * 128, :]).then_inc(s_o.h, 16)
        sxl.inc(16)
        s_o.inc(16)
        xl_tgt = sxl.v

        @p.on("vector")
        def _(v, buf=buf, xl_tgt=xl_tgt, prev_tr=c.get(f"fx_tr_{tt - 1}", 0), sxl=sxl):
            v.wait_ge(sxl.h, xl_tgt)
            v.wait_ge(cht.h, prev_tr)
            v.tensor_copy(out=xbb[:], in_=buf[:])
        cv_tgt = chv.v
        v1 = chv.v

        @p.on("tensor")
        def _(t, cv_tgt=cv_tgt, v1=v1, first=(tt == 0)):
            t.wait_ge(chv.h, cv_tgt)
            if first:
                wait_consts(t, c)
            for j in range(NKT):
                if v1 + j - 1 > 0:
                    t.wait_ge(chv.h, v1 + j - 1)
                xv = xbb[:].rearrange("p (a b e) -> p a b e", a=32, b=4, e=32)[:, :, :, j]
                pv = bank[j % 2].ap().bitcast(BF16)
                t.transpose(pv[:, 0:128], xv, c.ident[:])
        c[f"fx_tr_{tt}"] = cht.v
        pe0 = cht.v - NKT

        @p.on("vector")
        def _(v, tt=tt, pe0=pe0):
            for j in range(NKT):
                v.wait_ge(cht.h, pe0 + j + 1)
                pv = bank[j % 2].ap().bitcast(BF16)
                v.tensor_copy(out=xT_ln1[:, j, tt * 128:(tt + 1) * 128],
                              in_=pv[:, 0:128])
        c[f"fx_used_{tt}"] = chv.v
    c.xT_done = chv.v
    c.x1_done = (s_o.h, s_o.v)


def emit_gate_up(c):
    nc, T, sb, p, bank = c.nc, c.T, c.sb, c.p, c.bank
    cht, chv, chs = c.cht, c.chv, c.chs
    xT = c.xT_ln1
    s_pl = [c.sem("s_gpl0"), c.sem("s_gpl1")]
    s_sl = [c.sem("s_gsl0"), c.sem("s_gsl1")]
    s_zl = c.sem("s_gzl")
    s_hs = [c.sem("s_ghs0"), c.sem("s_ghs1")]
    s_Hsp = c.sem("s_gHsp")

    pkb = [sb(f"gpk{i}", [128, 2, NKT, 16], I32) for i in range(2)]
    nib = sb("gnib", [128, NKT * 128], I16)
    wsl = [sb(f"gws{i}", [128, 2, NKT, 128], BF16) for i in range(2)]
    ssl = [sb(f"gss{i}", [128, 2, 128], F32) for i in range(2)]
    negz = sb("gnegz", [32, 2 * NFF], BF16)
    Xs = sb("gXs", [32, M], BF16)
    est = sb("gest", [128, M], F32)
    sgt = sb("gsgt", [128, M], F32)
    hst = [sb(f"ghst{i}", [128, M], BF16) for i in range(2)]
    hrow = sb("ghrow", [1, M], BF16)

    @p.on("gpsimd")
    def _(g):
        g.dma_start(out=negz[:, 0:NFF], in_=T["zs_sp"][0:32, 0:NFF]).then_inc(s_zl.h, 16)
        g.dma_start(out=negz[:, NFF:2 * NFF], in_=T["zs_sp"][0:32, NFF:2 * NFF]).then_inc(s_zl.h, 16)
    s_zl.inc(32)

    @p.on("tensor")
    def _(t, xT_done=c.xT_done, zl=s_zl.v):
        t.wait_ge(chv.h, xT_done)
        for half in range(2):
            bnk = bank[6] if half == 0 else bank[7]
            for j in range(NKT):
                t.matmul(bnk.ap()[0:32, 0:512], c.G32[:],
                         xT[:, j, half * 512:(half + 1) * 512],
                         start=(j == 0), stop=(j == NKT - 1))
    Xmm = cht.v

    @p.on("vector")
    def _(v, tgt=Xmm):
        v.wait_ge(cht.h, tgt)
        v.tensor_copy(out=Xs[:, 0:512], in_=bank[6].ap()[0:32, 0:512])
        v.tensor_copy(out=Xs[:, 512:1024], in_=bank[7].ap()[0:32, 0:512])
    X_done = chv.v

    def emit_gu_dequant(f):
        pk = pkb[f % 2]
        ws = wsl[f % 2]
        ss = ssl[f % 2]
        dq_prev = c.get(f"g_dq_slot{f % 2}", 0)
        mm_prev = c.get(f"g_mm_slot{f % 2}", 0)
        spl = s_pl[f % 2]
        ssl_s = s_sl[f % 2]

        @p.on("sync")
        def _(sy, f=f, pk=pk, dq_prev=dq_prev, spl=spl):
            sy.wait_ge(chv.h, dq_prev)
            sy.dma_start(out=pk[:, 0], in_=T["gq_t"][f]).then_inc(spl.h, 16)
            sy.dma_start(out=pk[:, 1], in_=T["uq_t"][f]).then_inc(spl.h, 16)
        spl.inc(32)
        pl_tgt = spl.v

        @p.on("gpsimd")
        def _(g, f=f, ss=ss, dq_prev=dq_prev, ssl_s=ssl_s):
            g.wait_ge(chv.h, dq_prev)
            g.dma_start(out=ss[:, 0, :], in_=T["gsr"][:, f * 128:(f + 1) * 128]).then_inc(ssl_s.h, 16)
            g.dma_start(out=ss[:, 1, :], in_=T["usr"][:, f * 128:(f + 1) * 128]).then_inc(ssl_s.h, 16)
        ssl_s.inc(32)
        sl_tgt = ssl_s.v

        @p.on("vector")
        def _(v, f=f, pk=pk, ws=ws, ss=ss, pl_tgt=pl_tgt, sl_tgt=sl_tgt,
              mm_prev=mm_prev, spl=spl, ssl_s=ssl_s):
            v.wait_ge(spl.h, pl_tgt)
            v.wait_ge(ssl_s.h, sl_tgt)
            v.wait_ge(cht.h, mm_prev)
            for mat in range(2):
                z16 = pk[:, mat].bitcast(I16).rearrange("p j (w h) -> p j w h", h=2)
                ov = nib[:].rearrange("p (j w e h) -> p j w e h", j=NKT, e=4, h=2)
                for jb in range(4):
                    v.tensor_scalar(out=ov[:, :, :, jb, :], in0=z16, scalar1=4 * jb,
                                    scalar2=15, op0=ALU.logical_shift_right, op1=ALU.bitwise_and)
                nv = nib[:].rearrange("p (j n) -> p j n", j=NKT)
                v.tensor_tensor(out=ws[:, mat], in0=nv,
                                in1=ss[:, mat, :].unsqueeze(1).broadcast_to([128, NKT, 128]),
                                op=ALU.mult)
        c[f"g_dq_slot{f % 2}"] = chv.v

    emit_gu_dequant(0)
    for f in range(NFT):
        ws = wsl[f % 2]
        dq_tgt = c[f"g_dq_slot{f % 2}"]
        if f % 2 == 0:
            b_g, b_u = [bank[0], bank[1]], [bank[2], bank[3]]
        else:
            b_g, b_u = [bank[4], bank[5]], [bank[6], bank[7]]

        if "g_pend_H" in c:
            c.pop("g_pend_H")()

        @p.on("tensor")
        def _(t, f=f, ws=ws, dq_tgt=dq_tgt, b_g=b_g, b_u=b_u, X_done=X_done,
              ev_prev2=c.get(f"g_ev_slot{f % 2}", 0), es_prev2=c.get(f"g_es_slot{f % 2}", 0),
              hv_prev=c.get(f"g_hv_slot{f % 2}", 0), zl=s_zl.v):
            t.wait_ge(chv.h, max(dq_tgt, X_done, ev_prev2, hv_prev))
            t.wait_ge(chs.h, es_prev2)
            if f == 0:
                t.wait_ge(s_zl.h, zl)
            for mat, bb in ((0, b_g), (1, b_u)):
                for half in range(2):
                    bnk = bb[half]
                    for j in range(NKT):
                        t.matmul(bnk.ap(), ws[:, mat, j, :],
                                 xT[:, j, half * 512:(half + 1) * 512],
                                 start=(j == 0), stop=False)
                    t.matmul(bnk.ap(), negz[:, mat * NFF + f * 128:mat * NFF + (f + 1) * 128],
                             Xs[:, half * 512:(half + 1) * 512], start=False, stop=True)
        c[f"g_mm_slot{f % 2}"] = cht.v
        mm_tgt = cht.v
        hb = hst[f % 2]

        if f + 1 < NFT:
            emit_gu_dequant(f + 1)

        @p.on("scalar")
        def _(s, mm_tgt=mm_tgt, b_g=b_g):
            s.wait_ge(cht.h, mm_tgt)
            for half in range(2):
                sl = slice(half * 512, (half + 1) * 512)
                s.activation(out=est[:, sl], in_=b_g[half].ap(), func=AF.Exp, scale=-1.0)
                s.activation(out=est[:, sl], in_=est[:, sl], func=AF.Ln, bias=1.0)
                s.activation(out=sgt[:, sl], in_=est[:, sl], func=AF.Exp, scale=-1.0)
        c[f"g_es_slot{f % 2}"] = chs.v
        es_tgt = chs.v
        shs = s_hs[f % 2]

        @p.on("vector")
        def _(v, es_tgt=es_tgt, b_g=b_g, b_u=b_u, hb=hb, hs_prev=c.get(f"g_hs_slot{f % 2}", 0), shs=shs):
            v.wait_ge(chs.h, es_tgt)
            if hs_prev:
                v.wait_ge(shs.h, hs_prev)
            for half in range(2):
                sl = slice(half * 512, (half + 1) * 512)
                v.tensor_tensor(out=sgt[:, sl], in0=b_g[half].ap(), in1=sgt[:, sl], op=ALU.mult)
                v.tensor_tensor(out=hb[:, sl], in0=sgt[:, sl], in1=b_u[half].ap(), op=ALU.mult)
        c[f"g_ev_slot{f % 2}"] = chv.v
        ev_tgt = chv.v

        @p.on("sync")
        def _(sy, f=f, hb=hb, ev_tgt=ev_tgt, shs=shs):
            sy.wait_ge(chv.h, ev_tgt)
            sy.dma_start(out=T["h_sp"][f], in_=hb[:]).then_inc(shs.h, 16)
        shs.inc(16)
        c[f"g_hs_slot{f % 2}"] = shs.v

        def emit_H(f=f, hb=hb, b_g=b_g, ev_tgt=ev_tgt):
            @p.on("tensor")
            def _(t, hb=hb, b_g=b_g, ev_tgt=ev_tgt):
                t.wait_ge(chv.h, ev_tgt)
                t.matmul(b_g[0].ap()[0:1, 0:512], c.ones1[:], hb[:, 0:512], start=True, stop=True)
                t.matmul(b_g[1].ap()[0:1, 0:512], c.ones1[:], hb[:, 512:1024], start=True, stop=True)
            Hm = cht.v

            @p.on("vector")
            def _(v, b_g=b_g, tgt=Hm, hsp_prev=s_Hsp.v):
                v.wait_ge(cht.h, tgt)
                if hsp_prev:
                    v.wait_ge(s_Hsp.h, hsp_prev)
                v.tensor_copy(out=hrow[0:1, 0:512], in_=b_g[0].ap()[0:1, 0:512])
                v.tensor_copy(out=hrow[0:1, 512:1024], in_=b_g[1].ap()[0:1, 0:512])
            c[f"g_hv_slot{f % 2}"] = chv.v

            @p.on("scalar")
            def _(s, f=f, tgt=chv.v):
                s.wait_ge(chv.h, tgt)
                s.dma_start(out=T["H_sp"][f:f + 1, :], in_=hrow[:]).then_inc(s_Hsp.h, 16)
            s_Hsp.inc(16)
        c["g_pend_H"] = emit_H
    c.pop("g_pend_H")()
    c.h_done = (s_hs[0].h, s_hs[0].v)
    c.h_done2 = (s_hs[1].h, s_hs[1].v)
    c.H_done = (s_Hsp.h, s_Hsp.v)


def emit_down(c):
    nc, T, sb, p, bank = c.nc, c.T, c.sb, c.p, c.bank
    cht, chv, chs = c.cht, c.chv, c.chs
    s_hl = c.sem("s_dhl")
    s_pl = [c.sem("s_dpl0"), c.sem("s_dpl1")]
    s_sl = c.sem("s_dsl")
    s_wl = [c.sem("s_dwl0"), c.sem("s_dwl1")]
    s_sp = c.sem("s_dsp")
    s_wsp = [c.sem("s_dwsp0"), c.sem("s_dwsp1")]
    s_zl = c.sem("s_dzl")
    s_H = c.sem("s_dH")

    hb = sb("dh", [128, NFT, 512], BF16)
    pkb = [sb(f"dpk{i}", [128, NFT, 16], I32) for i in range(2)]
    wd = [sb(f"dwd{i}", [128, NFT, 128], BF16) for i in range(2)]
    srd = sb("dsrd", [128, NFF], BF16)
    negzd = sb("dnegz", [NFT, HID], BF16)
    Hs = sb("dHs", [NFT, 512], BF16)
    dstage = sb("dstage", [128, 512], BF16)

    @p.on("gpsimd")
    def _(g):
        g.dma_start(out=negzd[:], in_=T["zs_sp"][0:NFT, 2 * NFF:2 * NFF + HID]).then_inc(s_zl.h, 16)
    s_zl.inc(16)

    for half in range(2):
        @p.on("sync")
        def _(sy, half=half, mm_prev=cht.v):
            sy.wait_ge(cht.h, mm_prev)
            sy.dma_start(out=hb[:],
                         in_=T["h_sp"].rearrange("f p t -> p f t")[:, :, half * 512:(half + 1) * 512]
                         ).then_inc(s_hl.h, 16)
        s_hl.inc(16)
        hl_tgt = s_hl.v

        @p.on("gpsimd")
        def _(g, half=half, H_used=c.get("d_H_used", 0)):
            g.wait_ge(cht.h, H_used)
            g.dma_start(out=Hs[:], in_=T["H_sp"][:, half * 512:(half + 1) * 512]
                        ).then_inc(s_H.h, 16)
        s_H.inc(16)
        H_done = s_H.v

        def emit_d_dequant(ht):
            pk = pkb[ht % 2]
            wdb = wd[ht % 2]
            dq_prev = c.get(f"d_dq_slot{ht % 2}", 0)
            mm_slot = c.get(f"d_mm_slot{ht % 2}", 0)
            wsp_slot = c.get(f"d_wsp_slot{ht % 2}", 0)
            spl = s_pl[ht % 2]
            swsp = s_wsp[ht % 2]

            @p.on("sync")
            def _(sy, ht=ht, pk=pk, dq_prev=dq_prev, spl=spl):
                sy.wait_ge(chv.h, dq_prev)
                sy.dma_start(out=pk[:], in_=T["dq_t"][ht]).then_inc(spl.h, 16)
            spl.inc(16)
            pl_tgt = spl.v

            @p.on("gpsimd")
            def _(g, ht=ht, dq_all=c.get("d_dq_all", 0), zl=s_zl.v):
                g.wait_ge(chv.h, dq_all)
                if ht == 0:
                    g.wait_ge(s_zl.h, zl)
                g.dma_start(out=srd[:], in_=T["dsr"][ht:ht + 1, :]
                            .broadcast_to([128, NFF])).then_inc(s_sl.h, 16)
            s_sl.inc(16)
            sl_tgt = s_sl.v

            @p.on("vector")
            def _(v, pk=pk, wdb=wdb, pl_tgt=pl_tgt, sl_tgt=sl_tgt,
                  mm_slot=mm_slot, wsp_slot=wsp_slot, spl=spl, swsp=swsp):
                v.wait_ge(spl.h, pl_tgt)
                v.wait_ge(cht.h, mm_slot)
                if wsp_slot:
                    v.wait_ge(swsp.h, wsp_slot)
                z16 = pk[:].bitcast(I16).rearrange("p k (w h) -> p k w h", h=2)
                wdi = wdb[:].rearrange("p k n -> p (k n)").bitcast(I16)
                ov = wdi.rearrange("p (k w e h) -> p k w e h", k=NFT, e=4, h=2)
                for jb in range(4):
                    v.tensor_scalar(out=ov[:, :, :, jb, :], in0=z16, scalar1=4 * jb,
                                    scalar2=15, op0=ALU.logical_shift_right, op1=ALU.bitwise_and)
                v.wait_ge(s_sl.h, sl_tgt)
                v.tensor_tensor(out=wdb[:].rearrange("p k n -> p (k n)"),
                                in0=wdi, in1=srd[:], op=ALU.mult)
            c[f"d_dq_slot{ht % 2}"] = chv.v
            c["d_dq_all"] = chv.v

            @p.on("scalar")
            def _(s, ht=ht, wdb=wdb, tgt=chv.v, swsp=swsp):
                s.wait_ge(chv.h, tgt)
                s.dma_start(out=T["wd_sp"][ht], in_=wdb[:].rearrange("p k n -> p (k n)")
                            ).then_inc(swsp.h, 16)
            swsp.inc(16)
            c[f"d_wsp_slot{ht % 2}"] = swsp.v

        def emit_d_load(ht):
            wdb = wd[ht % 2]
            mm_slot = c.get(f"d_mm_slot{ht % 2}", 0)
            swl = s_wl[ht % 2]

            @p.on("sync")
            def _(sy, ht=ht, wdb=wdb, mm_slot=mm_slot, swl=swl,
                  wsp_fin=s_wsp[ht % 2].v):
                sy.wait_ge(cht.h, mm_slot)
                sy.wait_ge(s_wsp[ht % 2].h, wsp_fin)
                sy.dma_start(out=wdb[:].rearrange("p k n -> p (k n)"),
                             in_=T["wd_sp"][ht]).then_inc(swl.h, 16)
            swl.inc(16)
            c[f"d_wl_slot{ht % 2}"] = swl.v

        if half == 0:
            emit_d_dequant(0)
        else:
            emit_d_load(0)
        for ht in range(NKT):
            wdb = wd[ht % 2]
            if half == 0:
                w_sem, w_tgt = None, c[f"d_dq_slot{ht % 2}"]
            else:
                w_sem, w_tgt = s_wl[ht % 2], c[f"d_wl_slot{ht % 2}"]

            bnk = bank[ht % 2]
            ev_prev = c.get(f"d_ev_slot{ht % 2}", 0)

            @p.on("tensor")
            def _(t, ht=ht, wdb=wdb, bnk=bnk, w_sem=w_sem, w_tgt=w_tgt, H_done=H_done,
                  ev_prev=ev_prev, hl_tgt=hl_tgt):
                t.wait_ge(s_hl.h, hl_tgt)
                if w_sem is None:
                    t.wait_ge(chv.h, w_tgt)
                else:
                    t.wait_ge(w_sem.h, w_tgt)
                t.wait_ge(s_H.h, H_done)
                t.wait_ge(chv.h, ev_prev)
                for kt in range(NFT):
                    t.matmul(bnk.ap(), wdb[:, kt, :], hb[:, kt, :],
                             start=(kt == 0), stop=False)
                t.matmul(bnk.ap(), negzd[:, ht * 128:(ht + 1) * 128], Hs[:],
                         start=False, stop=True)
            c[f"d_mm_slot{ht % 2}"] = cht.v
            mm_tgt = cht.v

            if ht + 1 < NKT:
                if half == 0:
                    emit_d_dequant(ht + 1)
                else:
                    emit_d_load(ht + 1)

            @p.on("vector")
            def _(v, mm_tgt=mm_tgt, bnk=bnk, sp_prev=c.get("d_sp_prev", 0)):
                v.wait_ge(cht.h, mm_tgt)
                if sp_prev:
                    v.wait_ge(s_sp.h, sp_prev)
                v.tensor_copy(out=dstage[:], in_=bnk.ap())
            c[f"d_ev_slot{ht % 2}"] = chv.v

            @p.on("sync")
            def _(sy, half=half, ht=ht, tgt=chv.v):
                sy.wait_ge(chv.h, tgt)
                sy.dma_start(out=T["dn_sp"][half, ht], in_=dstage[:]).then_inc(s_sp.h, 16)
            s_sp.inc(16)
            c["d_sp_prev"] = s_sp.v
        c["d_H_used"] = cht.v
    c.dn_done = (s_sp.h, s_sp.v)


_NC_CACHE = {}


def _prep_inputs(inputs, b, hf):
    x = inputs["L0_attn_hidden_states"]
    wq = np.asarray(inputs["L0_attn_Wq"])
    wk = np.asarray(inputs["L0_attn_Wk"])
    wv = np.asarray(inputs["L0_attn_Wv"])
    wo = np.asarray(inputs["L0_attn_Wo"])
    gq = np.asarray(inputs["L0_ffn_gate_qweight"])
    gsc = np.asarray(inputs["L0_ffn_gate_scales"])
    gz = np.asarray(inputs["L0_ffn_gate_qzeros"])
    uq = np.asarray(inputs["L0_ffn_up_qweight"])
    usc = np.asarray(inputs["L0_ffn_up_scales"])
    uz = np.asarray(inputs["L0_ffn_up_qzeros"])
    dq = np.asarray(inputs["L0_ffn_down_qweight"])
    dsc = np.asarray(inputs["L0_ffn_down_scales"])
    dz = np.asarray(inputs["L0_ffn_down_qzeros"])

    glob = _prep_inputs._cache.get(id(inputs))
    if glob is None:
        wq_il = np.ascontiguousarray(wq[IL_PERM])
        wk_il = np.ascontiguousarray(wk[IL_PERM])
        wv_il = np.ascontiguousarray(wv[IL_PERM])
        gq_r = gq[IL_PERM].reshape(NKT, 128, 1376).transpose(1, 0, 2)
        gq_t = np.ascontiguousarray(
            gq_r.reshape(128, NKT, NFT, 16).transpose(2, 0, 1, 3))
        uq_r = uq[IL_PERM].reshape(NKT, 128, 1376).transpose(1, 0, 2)
        uq_t = np.ascontiguousarray(
            uq_r.reshape(128, NKT, NFT, 16).transpose(2, 0, 1, 3))
        dq_t = np.ascontiguousarray(
            dq.reshape(NFT, 128, 512).reshape(NFT, 128, NKT, 16).transpose(2, 1, 0, 3))
        gsr = np.ascontiguousarray(np.repeat(gsc, 4, axis=0))
        usr = np.ascontiguousarray(np.repeat(usc, 4, axis=0))
        dsr = np.ascontiguousarray(
            dsc.reshape(NFT, NKT, 128).transpose(1, 0, 2).reshape(NKT, NFF))
        glob = dict(wq_il=wq_il, wk_il=wk_il, wv_il=wv_il, wo=np.ascontiguousarray(wo),
                    gq_t=gq_t, uq_t=uq_t, dq_t=dq_t, gsr=gsr, usr=usr, dsr=dsr,
                    gz=np.ascontiguousarray(gz), uz=np.ascontiguousarray(uz),
                    dz=np.ascontiguousarray(dz), gsc=np.ascontiguousarray(gsc),
                    usc=np.ascontiguousarray(usc), dsc=np.ascontiguousarray(dsc))
        _prep_inputs._cache[id(inputs)] = glob

    x_main = np.ascontiguousarray(x[b, hf * M:(hf + 1) * M])
    x_pre = (np.ascontiguousarray(x[b, 0:M]) if hf == 1
             else np.zeros((M, HID), np.float32))
    start = hf * M
    d = dict(glob)
    d.update(
        x_main=x_main, x_pre=x_pre,
        identc=HC["ident"], Uc=HC["U"], maskPTc=HC["maskPT"], G32c=HC["G32"],
        ones1c=HC["ones1"], onesrc=HC["onesr"], maskTn=HC["maskT_norm"],
        maskTf=(HC["maskT_norm"] if hf == 1 else HC["maskT_first0"]),
        cosq=HC["cosq"][start:start + M], sinq=HC["sinq"][start:start + M],
        cosk=HC["cos"][start:start + M], sink=HC["sin"][start:start + M],
        coskh=HC["cos"][M - 64:M], sinkh=HC["sin"][M - 64:M],
    )
    return d


_prep_inputs._cache = {}


def kernel(**inputs):
    if "nc" not in _NC_CACHE:
        _NC_CACHE["nc"] = build_program("full")
    nc = _NC_CACHE["nc"]
    maps = []
    for core in range(8):
        b, hf = core // 2, core % 2
        maps.append(_prep_inputs(inputs, b, hf))
    _prep_inputs._cache.clear()
    res = run_bass_kernel_spmd(nc, maps, core_ids=list(range(8)))
    out = np.empty((B, N, HID), np.float32)
    for core in range(8):
        b, hf = core // 2, core % 2
        out[b, hf * M:(hf + 1) * M] = res.results[core]["out"]
    return out
